# version 16
# baseline (speedup 1.0000x reference)
import contextlib
import numpy as np
import concourse.bass as bass
import concourse.mybir as mybir
from concourse.bass_utils import run_bass_kernel_spmd
from concourse.alu_op_type import AluOpType as OP

F32 = mybir.dt.float32
F16 = mybir.dt.float16
BF16 = mybir.dt.bfloat16
AF = mybir.ActivationFunctionType
AX = mybir.AxisListType.X

L = 2048
H = 4096
DS = 4096
PROJ = 8512
CONVD = 4352
NH = 64
TT = 1024
NB = 8
NT = L // TT
EPS = 1e-5
NCORES = 8
EXT = TT + 3

ALL_ENG = ["sync", "gpsimd", "tensor", "vector", "scalar"]
_MERGE_UNTIL = 175

_DRAIN_OPS = frozenset({
    "tensor_scalar", "tensor_tensor", "scalar_tensor_tensor", "tensor_copy",
    "tensor_tensor_scan", "reduce_max", "reduce_sum", "tensor_reduce",
    "bn_stats", "bn_aggr", "reciprocal", "memset", "activation", "select",
    "copy",
})


class _RecCall:
    __slots__ = ("name", "a", "k", "chain")

    def __init__(self, name, a, k):
        self.name, self.a, self.k = name, a, k
        self.chain = []

    def then_inc(self, sem, n):
        self.chain.append((sem, n))
        return self


class _RecEngine:

    def __init__(self, buf):
        self._buf = buf

    def __getattr__(self, name):
        def call(*a, **k):
            rc = _RecCall(name, a, k)
            self._buf.append(rc)
            return rc
        return call


def _replay(eng, calls):
    for rc in calls:
        inst = getattr(eng, rc.name)(*rc.a, **rc.k)
        for sem, n in rc.chain:
            inst = inst.then_inc(sem, n)


class _AutoDrain:

    def __init__(self, eng):
        self._e = eng
        self._w = set()
        self._r = set()

    def __getattr__(self, name):
        attr = getattr(self._e, name)
        if name in _DRAIN_OPS:
            def w(*a, **k):
                aps = [x for x in list(a) + list(k.values())
                       if isinstance(x, bass.AP)]
                names = [ap.tensor.name for ap in aps]
                out, ins = set(names[:1]), set(names[1:])
                if (ins & self._w) or (out & self._w) or (out & self._r):
                    self._e.drain()
                    self._w.clear()
                    self._r.clear()
                self._w |= out
                self._r |= ins
                return attr(*a, **k)
            return w
        return attr


def bcast_ap(dram_ap, npart, cols):
    return bass.AP(dram_ap.tensor, 0, [[0, npart], [1, cols]])


class Ctx:

    def __init__(self, nc, es):
        self.nc = nc
        self.sems = {}
        self.cnt = {}
        for name in ["dw", "dwa", "dwb", "dx", "do", "doa", "dob", "tE", "vE",
                     "sE", "gE"]:
            self.sems[name] = es.enter_context(nc.semaphore(name))
            self.cnt[name] = 0
        self.bank_rd = {}
        self.merge_scalar = True

    def inc(self, instr, name, n=1):
        instr.then_inc(self.sems[name], n)
        self.cnt[name] += n
        return self.cnt[name]

    def dma(self, instr, name):
        return self.inc(instr, name, 16)

    def wait(self, eng, name, val=None):
        v = self.cnt[name] if val is None else val
        if v:
            eng.wait_ge(self.sems[name], v)

    def bank_guard(self, eng, key):
        if key in self.bank_rd:
            sem, val = self.bank_rd[key]
            if val:
                eng.wait_ge(self.sems[sem], val)

    def bank_mark(self, key, sem):
        self.bank_rd[key] = (sem, self.cnt[sem])


def build(debug=()):
    nc = bass.Bass("TRN2", target_bir_lowering=False)
    debug = set(debug)

    x_in = nc.dram_tensor("x", [L, H], F32, kind="ExternalInput").ap()
    w_in = nc.dram_tensor("w_in", [H, PROJ], F32, kind="ExternalInput").ap()
    conv_w = nc.dram_tensor("conv_w", [CONVD, 4], F32, kind="ExternalInput").ap()
    a_log = nc.dram_tensor("a_log", [1, NH], F32, kind="ExternalInput").ap()
    dt_bias = nc.dram_tensor("dt_bias", [1, NH], F32, kind="ExternalInput").ap()
    d_vec = nc.dram_tensor("d_vec", [1, NH], F32, kind="ExternalInput").ap()
    ln_w = nc.dram_tensor("ln_w", [1, H], F32, kind="ExternalInput").ap()
    norm_w = nc.dram_tensor("norm_w", [1, DS], F32, kind="ExternalInput").ap()
    mup = nc.dram_tensor("mup", [1, PROJ], F32, kind="ExternalInput").ap()
    w_out = nc.dram_tensor("w_out", [DS, H], F32, kind="ExternalInput").ap()
    ffn_a = nc.dram_tensor("ffn_a", [512, H], F32, kind="ExternalInput").ap()
    ffn_b = nc.dram_tensor("ffn_b", [H, 512], F32, kind="ExternalInput").ap()
    ffn_wq = nc.dram_tensor("ffn_wq", [16, H], F32, kind="ExternalInput").ap()
    ffn_keys = nc.dram_tensor("ffn_keys", [NH, 16], F32, kind="ExternalInput").ap()

    out_t = nc.dram_tensor("out", [L, H], F32, kind="ExternalOutput").ap()

    def scratch(name, shape, dtype):
        if name in debug:
            return nc.dram_tensor(name, shape, dtype, kind="ExternalOutput").ap()
        return nc.dram_tensor(name, shape, dtype).ap()

    zspill = scratch("zspill", [L, DS], BF16)
    xsspill = scratch("xsspill", [DS, L], BF16)
    x1spill = scratch("x1spill", [L, H], BF16)

    es = contextlib.ExitStack()
    with es:
        C = Ctx(nc, es)
        block = es.enter_context(nc.Block())

        rbuf = {"scalar": []}
        rec = {e: _RecEngine(rbuf[e]) for e in rbuf}
        wrapped = {"scalar": _AutoDrain(rec["scalar"])}

        def flush_one(e):
            calls = rbuf[e]
            if calls:
                rbuf[e] = []
                rec[e]._buf = rbuf[e]
                getattr(block, e)(lambda eng, calls=calls: _replay(eng, calls))

        def sec(eng_name, fn):
            if eng_name in rbuf and C.merge_scalar:
                fn(wrapped[eng_name])
                if len(rbuf[eng_name]) > 48:
                    flush_one(eng_name)
            elif eng_name == "scalar":
                flush_one("scalar")
                getattr(block, eng_name)(lambda e: fn(_AutoDrain(e)))
            elif eng_name == "vector":
                getattr(block, eng_name)(lambda e: fn(_AutoDrain(e)))
            else:
                getattr(block, eng_name)(fn)

        def flush():
            for e in rbuf:
                flush_one(e)

        def barrier():
            flush()
            snap = dict(C.cnt)
            for e in ALL_ENG:
                def f(eng, snap=snap):
                    for name, v in snap.items():
                        if v:
                            eng.wait_ge(C.sems[name], v)
                sec(e, f)

        A = lambda n, sh, dt: es.enter_context(nc.sbuf_tensor(n, sh, dt))

        ones_f32 = A("ones_f32", [128, 128], F32)
        ones_f16 = A("ones_f16", [128, 128], F16)
        ident_bf = A("ident_bf", [128, 128], BF16)
        ident_f32 = A("ident_f32", [128, 128], F32)
        tri_f16 = A("tri_f16", [128, 128], F16)
        su_f16 = A("su_f16", [128, 128], F16)
        tri_f32 = A("tri_f32", [128, 128], F32)
        e2_f16 = A("e2_f16", [64, DS], F16)
        e512_bf = A("e512_bf", [64, 512], BF16)
        dbias_bc = A("dbias_bc", [128, NH], F32)
        nega_bc = A("nega_bc", [128, NH], F32)
        dcol_bc = A("dcol_bc", [128, NH], F32)
        convwF = A("convwF", [128, 34 * 4], F32)
        iota_f = A("iota_f", [128, 128], F32)
        iota_c = A("iota_c", [128, 1], F32)
        smalls = A("smalls", [128, 32], F32)
        sm64 = A("sm64", [64, 16], F32)
        dtT = A("dtT", [128, NB * NH], F32)
        h_bf = A("h_bf", [128, DS], BF16)
        tailsave = A("tailsave", [128, 34 * 3], BF16)
        bmcf = A("bmcf", [128, 2 * TT], BF16)
        wbuf = [A(f"wbuf{i}", [128, 8192], BF16) for i in range(2)]
        bnb = [A(f"bnb{i}", [128, 24], F32) for i in range(2)]

        pa = [es.enter_context(nc.psum_tensor(f"pa{i}", [128, 512], F32)) for i in range(4)]
        pdm = es.enter_context(nc.psum_tensor("pdm", [128, 1024], F32))
        pb = [es.enter_context(nc.psum_tensor(f"pb{i}", [128, 1024], BF16)) for i in range(2)]

        with contextlib.ExitStack() as sp:
            e2scr = sp.enter_context(nc.sbuf_tensor("e2scr", [64, DS], F32))

            def setup_sync(s):
                C.dma(s.dma_start(convwF[:].rearrange("p (c k) -> p c k", k=4),
                                  conv_w.rearrange("(c p) k -> p c k", p=128)), "dx")
                C.dma(s.dma_start(dbias_bc[:], bcast_ap(dt_bias, 128, NH)), "dx")
                C.dma(s.dma_start(nega_bc[:], bcast_ap(a_log, 128, NH)), "dx")
                C.dma(s.dma_start(dcol_bc[:], bcast_ap(d_vec, 128, NH)), "dx")
            sec("sync", setup_sync)

            def setup_gp(g):
                g.iota(iota_f[:], [[1, 128]], base=0, channel_multiplier=0,
                       allow_small_or_imprecise_dtypes=True)
                g.iota(iota_c[:], [[0, 1]], base=0, channel_multiplier=1,
                       allow_small_or_imprecise_dtypes=True)
                C.inc(g.iota(e2scr[:, :].rearrange("p (a b) -> p a b", b=64),
                             [[1, 64], [0, 64]], base=0, channel_multiplier=0,
                             allow_small_or_imprecise_dtypes=True), "gE")
            sec("gpsimd", setup_gp)

            def setup_v(v):
                C.wait(v, "gE")
                C.wait(v, "dx")
                v.memset(ones_f32[:], 1.0)
                v.memset(ones_f16[:], 1.0)
                v.memset(h_bf[:], 0.0)
                v.memset(tailsave[:], 0.0)
                v.tensor_scalar(ident_f32[:], iota_f[:], iota_c[:], None, OP.is_equal)
                v.tensor_copy(ident_bf[:], ident_f32[:])
                v.tensor_scalar(tri_f32[:], iota_f[:], iota_c[:], None, OP.is_ge)
                v.tensor_copy(tri_f16[:], tri_f32[:])
                v.tensor_scalar(su_f16[:], iota_f[:], iota_c[:], None, OP.is_lt)
                v.tensor_scalar(e2scr[:, :], e2scr[:, :], iota_c[0:64, :],
                                None, OP.is_equal)
                C.inc(v.tensor_copy(e2_f16[:], e2scr[:, :]), "vE")
            sec("vector", setup_v)
            v_e2 = C.cnt["vE"]

            def setup_gp2(g):
                C.wait(g, "vE", v_e2)
                C.inc(g.iota(e2scr[:, 0:512].rearrange("p (a b) -> p a b", b=8),
                             [[1, 64], [0, 8]], base=0, channel_multiplier=0,
                             allow_small_or_imprecise_dtypes=True), "gE")
            sec("gpsimd", setup_gp2)

            def setup_v2(v):
                C.wait(v, "gE")
                v.tensor_scalar(e2scr[:, 0:512], e2scr[:, 0:512], iota_c[0:64, :],
                                None, OP.is_equal)
                C.inc(v.tensor_copy(e512_bf[:], e2scr[:, 0:512]), "vE")
            sec("vector", setup_v2)
            v_e5 = C.cnt["vE"]

            def setup_s(s):
                C.wait(s, "vE", v_e5)
                C.inc(s.activation(nega_bc[:], nega_bc[:], AF.Exp), "sE")
            sec("scalar", setup_s)

            def setup_v3(v):
                C.wait(v, "sE")
                C.inc(v.tensor_scalar(nega_bc[:], nega_bc[:], -1.0, None, OP.mult),
                      "vE")
            sec("vector", setup_v3)

            barrier()

        for tile in range(NT):
            tok0 = tile * TT

            with contextlib.ExitStack() as ph:
                Aph = lambda n, sh, dt: ph.enter_context(
                    nc.sbuf_tensor(f"{n}_t{tile}", sh, dt))
                xhatF = Aph("xhatF", [128, 32 * TT], BF16)

                with contextlib.ExitStack() as p0s:
                    A0 = lambda n, sh, dt: p0s.enter_context(
                        nc.sbuf_tensor(f"{n}_t{tile}", sh, dt))
                    lnw_bc = A0("lnw_bc", [128, H], F16)
                    xblk = [A0(f"xblk{i}", [128, H], F32) for i in range(2)]
                    xhat_tok = A0("xhat_tok", [128, H], BF16)
                    with nc.named_scope(f"P0_t{tile}"):
                        _p0(nc, C, sec, tok0, x_in, ln_w, lnw_bc, xblk, xhat_tok,
                            xhatF, bnb, smalls, ident_bf, pa, pb)
                        barrier()

                with contextlib.ExitStack() as a1s:
                    A1 = lambda n, sh, dt: a1s.enter_context(
                        nc.sbuf_tensor(f"{n}_t{tile}", sh, dt))
                    mup_bc = A1("mup_bc", [128, PROJ], F16)
                    xbct = [A1(f"xbct{i}", [128, EXT], BF16) for i in range(8)]
                    ztmp = A1("ztmp", [128, 512], F32)
                    bst = A1("bst", [128, 512], BF16)
                    accb = A1("accb", [128, TT], F32)
                    zstg = [A1(f"zstg{i}", [128, 512], BF16) for i in range(2)]
                    xstg = [A1(f"xstg{i}", [128, TT], BF16) for i in range(2)]

                    def mup_load(g):
                        C.dma(g.dma_start(mup_bc[:], bcast_ap(mup, 128, PROJ)), "dw")
                    sec("gpsimd", mup_load)

                    with nc.named_scope(f"A1_t{tile}"):
                        _a1(nc, C, sec, tok0, w_in, xhatF, wbuf, mup_bc, ztmp, bst,
                            xbct, accb, convwF, tailsave, bmcf, dtT, xstg, xsspill,
                            ident_bf, pa, pb)
                        barrier()
                    with nc.named_scope(f"A2_t{tile}"):
                        _a2(nc, C, sec, tok0, w_in, xhatF, wbuf, mup_bc, ztmp,
                            zstg, zspill, pa)
                        barrier()

            with contextlib.ExitStack() as ph:
                Aph = lambda n, sh, dt: ph.enter_context(
                    nc.sbuf_tensor(f"{n}_t{tile}", sh, dt))
                yhatF = Aph("yhatF", [128, 32 * TT], BF16)
                nw_bc = Aph("nw_bc", [128, DS], F16)
                ysb = Aph("ysb", [128, DS], BF16)
                sc = {n: Aph(n, sh, dt) for n, sh, dt in [
                    ("xsT", [128, DS], BF16),
                    ("yhat_tok", [128, DS], BF16),
                    ("vexp", [128, 512], F32),
                    ("ytmp", [128, 512], BF16),
                    ("dxst", [128, 512], BF16),
                    ("yA", [128, 512], BF16),
                    ("xtld", [128, 512], BF16),
                    ("hdec", [128, 512], BF16),
                    ("dtc", [128, NH], F32),
                    ("dta32", [128, NH], F32),
                    ("dta16", [128, NH], F16),
                    ("srow", [1, 64], F32),
                    ("sbc", [128, 64], F32),
                    ("t64a", [128, NH], F32),
                    ("t64b", [128, NH], F32),
                    ("t64c", [128, NH], F32),
                ]}
                sc["cbtm"] = [Aph(f"cbtm{i}", [128, 128], F32) for i in range(2)]
                sc["bmt"] = [Aph(f"bmt{i}", [128, 128], BF16) for i in range(2)]
                sc["vf16"] = [Aph(f"vf16{i}", [64, 128], F16) for i in range(2)]
                sc["expd"] = [Aph(f"expd{i}", [128, 1024], F16) for i in range(2)]
                sc["rhsd"] = [Aph(f"rhsd{i}", [128, 512], F16) for i in range(2)]
                sc["mtil"] = [Aph(f"mtil{i}", [128, 1024], BF16) for i in range(2)]
                sc["xsTd"] = [Aph(f"xsTd{i}", [128, 512], BF16) for i in range(2)]
                x1stg = [Aph(f"x1stg{i}", [128, 512], BF16) for i in range(2)]

                def nw_load(g):
                    C.dma(g.dma_start(nw_bc[:], bcast_ap(norm_w, 128, DS)), "dw")
                sec("gpsimd", nw_load)

                with nc.named_scope(f"SCAN_t{tile}"):
                    C.merge_scalar = False
                    X = {"xrd_free": {0: 0, 1: 0}, "zrd_free": {0: 0, 1: 0},
                         "st_last": 0, "scum": 0, "tcu_prev": 0, "vyx_last": 0,
                         "front_v": 0, "vt2": 0, "ytr": 0, "base_g": 0,
                         "dmm": {}, "expfree": {}, "mfree": {}, "dxs": {},
                         "st": {}, "hev": {}}
                    fr = lambda ch: _scan_front(
                        nc, C, sec, tok0, ch, sc, wbuf, xsspill, zspill, dtT,
                        dbias_bc, nega_bc, bmcf, ones_f16, ones_f32, tri_f16,
                        su_f16, tri_f32, ident_bf, ident_f32, sm64, pa, pb, X)
                    fr(0)
                    for ch in range(NB):
                        _scan_groups(nc, C, sec, tok0, ch, sc, wbuf, dcol_bc,
                                     h_bf, bmcf, ysb, e2_f16, bnb, tri_f16,
                                     su_f16, ident_bf, pa, pdm, pb, X)
                        if ch + 1 < NB:
                            fr(ch + 1)
                        _scan_tail(nc, C, sec, tok0, ch, sc, nw_bc, ysb, yhatF,
                                   ident_bf, smalls, sm64, bnb, pa, pb, X)
                    barrier()
                    C.merge_scalar = True
                with nc.named_scope(f"E_t{tile}"):
                    _e_outproj(nc, C, sec, tok0, w_out, yhatF, wbuf, x1stg,
                               x1spill, pa)
                    barrier()

            with contextlib.ExitStack() as ph:
                Aph = lambda n, sh, dt: ph.enter_context(
                    nc.sbuf_tensor(f"{n}_t{tile}", sh, dt))
                awT = Aph("awT", [128, 32 * 512], BF16)
                bwT = Aph("bwT", [128, 4 * H], BF16)
                mT = Aph("mT", [128, 32 * 64], BF16)
                keysT = Aph("keysT", [16, 64], BF16)
                x2F = Aph("x2F", [128, 32 * 128], BF16)
                x2tok = [Aph(f"x2tok{i}", [128, H], BF16) for i in range(2)]
                bnb2 = [Aph(f"bnb2_{i}", [128, 24], F32) for i in range(2)]
                x3f = Aph("x3f", [128, H], F32)
                xblk2 = Aph("xblk2", [128, H], F32)
                ysb2 = Aph("ysb2", [128, H], F32)
                fscr = Aph("fscr", [128, 512], F32)
                fscr2 = Aph("fscr2", [128, 64], F32)
                gq = Aph("gq", [128, 64], F32)
                gwork = Aph("gwork", [128, 64], F32)
                zhat = Aph("zhat", [128, 512], BF16)
                zhT = Aph("zhT", [128, 512], BF16)
                rcols = [Aph(f"rcol{i}", [128, 1], F32) for i in range(4)]

                with nc.named_scope(f"FP_t{tile}"):
                    _f_prep(nc, C, sec, ffn_a, ffn_b, ffn_wq, ffn_keys, wbuf,
                            awT, bwT, mT, keysT, ident_bf, pa, pb)
                    barrier()
                do_at = {}
                Xf = {"aq": {}, "x2Fev": {}, "x2tr": {}, "s1": {}, "vln": {},
                      "tb": {}}
                with nc.named_scope(f"F_t{tile}"):
                    fx = lambda b: _f_x(nc, C, sec, tok0, b, x_in, x1spill,
                                        xblk2, ysb2, x2F, x2tok, awT, mT, bnb,
                                        smalls, ident_bf, pa, pb, Xf)
                    fy = lambda b: _f_y(nc, C, sec, tok0, b, out_t, x3f, x2tok,
                                        bwT, fscr, fscr2, gq, gwork, wbuf[1], zhat,
                                        zhT, bnb2, smalls, rcols, ident_bf, pa,
                                        pdm, pb, do_at, Xf)
                    fx(0)
                    for b in range(NB):
                        if b + 1 < NB:
                            fx(b + 1)
                        fy(b)
                    barrier()

        def fin(s):
            C.wait(s, "do")
            C.wait(s, "doa")
            C.wait(s, "dob")
        sec("sync", fin)
        flush()

    return nc



def _ln_stats(C, v, bnb, src_ap, ncols, mu_col, m2_col, sm, c0, want_meansq):
    nst = ncols // 512
    for i in range(nst):
        v.bn_stats(bnb[i % 2][:, (i // 2) * 6:(i // 2) * 6 + 6],
                   src_ap[:, i * 512:(i + 1) * 512])
    v.bn_aggr(sm[:, c0:c0 + 2], bnb[0][:, 0:(nst // 2) * 6])
    v.bn_aggr(sm[:, c0 + 2:c0 + 4], bnb[1][:, 0:(nst // 2) * 6])
    mA, vA = sm[:, c0:c0 + 1], sm[:, c0 + 1:c0 + 2]
    mB, vB = sm[:, c0 + 2:c0 + 3], sm[:, c0 + 3:c0 + 4]
    t0, t1 = sm[:, c0 + 4:c0 + 5], sm[:, c0 + 5:c0 + 6]
    v.tensor_tensor(mu_col, mA, mB, OP.add)
    v.tensor_scalar(mu_col, mu_col, 0.5, None, OP.mult)
    v.tensor_tensor(t0, mA, mB, OP.subtract)
    v.tensor_tensor(t0, t0, t0, OP.mult)
    v.tensor_tensor(t1, vA, vB, OP.add)
    v.tensor_scalar(t1, t1, 0.5, None, OP.mult)
    v.scalar_tensor_tensor(m2_col, t0, 0.25, t1, OP.mult, OP.add)
    if want_meansq:
        v.tensor_tensor(t0, mu_col, mu_col, OP.mult)
        v.tensor_tensor(m2_col, m2_col, t0, OP.add)
    v.tensor_scalar(m2_col, m2_col, EPS, None, OP.add)


def _rstd_finish(s, src_col, dst_col, tmp_col):
    s.activation(tmp_col, src_col, AF.Ln)
    return s.activation(dst_col, tmp_col, AF.Exp, scale=-0.5)


def _p0(nc, C, sec, tok0, x_in, ln_w, lnw_bc, xblk, xhat_tok, xhatF, bnb,
        smalls, ident_bf, pa, pb):
    dx_at = {}
    v_stt = {}
    t_done = {}

    def g_lnw(g):
        C.dma(g.dma_start(lnw_bc[:], bcast_ap(ln_w, 128, H)), "dw")
    sec("gpsimd", g_lnw)
    dw_lnw = C.cnt["dw"]

    def d_pre(s):
        for b in range(2):
            dx_at[b] = C.dma(s.dma_start(
                xblk[b % 2][:], x_in[tok0 + b * 128:tok0 + (b + 1) * 128, :]), "dx")
    sec("sync", d_pre)

    for b in range(NB):
        def vf(v, b=b, dxf=None):
            C.wait(v, "dx")
            if b == 0:
                C.wait(v, "dw", dw_lnw)
            _ln_stats(C, v, bnb, xblk[b % 2][:], H, smalls[:, 8:9], smalls[:, 9:10],
                      smalls, 12, want_meansq=True)
            C.inc(v.tensor_copy(smalls[:, 18:19], smalls[:, 9:10]), "vE")
        sec("vector", vf)
        v_bn = C.cnt["vE"]

        def sf(s, b=b):
            C.wait(s, "vE", v_bn)
            C.inc(_rstd_finish(s, smalls[:, 18:19], smalls[:, 10:11],
                               smalls[:, 11:12]), "sE")
        sec("scalar", sf)
        s_rs = C.cnt["sE"]

        def vg(v, b=b):
            C.wait(v, "sE", s_rs)
            if b >= 1:
                C.wait(v, "tE", t_done[b - 1])
            C.inc(v.scalar_tensor_tensor(xhat_tok[:], xblk[b % 2][:],
                                         smalls[:, 10:11], lnw_bc[:],
                                         OP.mult, OP.mult), "vE")
        sec("vector", vg)
        v_stt[b] = C.cnt["vE"]

        if b + 2 < NB:
            def df(s, b=b):
                C.wait(s, "vE", v_stt[b])
                dx_at[b + 2] = C.dma(s.dma_start(
                    xblk[b % 2][:],
                    x_in[tok0 + (b + 2) * 128:tok0 + (b + 3) * 128, :]), "dx")
            sec("sync", df)

        for jb in range(4):
            def tf(t, b=b, jb=jb):
                if jb == 0:
                    C.wait(t, "vE", v_stt[b])
                key = f"pb{jb % 2}"
                C.bank_guard(t, key)
                for i in range(8):
                    c = jb * 8 + i
                    mm = t.transpose(pb[jb % 2][:, i * 128:(i + 1) * 128],
                                     xhat_tok[:, c * 128:(c + 1) * 128], ident_bf[:])
                C.inc(mm, "tE")
            sec("tensor", tf)
            t_tr = C.cnt["tE"]

            def sg(v, b=b, jb=jb, t_tr=t_tr):
                v.wait_ge(C.sems["tE"], t_tr)
                dstap = xhatF[:, jb * 8 * TT:(jb * 8 + 8) * TT] \
                    .rearrange("p (c t) -> p c t", t=TT)[:, :, b * 128:(b + 1) * 128]
                C.inc(v.tensor_copy(
                    dstap, pb[jb % 2][:, 0:1024].rearrange("p (c t) -> p c t", t=128)),
                    "vE")
                C.bank_mark(f"pb{jb % 2}", "vE")
            sec("vector", sg)
        t_done[b] = C.cnt["tE"]


def _a1(nc, C, sec, tok0, w_in, xhatF, wbuf, mup_bc, ztmp, bst, xbct, accb,
        convwF, tailsave, bmcf, dtT, xstg, xsspill, ident_bf, pa, pb):
    n_slab = 18
    dw_at = {}
    mm_done = {}
    ev_done = {}
    tg_at = {}
    conv_tap = {}
    spill_at = {}
    s_silu_at = {}

    def pre(g):
        def f(gp, g=g):
            col0 = 4096 + g * 256
            w = 64 if g == 17 else 256
            if g >= 2:
                gp.wait_ge(C.sems["tE"], mm_done[g - 2])
            dw_at[g] = C.dma(gp.dma_start(
                wbuf[g % 2][:, 0:32 * w].rearrange("p (c f) -> p c f", f=w),
                w_in.rearrange("(c p) f -> p c f", p=128)[:, :, col0:col0 + w]),
                "dwa" if g % 2 == 0 else "dwb")
        sec("gpsimd", f)

    pre(0)
    pre(1)

    for g in range(n_slab):
        col0 = 4096 + g * 256
        w = 64 if g == 17 else 256

        if g < 16:
            def gtail(gp, g=g):
                if 2 * g - 8 >= 0:
                    gp.wait_ge(C.sems["vE"], conv_tap[2 * g + 1 - 8])
                cc0 = 2 * g
                gp.tensor_copy(xbct[cc0 % 8][:, 0:3],
                               tailsave[:, cc0 * 3:cc0 * 3 + 3])
                C.inc(gp.tensor_copy(xbct[(cc0 + 1) % 8][:, 0:3],
                                     tailsave[:, (cc0 + 1) * 3:(cc0 + 1) * 3 + 3]),
                      "gE")
            sec("gpsimd", gtail)
        g_tail_in = C.cnt["gE"]

        for b in range(NB):
            idx = g * NB + b

            def tf(t, g=g, b=b, w=w, idx=idx):
                if b == 0:
                    C.wait(t, "dwa" if g % 2 == 0 else "dwb")
                C.bank_guard(t, f"pa{idx % 3}")
                for c in range(32):
                    mm = t.matmul(pa[idx % 3][:, 0:w],
                                  xhatF[:, c * TT + b * 128:c * TT + (b + 1) * 128],
                                  wbuf[g % 2][:, c * w:(c + 1) * w],
                                  start=(c == 0), stop=(c == 31))
                C.inc(mm, "tE")
                mm_done[g] = C.cnt["tE"]
            sec("tensor", tf)
            t_mm = C.cnt["tE"]

            def vf(v, g=g, b=b, w=w, col0=col0, idx=idx, t_mm=t_mm):
                C.wait(v, "tE", t_mm)
                if g < 17 and idx - 1 in tg_at:
                    C.wait(v, "tE", tg_at[idx - 1])
                if g == 17:
                    C.inc(v.tensor_tensor(dtT[:, b * 64:(b + 1) * 64],
                                          pa[idx % 3][:, 0:64],
                                          mup_bc[:, col0:col0 + 64], OP.mult), "vE")
                else:
                    C.inc(v.tensor_tensor(bst[:, 0:w], pa[idx % 3][:, 0:w],
                                          mup_bc[:, col0:col0 + w], OP.mult), "vE")
                C.bank_mark(f"pa{idx % 3}", "vE")
                ev_done[idx] = C.cnt["vE"]
            sec("vector", vf)
            v_ev = C.cnt["vE"]

            if g < 17:
                def tg(t, g=g, b=b, idx=idx, v_ev=v_ev):
                    C.wait(t, "vE", v_ev)
                    key = f"pb{idx % 2}"
                    C.bank_guard(t, key)
                    for i in range(2):
                        mm = t.transpose(pb[idx % 2][:, i * 128:(i + 1) * 128],
                                         bst[:, i * 128:(i + 1) * 128], ident_bf[:])
                    C.inc(mm, "tE")
                    tg_at[idx] = C.cnt["tE"]
                sec("tensor", tg)
                t_tr = C.cnt["tE"]

                def sg(v, g=g, b=b, idx=idx, t_tr=t_tr):
                    v.wait_ge(C.sems["tE"], t_tr)
                    last = None
                    for i in range(2):
                        cc = 2 * g + i
                        if cc < 32:
                            dst = xbct[cc % 8][:, 3 + b * 128:3 + (b + 1) * 128]
                        else:
                            dst = bmcf[:, (cc - 32) * TT + b * 128:
                                       (cc - 32) * TT + (b + 1) * 128]
                        last = v.tensor_copy(dst, pb[idx % 2][:, i * 128:(i + 1) * 128])
                    C.inc(last, "vE")
                    C.bank_mark(f"pb{idx % 2}", "vE")
                sec("vector", sg)

        if g + 2 < n_slab:
            pre(g + 2)

        if g < 16:
            for i in range(2):
                cc = 2 * g + i
                _conv_chunk(nc, C, sec, tok0, cc, xbct[cc % 8], accb, convwF,
                            tailsave, xstg, xsspill, bmcf, conv_tap, spill_at)
        elif g == 16:
            for i in range(2):
                cc = 32 + i
                _conv_chunk(nc, C, sec, tok0, cc, None, accb, convwF,
                            tailsave, xstg, xsspill, bmcf, conv_tap, spill_at)


def _conv_chunk(nc, C, sec, tok0, cc, buf, accb, convwF, tailsave, xstg,
                xsspill, bmcf, conv_tap, spill_at):
    if cc >= 32:
        data = bmcf[:, (cc - 32) * TT:(cc - 32 + 1) * TT]
    else:
        data = buf[:, 3:3 + TT]
    sE_now = C.cnt["sE"]
    vE_now = C.cnt["vE"]
    gE_now = C.cnt["gE"]

    def vf(v, cc=cc, data=data, sE_now=sE_now, vE_now=vE_now, gE_now=gE_now):
        C.wait(v, "sE", sE_now)
        C.wait(v, "vE", vE_now)
        C.wait(v, "gE", gE_now)
        w = lambda k: convwF[:, cc * 4 + k:cc * 4 + k + 1]
        if cc < 32:
            ext = buf
            v.tensor_scalar(accb[:], ext[:, 0:TT], w(0), None, OP.mult)
            for k in range(1, 3):
                v.scalar_tensor_tensor(accb[:], ext[:, k:k + TT], w(k), accb[:],
                                       OP.mult, OP.add)
            last = v.scalar_tensor_tensor(accb[:], ext[:, 3:3 + TT], w(3), accb[:],
                                          OP.mult, OP.add)
        else:
            ts3 = tailsave[:, cc * 3:cc * 3 + 3]
            v.tensor_scalar(accb[:], data[:], w(3), None, OP.mult)
            for k in range(3):
                sh = 3 - k
                v.scalar_tensor_tensor(accb[:, sh:TT], data[:, 0:TT - sh], w(k),
                                       accb[:, sh:TT], OP.mult, OP.add)
                last = v.scalar_tensor_tensor(accb[:, 0:sh], ts3[:, 3 - sh:3], w(k),
                                              accb[:, 0:sh], OP.mult, OP.add)
        C.inc(last, "vE")
        conv_tap[cc] = C.cnt["vE"]
    sec("vector", vf)
    v_tap = C.cnt["vE"]

    def gp2(g, cc=cc, data=data, v_tap=v_tap):
        C.wait(g, "vE", v_tap)
        C.inc(g.tensor_copy(tailsave[:, cc * 3:cc * 3 + 3], data[:, TT - 3:TT]), "gE")
    sec("gpsimd", gp2)
    g_tail = C.cnt["gE"]

    def sf(s, cc=cc, data=data, v_tap=v_tap, g_tail=g_tail):
        C.wait(s, "vE", v_tap)
        C.wait(s, "gE", g_tail)
        if cc < 32:
            if cc >= 2:
                C.wait(s, "doa" if cc % 2 == 0 else "dob")
            dst = xstg[cc % 2][:]
        else:
            dst = data
        C.inc(s.activation(dst, accb[:], AF.Silu), "sE")
    sec("scalar", sf)
    s_silu = C.cnt["sE"]

    if cc < 32:
        def df(sy, cc=cc, s_silu=s_silu):
            C.wait(sy, "sE", s_silu)
            spill_at[cc] = C.dma(sy.dma_start(
                xsspill[cc * 128:(cc + 1) * 128, tok0:tok0 + TT], xstg[cc % 2][:]),
                "doa" if cc % 2 == 0 else "dob")
        sec("sync", df)


def _a2(nc, C, sec, tok0, w_in, xhatF, wbuf, mup_bc, ztmp, zstg, zspill, pa):
    dw_at = {}
    mm_done = {}
    do_at = {}
    sz_at = {}

    def pre(g):
        def f(gp, g=g):
            col0 = g * 256
            if g >= 2:
                gp.wait_ge(C.sems["tE"], mm_done[g - 2])
            dw_at[g] = C.dma(gp.dma_start(
                wbuf[g % 2][:, 0:32 * 256].rearrange("p (c f) -> p c f", f=256),
                w_in.rearrange("(c p) f -> p c f", p=128)[:, :, col0:col0 + 256]),
                "dwa" if g % 2 == 0 else "dwb")
        sec("gpsimd", f)

    pre(0)
    pre(1)
    for g in range(16):
        col0 = g * 256
        for b in range(NB):
            idx = g * NB + b

            def tf(t, g=g, b=b, idx=idx):
                if b == 0:
                    C.wait(t, "dwa" if g % 2 == 0 else "dwb")
                C.bank_guard(t, f"pa{idx % 3}")
                for c in range(32):
                    mm = t.matmul(pa[idx % 3][:, 0:256],
                                  xhatF[:, c * TT + b * 128:c * TT + (b + 1) * 128],
                                  wbuf[g % 2][:, c * 256:(c + 1) * 256],
                                  start=(c == 0), stop=(c == 31))
                C.inc(mm, "tE")
                mm_done[g] = C.cnt["tE"]
            sec("tensor", tf)
            t_mm = C.cnt["tE"]

            def vf(v, col0=col0, idx=idx, t_mm=t_mm):
                C.wait(v, "tE", t_mm)
                if idx - 1 in sz_at:
                    C.wait(v, "sE", sz_at[idx - 1])
                C.inc(v.tensor_tensor(ztmp[:, 0:256], pa[idx % 3][:, 0:256],
                                      mup_bc[:, col0:col0 + 256], OP.mult), "vE")
                C.bank_mark(f"pa{idx % 3}", "vE")
            sec("vector", vf)
            v_ev = C.cnt["vE"]

            def sf(s, idx=idx, v_ev=v_ev):
                C.wait(s, "vE", v_ev)
                if idx >= 2:
                    C.wait(s, "doa" if idx % 2 == 0 else "dob")
                C.inc(s.activation(zstg[idx % 2][:, 0:256], ztmp[:, 0:256],
                                   AF.Silu), "sE")
                sz_at[idx] = C.cnt["sE"]
            sec("scalar", sf)
            s_z = C.cnt["sE"]

            def df(sy, b=b, col0=col0, idx=idx, s_z=s_z):
                C.wait(sy, "sE", s_z)
                do_at[idx] = C.dma(sy.dma_start(
                    zspill[tok0 + b * 128:tok0 + (b + 1) * 128, col0:col0 + 256],
                    zstg[idx % 2][:, 0:256]), "doa" if idx % 2 == 0 else "dob")
            sec("sync", df)
        if g + 2 < 16:
            pre(g + 2)


def _bc(ap, dims):
    return bass.AP(ap.tensor, ap.offset, dims)


def _scan_front(nc, C, sec, tok0, ch, sc, wbuf, xsspill, zspill, dtT,
                dbias_bc, nega_bc, bmcf, ones_f16, ones_f32, tri_f16,
                su_f16, tri_f32, ident_bf, ident_f32, sm64, pa, pb, X):
    t0c = tok0 + ch * 128
    par = ch % 2
    xrd = wbuf[0][:, par * 4096:par * 4096 + 4096]
    zrd = wbuf[1][:, par * 4096:par * 4096 + 4096]

    def d0(sy, par=par):
        if X["xrd_free"][par]:
            sy.wait_ge(C.sems["tE"], X["xrd_free"][par])
        if X["zrd_free"][par]:
            sy.wait_ge(C.sems["vE"], X["zrd_free"][par])
        C.dma(sy.dma_start(
            xrd.rearrange("p (c t) -> p c t", t=128),
            xsspill.rearrange("(c p) t -> p c t", p=128)[:, :, t0c:t0c + 128]), "dx")
        C.dma(sy.dma_start(zrd, zspill[t0c:t0c + 128, :]), "dx")
    sec("sync", d0)
    dxv = C.cnt["dx"]

    for jb in range(4):
        def tf(t, jb=jb, dxv=dxv):
            if jb == 0:
                C.wait(t, "dx", dxv)
            key = f"pb{jb % 2}"
            C.bank_guard(t, key)
            for i in range(8):
                c = jb * 8 + i
                mm = t.transpose(pb[jb % 2][:, i * 128:(i + 1) * 128],
                                 xrd[:, c * 128:(c + 1) * 128], ident_bf[:])
            C.inc(mm, "tE")
        sec("tensor", tf)
        t_tr = C.cnt["tE"]

        def sg(v, jb=jb, t_tr=t_tr):
            v.wait_ge(C.sems["tE"], t_tr)
            C.inc(v.tensor_copy(sc["xsT"][:, jb * 1024:(jb + 1) * 1024],
                                pb[jb % 2][:, 0:1024]), "vE")
            C.bank_mark(f"pb{jb % 2}", "vE")
        sec("vector", sg)
    X["xrd_free"][par] = C.cnt["tE"]

    def v_dt(v):
        dts = dtT[:, ch * 64:(ch + 1) * 64]
        if X["scum"]:
            C.wait(v, "sE", X["scum"])
        v.tensor_tensor(sc["t64a"][:], dts, dbias_bc[:], OP.add)
        C.inc(v.scalar_tensor_tensor(sc["t64b"][:], sc["t64a"][:], -1.0,
                                     sc["t64a"][:], OP.mult, OP.min), "vE")
    sec("vector", v_dt)
    v_1 = C.cnt["vE"]

    def s_dt(s, v_1=v_1):
        C.wait(s, "vE", v_1)
        s.activation(sc["t64c"][:], sc["t64b"][:], AF.Exp)
        C.inc(s.activation(sc["t64b"][:], sc["t64c"][:], AF.Ln, bias=1.0), "sE")
    sec("scalar", s_dt)
    s_1 = C.cnt["sE"]

    def v_dt2(v, s_1=s_1):
        C.wait(v, "sE", s_1)
        if X["tcu_prev"]:
            v.wait_ge(C.sems["tE"], X["tcu_prev"])
        v.tensor_scalar(sc["t64c"][:], sc["t64a"][:], 0.0, None, OP.max)
        v.tensor_tensor(sc["dtc"][:], sc["t64b"][:], sc["t64c"][:], OP.add)
        v.tensor_tensor(sc["dta32"][:], sc["dtc"][:], nega_bc[:], OP.mult)
        C.inc(v.tensor_copy(sc["dta16"][:], sc["dta32"][:]), "vE")
    sec("vector", v_dt2)
    v_2 = C.cnt["vE"]

    def t_cum(t, v_2=v_2):
        C.wait(t, "vE", v_2)
        C.bank_guard(t, "pa3")
        t.matmul(pa[3][0:64, 0:128], sc["dta16"][:], tri_f16[:], start=True,
                 stop=True)
        t.matmul(pa[3][:, 128:192], su_f16[:], sc["dta16"][:], start=True,
                 stop=True)
        C.inc(t.matmul(pa[3][0:1, 192:256], ones_f16[:, 0:1], sc["dta16"][:],
                       start=True, stop=True), "tE")
    sec("tensor", t_cum)
    t_cu = C.cnt["tE"]
    X["tcu_prev"] = t_cu

    def s_cum(s, t_cu=t_cu, par=par):
        C.wait(s, "tE", t_cu)
        if X["vyx_last"]:
            C.wait(s, "vE", X["vyx_last"])
        s.activation(sc["vf16"][par][:], pa[3][0:64, 0:128], AF.Exp)
        C.inc(s.activation(sc["t64a"][:], pa[3][:, 128:192], AF.Exp), "sE")
        C.bank_mark("pa3", "sE")
    sec("scalar", s_cum)
    s_2 = C.cnt["sE"]
    X["scum"] = s_2

    def v_sb(v, t_cu=t_cu):
        v.wait_ge(C.sems["tE"], t_cu)
        C.inc(v.tensor_copy(sc["srow"][:], pa[3][0:1, 192:256]), "vE")
    sec("vector", v_sb)
    v_sr = C.cnt["vE"]

    def t_sb2(t, v_sr=v_sr):
        C.wait(t, "vE", v_sr)
        C.bank_guard(t, "pa3")
        C.inc(t.matmul(pa[3][:, 448:512], ones_f32[0:1, :], sc["srow"][:],
                       start=True, stop=True), "tE")
    sec("tensor", t_sb2)
    t_sb2a = C.cnt["tE"]

    def s_sb(s, t_sb2a=t_sb2a):
        s.wait_ge(C.sems["tE"], t_sb2a)
        C.inc(s.activation(sc["sbc"][:], pa[3][:, 448:512], AF.Exp), "sE")
        C.bank_mark("pa3", "sE")
    sec("scalar", s_sb)

    def t_cb(t, par=par):
        C.wait(t, "vE")
        C.bank_guard(t, "pa3")
        t.matmul(pa[3][:, 256:384], bmcf[:, ch * 128:(ch + 1) * 128],
                 bmcf[:, TT + ch * 128:TT + (ch + 1) * 128], start=True, stop=True)
        C.bank_guard(t, "pb0")
        C.inc(t.transpose(pb[0][:, 0:128], bmcf[:, ch * 128:(ch + 1) * 128],
                          ident_bf[:]), "tE")
    sec("tensor", t_cb)
    t_cb_at = C.cnt["tE"]

    def v_cb(v, t_cb_at=t_cb_at, par=par):
        v.wait_ge(C.sems["tE"], t_cb_at)
        v.tensor_tensor(sc["cbtm"][par][:], pa[3][:, 256:384], tri_f32[:], OP.mult)
        C.bank_mark("pa3", "vE")
        C.inc(v.tensor_copy(sc["bmt"][par][:], pb[0][:, 0:128]), "vE")
        C.bank_mark("pb0", "vE")
    sec("vector", v_cb)
    X["front_v"] = C.cnt["vE"]


def _scan_groups(nc, C, sec, tok0, ch, sc, wbuf, dcol_bc, h_bf, bmcf, ysb,
                 e2_f16, bnb, tri_f16, su_f16, ident_bf, pa, pdm, pb, X):
    par = ch % 2
    zrd = wbuf[1][:, par * 4096:par * 4096 + 4096]
    front_v = X["front_v"]

    for g in range(8):
        gg = X["base_g"] + g
        heads = list(range(8 * g, 8 * g + 8))

        for half in range(2):
            hh = 2 * gg + half
            h0 = 8 * g + half * 4

            def v_r(v, hh=hh, h0=h0, front_v=front_v):
                C.wait(v, "vE", front_v)
                if X["dmm"].get(hh - 2):
                    v.wait_ge(C.sems["tE"], X["dmm"][hh - 2])
                C.inc(v.tensor_tensor(
                    sc["rhsd"][hh % 2][:].rearrange("p (a b) -> p a b", b=128),
                    _bc(tri_f16[:], [[128, 128], [0, 4], [1, 128]]),
                    _bc(sc["dta32"][:, h0:h0 + 4], [[64, 128], [1, 4], [0, 128]]),
                    OP.mult), "vE")
            sec("vector", v_r)
            v_rh = C.cnt["vE"]

            def t_d(t, hh=hh, v_rh=v_rh):
                C.wait(t, "vE", v_rh)
                key = f"pd{hh % 2}"
                C.bank_guard(t, key)
                half0 = (hh % 2) * 512
                for i in range(4):
                    mm = t.matmul(pdm[:, half0 + i * 128:half0 + (i + 1) * 128],
                                  su_f16[:],
                                  sc["rhsd"][hh % 2][:, i * 128:(i + 1) * 128],
                                  start=(i == 0), stop=(i == 3))
                C.inc(mm, "tE")
            sec("tensor", t_d)
            X["dmm"][hh] = C.cnt["tE"]

        def s_e(s, gg=gg):
            s.wait_ge(C.sems["tE"], X["dmm"][2 * gg + 1])
            if X["expfree"].get(gg - 2):
                C.wait(s, "vE", X["expfree"][gg - 2])
            C.inc(s.activation(sc["expd"][gg % 2][:], pdm[:, 0:1024], AF.Exp),
                  "sE")
            C.bank_mark("pd0", "sE")
            C.bank_mark("pd1", "sE")
        sec("scalar", s_e)
        exp_g = C.cnt["sE"]

        def v_gA(v, g=g, gg=gg, par=par, exp_g=exp_g):
            C.wait(v, "sE", exp_g)
            if X["mfree"].get(gg - 2):
                v.wait_ge(C.sems["tE"], X["mfree"][gg - 2])
            v.tensor_tensor(
                sc["mtil"][gg % 2][:].rearrange("p (a b) -> p a b", b=128),
                sc["expd"][gg % 2][:].rearrange("p (a b) -> p a b", b=128),
                _bc(sc["cbtm"][par][:], [[128, 128], [0, 8], [1, 128]]),
                OP.mult)
            v.tensor_tensor(
                sc["xsTd"][gg % 2][:].rearrange("p (a b) -> p a b", b=64),
                sc["xsT"][:, g * 512:(g + 1) * 512].rearrange("p (a b) -> p a b", b=64),
                _bc(sc["dtc"][:, g * 8:g * 8 + 8], [[64, 128], [1, 8], [0, 64]]),
                OP.mult)
            C.inc(v.tensor_tensor(
                sc["dxst"][:].rearrange("p (a b) -> p a b", b=64),
                sc["xsT"][:, g * 512:(g + 1) * 512].rearrange("p (a b) -> p a b", b=64),
                _bc(dcol_bc[:, g * 8:g * 8 + 8], [[64, 128], [1, 8], [0, 64]]),
                OP.mult), "vE")
        sec("vector", v_gA)
        v_ga = C.cnt["vE"]
        X["expfree"][gg] = v_ga

        def t_iv(t, g=g, gg=gg, par=par, v_ga=v_ga):
            C.wait(t, "vE", v_ga)
            C.bank_guard(t, "pa1")
            for i in range(8):
                t.matmul(pa[1][:, i * 64:(i + 1) * 64],
                         sc["mtil"][gg % 2][:, i * 128:(i + 1) * 128],
                         sc["xsTd"][gg % 2][:, i * 64:(i + 1) * 64],
                         start=(i == 0), stop=(i == 7))
            C.bank_guard(t, "pa0")
            if X["hev"].get(g):
                t.wait_ge(C.sems["vE"], X["hev"][g])
            for i, r in enumerate(heads):
                mm = t.matmul(pa[0][:, i * 64:(i + 1) * 64],
                              bmcf[:, TT + ch * 128:TT + (ch + 1) * 128],
                              h_bf[:, r * 64:(r + 1) * 64],
                              start=(i == 0), stop=(i == 7))
            C.inc(mm, "tE")
            key = f"pd{g % 2}"
            C.bank_guard(t, key)
            C.inc(t.matmul(pdm[:, (g % 2) * 512:(g % 2) * 512 + 512],
                           sc["vf16"][par][:], e2_f16[:, g * 512:(g + 1) * 512],
                           start=True, stop=True), "tE")
        sec("tensor", t_iv)
        t_ve = C.cnt["tE"]
        t_int = t_ve - 1
        X["mfree"][gg] = t_int

        def v_gB(v, g=g, t_ve=t_ve, t_int=t_int):
            v.wait_ge(C.sems["tE"], t_ve)
            v.tensor_copy(sc["vexp"][:], pdm[:, (g % 2) * 512:(g % 2) * 512 + 512])
            C.bank_mark(f"pd{g % 2}", "vE")
            C.inc(v.scalar_tensor_tensor(sc["ytmp"][:], pa[0][:, 0:512], 1.0,
                                         sc["vexp"][:], OP.mult, OP.mult), "vE")
            C.bank_mark("pa0", "vE")
        sec("vector", v_gB)
        v_yt = C.cnt["vE"]

        def v_gC(v, g=g, gg=gg, par=par, t_int=t_int):
            v.scalar_tensor_tensor(sc["yA"][:], pa[1][:, 0:512], 1.0,
                                   sc["ytmp"][:], OP.mult, OP.add)
            C.bank_mark("pa1", "vE")
            v.scalar_tensor_tensor(sc["yA"][:], sc["dxst"][:], 1.0, sc["yA"][:],
                                   OP.mult, OP.add)
            v.tensor_tensor(ysb[:, g * 512:(g + 1) * 512], sc["yA"][:],
                            zrd[:, g * 512:(g + 1) * 512], OP.mult)
            v.bn_stats(bnb[g % 2][:, (g // 2) * 6:(g // 2) * 6 + 6],
                       ysb[:, g * 512:(g + 1) * 512])
            if X["st"].get(gg - 1):
                v.wait_ge(C.sems["tE"], X["st"][gg - 1])
            v.tensor_tensor(
                sc["xtld"][:].rearrange("p (a b) -> p a b", b=64),
                sc["xsTd"][gg % 2][:].rearrange("p (a b) -> p a b", b=64),
                _bc(sc["t64a"][:, g * 8:g * 8 + 8], [[64, 128], [1, 8], [0, 64]]),
                OP.mult)
            C.inc(v.tensor_tensor(
                sc["hdec"][:].rearrange("p (a b) -> p a b", b=64),
                h_bf[:, g * 512:(g + 1) * 512].rearrange("p (a b) -> p a b", b=64),
                _bc(sc["sbc"][:, g * 8:g * 8 + 8], [[64, 128], [1, 8], [0, 64]]),
                OP.mult), "vE")
        sec("vector", v_gC)
        v_xt = C.cnt["vE"]

        def t_st(t, g=g, gg=gg, par=par, v_xt=v_xt):
            C.wait(t, "vE", v_xt)
            C.bank_guard(t, "pa2")
            for i in range(8):
                t.matmul(pa[2][:, i * 64:(i + 1) * 64], sc["bmt"][par][:],
                         sc["xtld"][:, i * 64:(i + 1) * 64],
                         start=(i == 0), stop=False)
            C.inc(t.matmul(pa[2][:, 0:512], ident_bf[:], sc["hdec"][:],
                           start=False, stop=True), "tE")
            X["st"][gg] = C.cnt["tE"]
        sec("tensor", t_st)
        t_hu = C.cnt["tE"]
        if g == 7:
            X["st_last"] = t_hu

        def v_h(v, g=g, t_hu=t_hu):
            v.wait_ge(C.sems["tE"], t_hu)
            C.inc(v.tensor_copy(h_bf[:, g * 512:(g + 1) * 512], pa[2][:, 0:512]),
                  "vE")
            C.bank_mark("pa2", "vE")
            X["hev"][g] = C.cnt["vE"]
        sec("vector", v_h)

    X["vyx_last"] = C.cnt["vE"]
    X["zrd_free"][par] = C.cnt["vE"]
    X["base_g"] += 8


def _scan_tail(nc, C, sec, tok0, ch, sc, nw_bc, ysb, yhatF, ident_bf, smalls,
               sm64, bnb, pa, pb, X):
    def v_tail(v):
        v.bn_aggr(smalls[:, 12:14], bnb[0][:, 0:24])
        v.bn_aggr(smalls[:, 20:22], bnb[1][:, 0:24])
        mA, vA = smalls[:, 12:13], smalls[:, 13:14]
        mB, vB = smalls[:, 20:21], smalls[:, 21:22]
        t0, t1 = smalls[:, 16:17], smalls[:, 17:18]
        v.tensor_tensor(smalls[:, 8:9], mA, mB, OP.add)
        v.tensor_scalar(smalls[:, 8:9], smalls[:, 8:9], 0.5, None, OP.mult)
        v.tensor_tensor(t0, mA, mB, OP.subtract)
        v.tensor_tensor(t0, t0, t0, OP.mult)
        v.tensor_tensor(t1, vA, vB, OP.add)
        v.tensor_scalar(t1, t1, 0.5, None, OP.mult)
        v.scalar_tensor_tensor(smalls[:, 9:10], t0, 0.25, t1, OP.mult, OP.add)
        v.tensor_tensor(t0, smalls[:, 8:9], smalls[:, 8:9], OP.mult)
        v.tensor_tensor(smalls[:, 9:10], smalls[:, 9:10], t0, OP.add)
        v.tensor_scalar(smalls[:, 9:10], smalls[:, 9:10], EPS, None, OP.add)
        C.inc(v.tensor_copy(smalls[:, 18:19], smalls[:, 9:10]), "vE")
    sec("vector", v_tail)
    v_t1 = C.cnt["vE"]

    def s_tail(s, v_t1=v_t1):
        C.wait(s, "vE", v_t1)
        if X["vt2"]:
            C.wait(s, "vE", X["vt2"])
        C.inc(_rstd_finish(s, smalls[:, 18:19], smalls[:, 10:11],
                           smalls[:, 11:12]), "sE")
    sec("scalar", s_tail)
    s_t1 = C.cnt["sE"]

    def v_tail2(v, s_t1=s_t1):
        C.wait(v, "sE", s_t1)
        if X["ytr"]:
            v.wait_ge(C.sems["tE"], X["ytr"])
        C.inc(v.scalar_tensor_tensor(sc["yhat_tok"][:], ysb[:], smalls[:, 10:11],
                                     nw_bc[:], OP.mult, OP.mult), "vE")
    sec("vector", v_tail2)
    v_t2 = C.cnt["vE"]
    X["vt2"] = v_t2

    for jb in range(4):
        def tf2(t, jb=jb, v_t2=v_t2):
            if jb == 0:
                C.wait(t, "vE", v_t2)
            key = f"pb{jb % 2}"
            C.bank_guard(t, key)
            for i in range(8):
                c = jb * 8 + i
                mm = t.transpose(pb[jb % 2][:, i * 128:(i + 1) * 128],
                                 sc["yhat_tok"][:, c * 128:(c + 1) * 128],
                                 ident_bf[:])
            C.inc(mm, "tE")
        sec("tensor", tf2)
        t_tr2 = C.cnt["tE"]

        def sg2(v, jb=jb, t_tr2=t_tr2):
            v.wait_ge(C.sems["tE"], t_tr2)
            dstap = yhatF[:, jb * 8 * TT:(jb * 8 + 8) * TT] \
                .rearrange("p (c t) -> p c t", t=TT)[:, :, ch * 128:(ch + 1) * 128]
            C.inc(v.tensor_copy(
                dstap, pb[jb % 2][:, 0:1024].rearrange("p (c t) -> p c t", t=128)),
                "vE")
            C.bank_mark(f"pb{jb % 2}", "vE")
        sec("vector", sg2)
    X["ytr"] = C.cnt["tE"]


def _e_outproj(nc, C, sec, tok0, w_out, yhatF, wbuf, x1stg, x1spill, pa):
    dw_at = {}
    mm_done = {}
    do_at = {}

    def pre(g):
        def f(gp, g=g):
            col0 = g * 256
            if g >= 2:
                gp.wait_ge(C.sems["tE"], mm_done[g - 2])
            dw_at[g] = C.dma(gp.dma_start(
                wbuf[g % 2][:, 0:32 * 256].rearrange("p (c f) -> p c f", f=256),
                w_out.rearrange("(c p) f -> p c f", p=128)[:, :, col0:col0 + 256]),
                "dwa" if g % 2 == 0 else "dwb")
        sec("gpsimd", f)

    pre(0)
    pre(1)
    for g in range(16):
        col0 = g * 256
        for b in range(NB):
            idx = g * NB + b

            def tf(t, g=g, b=b, idx=idx):
                if b == 0:
                    C.wait(t, "dwa" if g % 2 == 0 else "dwb")
                C.bank_guard(t, f"pa{idx % 3}")
                for c in range(32):
                    mm = t.matmul(pa[idx % 3][:, 0:256],
                                  yhatF[:, c * TT + b * 128:c * TT + (b + 1) * 128],
                                  wbuf[g % 2][:, c * 256:(c + 1) * 256],
                                  start=(c == 0), stop=(c == 31))
                C.inc(mm, "tE")
                mm_done[g] = C.cnt["tE"]
            sec("tensor", tf)
            t_mm = C.cnt["tE"]

            def sf(v, idx=idx, t_mm=t_mm):
                v.wait_ge(C.sems["tE"], t_mm)
                if idx >= 2:
                    C.wait(v, "doa" if idx % 2 == 0 else "dob")
                C.inc(v.tensor_copy(x1stg[idx % 2][:, 0:256], pa[idx % 3][:, 0:256]),
                      "vE")
                C.bank_mark(f"pa{idx % 3}", "vE")
            sec("vector", sf)
            s_x1 = C.cnt["vE"]

            def df(sy, b=b, col0=col0, idx=idx, s_x1=s_x1):
                C.wait(sy, "vE", s_x1)
                do_at[idx] = C.dma(sy.dma_start(
                    x1spill[tok0 + b * 128:tok0 + (b + 1) * 128, col0:col0 + 256],
                    x1stg[idx % 2][:, 0:256]), "doa" if idx % 2 == 0 else "dob")
            sec("sync", df)
        if g + 2 < 16:
            pre(g + 2)


def _f_prep(nc, C, sec, ffn_a, ffn_b, ffn_wq, ffn_keys, wbuf, awT, bwT,
            mT, keysT, ident_bf, pa, pb):
    def d0(gp):
        for h in range(2):
            C.dma(gp.dma_start(
                wbuf[h][:, 0:8192].rearrange("p (j d) -> p j d", d=4096),
                ffn_a.rearrange("(j p) d -> p j d", p=128)[:, 2 * h:2 * h + 2, :]),
                "dw")
    sec("gpsimd", d0)
    dw_a = C.cnt["dw"]

    for cb in range(16):
        def tf(t, cb=cb):
            if cb == 0:
                C.wait(t, "dw", dw_a)
            key = f"pb{cb % 2}"
            C.bank_guard(t, key)
            for cc in range(2):
                c = cb * 2 + cc
                for j in range(4):
                    mm = t.transpose(
                        pb[cb % 2][:, cc * 512 + j * 128:cc * 512 + (j + 1) * 128],
                        wbuf[j // 2][:, (j % 2) * 4096 + c * 128:
                                     (j % 2) * 4096 + (c + 1) * 128],
                        ident_bf[:])
            C.inc(mm, "tE")
        sec("tensor", tf)
        t_tr = C.cnt["tE"]

        def sg(s, cb=cb, t_tr=t_tr):
            s.wait_ge(C.sems["tE"], t_tr)
            C.inc(s.activation(awT[:, cb * 1024:(cb + 1) * 1024],
                               pb[cb % 2][:, 0:1024], AF.Copy), "sE")
            C.bank_mark(f"pb{cb % 2}", "sE")
        sec("scalar", sg)
    t_a_done = C.cnt["tE"]

    def d1(gp, t_a_done=t_a_done):
        gp.wait_ge(C.sems["tE"], t_a_done)
        for h in range(2):
            C.dma(gp.dma_start(
                wbuf[h][:, 0:8192].rearrange("p (j n) -> p j n", n=512),
                ffn_b.rearrange("(j p) n -> p j n", p=128)[:, 16 * h:16 * h + 16, :]),
                "dw")
    sec("gpsimd", d1)
    dw_b = C.cnt["dw"]

    for nch in range(4):
        for jb in range(4):
            def tf2(t, nch=nch, jb=jb):
                if nch == 0 and jb == 0:
                    C.wait(t, "dw", dw_b)
                key = f"pb{jb % 2}"
                C.bank_guard(t, key)
                for i in range(8):
                    j = jb * 8 + i
                    mm = t.transpose(pb[jb % 2][:, i * 128:(i + 1) * 128],
                                     wbuf[j // 16][:, (j % 16) * 512 + nch * 128:
                                                   (j % 16) * 512 + (nch + 1) * 128],
                                     ident_bf[:])
                C.inc(mm, "tE")
            sec("tensor", tf2)
            t_tr2 = C.cnt["tE"]

            def sg2(s, nch=nch, jb=jb, t_tr2=t_tr2):
                s.wait_ge(C.sems["tE"], t_tr2)
                C.inc(s.activation(
                    bwT[:, nch * 4096 + jb * 1024:nch * 4096 + (jb + 1) * 1024],
                    pb[jb % 2][:, 0:1024], AF.Copy), "sE")
                C.bank_mark(f"pb{jb % 2}", "sE")
            sec("scalar", sg2)
    t_b_done = C.cnt["tE"]

    def d2(gp, t_b_done=t_b_done):
        gp.wait_ge(C.sems["tE"], t_b_done)
        C.dma(gp.dma_start(wbuf[0][0:16, 0:4096], ffn_wq), "dw")
        C.dma(gp.dma_start(wbuf[0][0:64, 4096:4112], ffn_keys), "dw")
    sec("gpsimd", d2)
    dw_q = C.cnt["dw"]

    def tf4(t):
        C.wait(t, "dw", dw_q)
        C.bank_guard(t, "pb0")
        C.inc(t.transpose(pb[0][0:16, 0:64], wbuf[0][0:64, 4096:4112],
                          ident_bf[0:64, 0:64]), "tE")
    sec("tensor", tf4)
    t_k = C.cnt["tE"]

    def sg4(s, t_k=t_k):
        s.wait_ge(C.sems["tE"], t_k)
        C.inc(s.activation(keysT[:], pb[0][0:16, 0:64], AF.Copy), "sE")
        C.bank_mark("pb0", "sE")
    sec("scalar", sg4)
    s_k = C.cnt["sE"]

    for mb in range(4):
        def tm(t, mb=mb, s_k=s_k):
            if mb == 0:
                C.wait(t, "sE", s_k)
            C.bank_guard(t, f"pa{mb % 2}")
            for i in range(8):
                c = mb * 8 + i
                mm = t.matmul(pa[mb % 2][:, i * 64:(i + 1) * 64],
                              wbuf[0][0:16, c * 128:(c + 1) * 128],
                              keysT[:], start=True, stop=True)
            C.inc(mm, "tE")
        sec("tensor", tm)
        t_m = C.cnt["tE"]

        def vm(v, mb=mb, t_m=t_m):
            v.wait_ge(C.sems["tE"], t_m)
            C.inc(v.tensor_copy(mT[:, mb * 512:(mb + 1) * 512],
                                pa[mb % 2][:, 0:512]), "vE")
            C.bank_mark(f"pa{mb % 2}", "vE")
        sec("vector", vm)


def _f_x(nc, C, sec, tok0, b, x_in, x1spill, xblk2, ysb2, x2F, x2tok,
         awT, mT, bnb, smalls, ident_bf, pa, pb, Xf):
    t0b = tok0 + b * 128
    par = b % 2
    zi = 0 if par == 0 else 3
    si = 1 if par == 0 else 2

    def d0(sy, b=b):
        if Xf["aq"].get(b - 1):
            sy.wait_ge(C.sems["tE"], Xf["aq"][b - 1])
        if Xf["x2Fev"].get(b - 1):
            sy.wait_ge(C.sems["sE"], Xf["x2Fev"][b - 1])
        if Xf["vln"].get(b - 1):
            sy.wait_ge(C.sems["vE"], Xf["vln"][b - 1])
        C.dma(sy.dma_start(xblk2[:], x_in[t0b:t0b + 128, :]), "dx")
        C.dma(sy.dma_start(x2F[:], x1spill[t0b:t0b + 128, :]), "dx")
    sec("sync", d0)

    def v0(v):
        C.wait(v, "dx")
        if Xf["s1"].get(b - 1):
            v.wait_ge(C.sems["sE"], Xf["s1"][b - 1])
        v.tensor_tensor(ysb2[:], x2F[:, 0:H], xblk2[:], OP.add)
        _ln_stats(C, v, bnb, ysb2[:], H, smalls[:, 8:9], smalls[:, 9:10],
                  smalls, 12, want_meansq=False)
        C.inc(v.tensor_copy(smalls[:, 18:19], smalls[:, 9:10]), "vE")
    sec("vector", v0)
    v_ln = C.cnt["vE"]
    Xf["vln"][b] = v_ln

    def s0(s, v_ln=v_ln, par=par):
        C.wait(s, "vE", v_ln)
        _rstd_finish(s, smalls[:, 18:19], smalls[:, 10:11], smalls[:, 11:12])
        s.activation(smalls[:, 11:12], smalls[:, 8:9], AF.Copy,
                     scale=smalls[:, 10:11])
        s.activation(smalls[:, 19:20], smalls[:, 11:12], AF.Copy, scale=-1.0)
        if Xf["tb"].get(b - 2):
            s.wait_ge(C.sems["tE"], Xf["tb"][b - 2])
        C.inc(s.activation(x2tok[par][:], ysb2[:], AF.Identity,
                           scale=smalls[:, 10:11], bias=smalls[:, 19:20]), "sE")
    sec("scalar", s0)
    s_x2 = C.cnt["sE"]
    Xf["s1"][b] = s_x2

    for jb in range(4):
        def tf(t, jb=jb, s_x2=s_x2, par=par):
            if jb == 0:
                C.wait(t, "sE", s_x2)
            C.bank_guard(t, f"pb{jb % 2}")
            for i in range(8):
                c = jb * 8 + i
                mm = t.transpose(pb[jb % 2][:, i * 128:(i + 1) * 128],
                                 x2tok[par][:, c * 128:(c + 1) * 128], ident_bf[:])
            C.inc(mm, "tE")
        sec("tensor", tf)
        t_tr = C.cnt["tE"]

        def sg(s, jb=jb, t_tr=t_tr):
            s.wait_ge(C.sems["tE"], t_tr)
            C.inc(s.activation(x2F[:, jb * 1024:(jb + 1) * 1024],
                               pb[jb % 2][:, 0:1024], AF.Copy), "sE")
            C.bank_mark(f"pb{jb % 2}", "sE")
        sec("scalar", sg)
    Xf["x2tr"][b] = C.cnt["tE"]
    s_x2f = C.cnt["sE"]
    Xf["x2Fev"][b] = s_x2f

    def t_a(t, s_x2f=s_x2f, zi=zi, si=si):
        C.wait(t, "sE", s_x2f)
        C.bank_guard(t, f"pa{zi}")
        C.bank_guard(t, f"pa{si}")
        last = None
        for c in range(32):
            t.matmul(pa[zi][:, 0:512], x2F[:, c * 128:(c + 1) * 128],
                     awT[:, c * 512:(c + 1) * 512], start=(c == 0), stop=(c == 31))
            last = t.matmul(pa[si][:, 0:64], x2F[:, c * 128:(c + 1) * 128],
                            mT[:, c * 64:(c + 1) * 64], start=(c == 0),
                            stop=(c == 31))
        C.inc(last, "tE")
    sec("tensor", t_a)
    Xf["aq"][b] = C.cnt["tE"]


def _f_y(nc, C, sec, tok0, b, out_t, x3f, x2tok, bwT, fscr, fscr2, gq, gwork,
         fcmp, zhat, zhT, bnb2, smalls, rcols, ident_bf, pa, pdm, pb,
         do_at, Xf):
    t0b = tok0 + b * 128
    par = b % 2
    zi = 0 if par == 0 else 3
    si = 1 if par == 0 else 2
    t_aq = Xf["aq"][b]

    def v_sc(v, t_aq=t_aq, si=si):
        C.wait(v, "tE", t_aq)
        v.tensor_copy(gq[:], pa[si][:, 0:64])
        C.bank_mark(f"pa{si}", "vE")
        v.tensor_tensor(
            fcmp[:, 0:4096].rearrange("p (a b) -> p a b", b=64),
            _bc(gq[:], [[64, 128], [0, 64], [1, 64]]),
            _bc(gq[:], [[64, 128], [1, 64], [0, 64]]),
            OP.is_gt)
        v.reduce_sum(fscr2[:], fcmp[:, 0:4096].rearrange("p (a b) -> p a b", b=64),
                     axis=AX)
        v.reduce_max(rcols[0][:], gq[:], axis=AX)
        C.inc(v.tensor_scalar(rcols[1][:], rcols[0][:], -1.0, None, OP.mult),
              "vE")
    sec("vector", v_sc)
    v_rt = C.cnt["vE"]

    def s_r(s, v_rt=v_rt):
        C.wait(s, "vE", v_rt)
        C.inc(s.activation(fscr[:, 0:64], gq[:], AF.Exp, bias=rcols[1][:]), "sE")
    sec("scalar", s_r)
    s_re = C.cnt["sE"]

    def v_g(v, s_re=s_re):
        C.wait(v, "sE", s_re)
        v.tensor_scalar(gwork[:], fscr2[:], 15.0, None, OP.is_le)
        v.tensor_tensor(gwork[:], gwork[:], fscr[:, 0:64], OP.mult)
        v.reduce_sum(rcols[3][:], gwork[:], axis=AX)
        v.reciprocal(rcols[0][:], rcols[3][:])
        C.inc(v.tensor_scalar(gwork[:], gwork[:], rcols[0][:], 2.0,
                              OP.mult, OP.mult), "vE")
    sec("vector", v_g)

    def v_z(v, zi=zi):
        C.inc(v.scalar_tensor_tensor(
            zhat[:].rearrange("p (a b) -> p a b", b=8),
            pa[zi][:, 0:512].rearrange("p (a b) -> p a b", b=8), 1.0,
            _bc(gwork[:], [[64, 128], [1, 64], [0, 8]]),
            OP.mult, OP.mult), "vE")
        C.bank_mark(f"pa{zi}", "vE")
    sec("vector", v_z)
    v_zh = C.cnt["vE"]

    def t_zt(t, v_zh=v_zh):
        C.wait(t, "vE", v_zh)
        C.bank_guard(t, "pb0")
        last = None
        for i in range(4):
            last = t.transpose(pb[0][:, i * 128:(i + 1) * 128],
                               zhat[:, i * 128:(i + 1) * 128], ident_bf[:])
        C.inc(last, "tE")
    sec("tensor", t_zt)
    t_zT = C.cnt["tE"]

    def s_zt(s, t_zT=t_zT):
        s.wait_ge(C.sems["tE"], t_zT)
        C.inc(s.activation(zhT[:], pb[0][:, 0:512], AF.Copy), "sE")
        C.bank_mark("pb0", "sE")
    sec("scalar", s_zt)
    s_zTT = C.cnt["sE"]

    for og in range(8):
        def t_b(t, og=og, s_zTT=s_zTT, par=par):
            if og == 0:
                C.wait(t, "sE", s_zTT)
            key = f"pd{og % 2}"
            C.bank_guard(t, key)
            half = (og % 2) * 512
            for nch in range(4):
                t.matmul(pdm[:, half:half + 512],
                         zhT[:, nch * 128:(nch + 1) * 128],
                         bwT[:, nch * 4096 + og * 512:
                             nch * 4096 + (og + 1) * 512],
                         start=(nch == 0), stop=False)
            C.inc(t.matmul(pdm[:, half:half + 512], ident_bf[:],
                           x2tok[par][:, og * 512:(og + 1) * 512],
                           start=False, stop=True), "tE")
        sec("tensor", t_b)
        t_bo = C.cnt["tE"]

        def s_vo(s, og=og, t_bo=t_bo):
            s.wait_ge(C.sems["tE"], t_bo)
            if og == 0:
                C.wait(s, "do")
            half = (og % 2) * 512
            C.inc(s.activation(x3f[:, og * 512:(og + 1) * 512],
                               pdm[:, half:half + 512], AF.Copy), "sE")
            C.bank_mark(f"pd{og % 2}", "sE")
        sec("scalar", s_vo)
    Xf["tb"][b] = C.cnt["tE"]
    s_voe = C.cnt["sE"]

    def v_l2(v, s_voe=s_voe):
        C.wait(v, "sE", s_voe)
        _ln_stats(C, v, bnb2, x3f[:], H, smalls[:, 20:21], smalls[:, 21:22],
                  smalls, 24, want_meansq=False)
        C.inc(v.tensor_copy(smalls[:, 30:31], smalls[:, 21:22]), "vE")
    sec("vector", v_l2)
    v_l2a = C.cnt["vE"]

    def s_l2(s, v_l2a=v_l2a):
        C.wait(s, "vE", v_l2a)
        _rstd_finish(s, smalls[:, 30:31], smalls[:, 22:23], smalls[:, 23:24])
        s.activation(smalls[:, 23:24], smalls[:, 20:21], AF.Copy,
                     scale=smalls[:, 22:23])
        s.activation(smalls[:, 26:27], smalls[:, 23:24], AF.Copy, scale=-1.0)
        C.inc(s.activation(x3f[:], x3f[:], AF.Identity,
                           scale=smalls[:, 22:23], bias=smalls[:, 26:27]), "sE")
    sec("scalar", s_l2)
    s_ot = C.cnt["sE"]

    def d_out(sy, b=b, s_ot=s_ot):
        sy.wait_ge(C.sems["sE"], s_ot)
        do_at[b] = C.dma(sy.dma_start(out_t[t0b:t0b + 128, :], x3f[:]), "do")
    sec("sync", d_out)


def kernel(**inputs):
    _ = inputs.get("L0_attn_residual")
    x = np.ascontiguousarray(inputs["L0_attn_hidden_states"], dtype=np.float32)
    nc = build()
    base = {
        "w_in": np.ascontiguousarray(inputs["L0_attn_in_proj_w"], np.float32),
        "conv_w": np.ascontiguousarray(inputs["L0_attn_conv_w"], np.float32),
        "a_log": np.ascontiguousarray(inputs["L0_attn_A_log"], np.float32).reshape(1, NH),
        "dt_bias": np.ascontiguousarray(inputs["L0_attn_dt_bias"], np.float32).reshape(1, NH),
        "d_vec": np.ascontiguousarray(inputs["L0_attn_D"], np.float32).reshape(1, NH),
        "ln_w": np.ascontiguousarray(inputs["L0_attn_ln_w"], np.float32).reshape(1, H),
        "norm_w": np.ascontiguousarray(inputs["L0_attn_norm_w"], np.float32).reshape(1, DS),
        "mup": np.ascontiguousarray(inputs["L0_attn_mup_vector"], np.float32).reshape(1, PROJ),
        "w_out": np.ascontiguousarray(inputs["L0_attn_out_proj_w"], np.float32),
        "ffn_a": np.ascontiguousarray(inputs["L0_ffn_A_w"], np.float32),
        "ffn_b": np.ascontiguousarray(inputs["L0_ffn_B_w"], np.float32),
        "ffn_wq": np.ascontiguousarray(inputs["L0_ffn_Wq_w"], np.float32),
        "ffn_keys": np.ascontiguousarray(inputs["L0_ffn_keys"], np.float32),
    }
    in_maps = [dict(base, x=np.ascontiguousarray(x[c])) for c in range(NCORES)]
    res = run_bass_kernel_spmd(nc, in_maps, core_ids=list(range(NCORES)))
    return np.stack([res.results[c]["out"] for c in range(NCORES)], axis=0)


# version 17
# speedup vs baseline: 1.0038x; 1.0038x over previous
import contextlib
import numpy as np
import concourse.bass as bass
import concourse.mybir as mybir
from concourse.bass_utils import run_bass_kernel_spmd
from concourse.alu_op_type import AluOpType as OP

F32 = mybir.dt.float32
F16 = mybir.dt.float16
BF16 = mybir.dt.bfloat16
AF = mybir.ActivationFunctionType
AX = mybir.AxisListType.X

L = 2048
H = 4096
DS = 4096
PROJ = 8512
CONVD = 4352
NH = 64
TT = 1024
NB = 8
NT = L // TT
EPS = 1e-5
NCORES = 8
EXT = TT + 3

ALL_ENG = ["sync", "gpsimd", "tensor", "vector", "scalar"]
_MERGE_UNTIL = 175

_DRAIN_OPS = frozenset({
    "tensor_scalar", "tensor_tensor", "scalar_tensor_tensor", "tensor_copy",
    "tensor_tensor_scan", "reduce_max", "reduce_sum", "tensor_reduce",
    "bn_stats", "bn_aggr", "reciprocal", "memset", "activation", "select",
    "copy",
})


class _RecCall:
    __slots__ = ("name", "a", "k", "chain")

    def __init__(self, name, a, k):
        self.name, self.a, self.k = name, a, k
        self.chain = []

    def then_inc(self, sem, n):
        self.chain.append((sem, n))
        return self


class _RecEngine:

    def __init__(self, buf):
        self._buf = buf

    def __getattr__(self, name):
        def call(*a, **k):
            rc = _RecCall(name, a, k)
            self._buf.append(rc)
            return rc
        return call


def _replay(eng, calls):
    for rc in calls:
        inst = getattr(eng, rc.name)(*rc.a, **rc.k)
        for sem, n in rc.chain:
            inst = inst.then_inc(sem, n)


class _AutoDrain:

    def __init__(self, eng, state=None):
        self._e = eng
        self._w, self._r = state if state is not None else (set(), set())

    def __getattr__(self, name):
        attr = getattr(self._e, name)
        if name in _DRAIN_OPS:
            def w(*a, **k):
                aps = [x for x in list(a) + list(k.values())
                       if isinstance(x, bass.AP)]
                names = [ap.tensor.name for ap in aps]
                out, ins = set(names[:1]), set(names[1:])
                if (ins & self._w) or (out & self._w) or (out & self._r):
                    self._e.drain()
                    self._w.clear()
                    self._r.clear()
                self._w |= out
                self._r |= ins
                return attr(*a, **k)
            return w
        return attr


def bcast_ap(dram_ap, npart, cols):
    return bass.AP(dram_ap.tensor, 0, [[0, npart], [1, cols]])


class Ctx:

    def __init__(self, nc, es):
        self.nc = nc
        self.sems = {}
        self.cnt = {}
        for name in ["dw", "dwa", "dwb", "dx", "dxa", "dxb", "do", "do2",
                     "doa", "dob", "tE", "vE", "sE", "gE"]:
            self.sems[name] = es.enter_context(nc.semaphore(name))
            self.cnt[name] = 0
        self.bank_rd = {}
        self.merge_scalar = True

    def inc(self, instr, name, n=1):
        instr.then_inc(self.sems[name], n)
        self.cnt[name] += n
        return self.cnt[name]

    def dma(self, instr, name):
        return self.inc(instr, name, 16)

    def wait(self, eng, name, val=None):
        v = self.cnt[name] if val is None else val
        if v:
            eng.wait_ge(self.sems[name], v)

    def bank_guard(self, eng, key):
        if key in self.bank_rd:
            sem, val = self.bank_rd[key]
            if val:
                eng.wait_ge(self.sems[sem], val)

    def bank_mark(self, key, sem):
        self.bank_rd[key] = (sem, self.cnt[sem])


def build(debug=()):
    nc = bass.Bass("TRN2", target_bir_lowering=False)
    debug = set(debug)

    x_in = nc.dram_tensor("x", [L, H], F32, kind="ExternalInput").ap()
    w_in = nc.dram_tensor("w_in", [H, PROJ], F32, kind="ExternalInput").ap()
    conv_w = nc.dram_tensor("conv_w", [CONVD, 4], F32, kind="ExternalInput").ap()
    a_log = nc.dram_tensor("a_log", [1, NH], F32, kind="ExternalInput").ap()
    dt_bias = nc.dram_tensor("dt_bias", [1, NH], F32, kind="ExternalInput").ap()
    d_vec = nc.dram_tensor("d_vec", [1, NH], F32, kind="ExternalInput").ap()
    ln_w = nc.dram_tensor("ln_w", [1, H], F32, kind="ExternalInput").ap()
    norm_w = nc.dram_tensor("norm_w", [1, DS], F32, kind="ExternalInput").ap()
    mup = nc.dram_tensor("mup", [1, PROJ], F32, kind="ExternalInput").ap()
    w_out = nc.dram_tensor("w_out", [DS, H], F32, kind="ExternalInput").ap()
    ffn_a = nc.dram_tensor("ffn_a", [512, H], F32, kind="ExternalInput").ap()
    ffn_b = nc.dram_tensor("ffn_b", [H, 512], F32, kind="ExternalInput").ap()
    ffn_wq = nc.dram_tensor("ffn_wq", [16, H], F32, kind="ExternalInput").ap()
    ffn_keys = nc.dram_tensor("ffn_keys", [NH, 16], F32, kind="ExternalInput").ap()

    out_t = nc.dram_tensor("out", [L, H], F32, kind="ExternalOutput").ap()

    def scratch(name, shape, dtype):
        if name in debug:
            return nc.dram_tensor(name, shape, dtype, kind="ExternalOutput").ap()
        return nc.dram_tensor(name, shape, dtype).ap()

    zspill = scratch("zspill", [L, DS], BF16)
    xsspill = scratch("xsspill", [DS, L], BF16)
    x1spill = scratch("x1spill", [L, H], BF16)

    es = contextlib.ExitStack()
    with es:
        C = Ctx(nc, es)
        block = es.enter_context(nc.Block())

        rbuf = {"scalar": []}
        rec = {e: _RecEngine(rbuf[e]) for e in rbuf}
        wrapped = {"scalar": _AutoDrain(rec["scalar"])}
        _vstate = (set(), set())

        def flush_one(e):
            calls = rbuf[e]
            if calls:
                rbuf[e] = []
                rec[e]._buf = rbuf[e]
                getattr(block, e)(lambda eng, calls=calls: _replay(eng, calls))

        def sec(eng_name, fn):
            if eng_name in rbuf and C.merge_scalar:
                fn(wrapped[eng_name])
                if len(rbuf[eng_name]) > 48:
                    flush_one(eng_name)
            elif eng_name == "scalar":
                flush_one("scalar")
                getattr(block, eng_name)(lambda e: fn(_AutoDrain(e)))
            elif eng_name == "vector":
                getattr(block, eng_name)(
                    lambda e: fn(_AutoDrain(e, _vstate)))
            else:
                getattr(block, eng_name)(fn)

        def flush():
            for e in rbuf:
                flush_one(e)

        def barrier():
            flush()
            snap = dict(C.cnt)
            for e in ALL_ENG:
                def f(eng, snap=snap):
                    for name, v in snap.items():
                        if v:
                            eng.wait_ge(C.sems[name], v)
                sec(e, f)

        A = lambda n, sh, dt: es.enter_context(nc.sbuf_tensor(n, sh, dt))

        ones_f32 = A("ones_f32", [128, 128], F32)
        ones_f16 = A("ones_f16", [128, 128], F16)
        ident_bf = A("ident_bf", [128, 128], BF16)
        ident_f32 = A("ident_f32", [128, 128], F32)
        tri_f16 = A("tri_f16", [128, 128], F16)
        su_f16 = A("su_f16", [128, 128], F16)
        tri_f32 = A("tri_f32", [128, 128], F32)
        e2_f16 = A("e2_f16", [64, DS], F16)
        e512_bf = A("e512_bf", [64, 512], BF16)
        dbias_bc = A("dbias_bc", [128, NH], F32)
        nega_bc = A("nega_bc", [128, NH], F32)
        dcol_bc = A("dcol_bc", [128, NH], F32)
        convwF = A("convwF", [128, 34 * 4], F32)
        iota_f = A("iota_f", [128, 128], F32)
        iota_c = A("iota_c", [128, 1], F32)
        smalls = A("smalls", [128, 32], F32)
        sm64 = A("sm64", [64, 16], F32)
        dtT = A("dtT", [128, NB * NH], F32)
        h_bf = A("h_bf", [128, DS], BF16)
        tailsave = A("tailsave", [128, 34 * 3], BF16)
        bmcf = A("bmcf", [128, 2 * TT], BF16)
        wbuf = [A(f"wbuf{i}", [128, 8192], BF16) for i in range(2)]
        bnb = [A(f"bnb{i}", [128, 24], F32) for i in range(2)]

        pa = [es.enter_context(nc.psum_tensor(f"pa{i}", [128, 512], F32)) for i in range(4)]
        pdm = es.enter_context(nc.psum_tensor("pdm", [128, 1024], F32))
        pb = [es.enter_context(nc.psum_tensor(f"pb{i}", [128, 1024], BF16)) for i in range(2)]

        with contextlib.ExitStack() as sp:
            e2scr = sp.enter_context(nc.sbuf_tensor("e2scr", [64, DS], F32))

            def setup_sync(s):
                C.dma(s.dma_start(convwF[:].rearrange("p (c k) -> p c k", k=4),
                                  conv_w.rearrange("(c p) k -> p c k", p=128)), "dx")
                C.dma(s.dma_start(dbias_bc[:], bcast_ap(dt_bias, 128, NH)), "dx")
                C.dma(s.dma_start(nega_bc[:], bcast_ap(a_log, 128, NH)), "dx")
                C.dma(s.dma_start(dcol_bc[:], bcast_ap(d_vec, 128, NH)), "dx")
            sec("sync", setup_sync)

            def setup_gp(g):
                g.iota(iota_f[:], [[1, 128]], base=0, channel_multiplier=0,
                       allow_small_or_imprecise_dtypes=True)
                g.iota(iota_c[:], [[0, 1]], base=0, channel_multiplier=1,
                       allow_small_or_imprecise_dtypes=True)
                C.inc(g.iota(e2scr[:, :].rearrange("p (a b) -> p a b", b=64),
                             [[1, 64], [0, 64]], base=0, channel_multiplier=0,
                             allow_small_or_imprecise_dtypes=True), "gE")
            sec("gpsimd", setup_gp)

            def setup_v(v):
                C.wait(v, "gE")
                C.wait(v, "dx")
                v.memset(ones_f32[:], 1.0)
                v.memset(ones_f16[:], 1.0)
                v.memset(h_bf[:], 0.0)
                v.memset(tailsave[:], 0.0)
                v.tensor_scalar(ident_f32[:], iota_f[:], iota_c[:], None, OP.is_equal)
                v.tensor_copy(ident_bf[:], ident_f32[:])
                v.tensor_scalar(tri_f32[:], iota_f[:], iota_c[:], None, OP.is_ge)
                v.tensor_copy(tri_f16[:], tri_f32[:])
                v.tensor_scalar(su_f16[:], iota_f[:], iota_c[:], None, OP.is_lt)
                v.tensor_scalar(e2scr[:, :], e2scr[:, :], iota_c[0:64, :],
                                None, OP.is_equal)
                C.inc(v.tensor_copy(e2_f16[:], e2scr[:, :]), "vE")
            sec("vector", setup_v)
            v_e2 = C.cnt["vE"]

            def setup_gp2(g):
                C.wait(g, "vE", v_e2)
                C.inc(g.iota(e2scr[:, 0:512].rearrange("p (a b) -> p a b", b=8),
                             [[1, 64], [0, 8]], base=0, channel_multiplier=0,
                             allow_small_or_imprecise_dtypes=True), "gE")
            sec("gpsimd", setup_gp2)

            def setup_v2(v):
                C.wait(v, "gE")
                v.tensor_scalar(e2scr[:, 0:512], e2scr[:, 0:512], iota_c[0:64, :],
                                None, OP.is_equal)
                C.inc(v.tensor_copy(e512_bf[:], e2scr[:, 0:512]), "vE")
            sec("vector", setup_v2)
            v_e5 = C.cnt["vE"]

            def setup_s(s):
                C.wait(s, "vE", v_e5)
                C.inc(s.activation(nega_bc[:], nega_bc[:], AF.Exp), "sE")
            sec("scalar", setup_s)

            def setup_v3(v):
                C.wait(v, "sE")
                C.inc(v.tensor_scalar(nega_bc[:], nega_bc[:], -1.0, None, OP.mult),
                      "vE")
            sec("vector", setup_v3)

            barrier()

        for tile in range(NT):
            tok0 = tile * TT

            with contextlib.ExitStack() as ph:
                Aph = lambda n, sh, dt: ph.enter_context(
                    nc.sbuf_tensor(f"{n}_t{tile}", sh, dt))
                xhatF = Aph("xhatF", [128, 32 * TT], BF16)

                with contextlib.ExitStack() as p0s:
                    A0 = lambda n, sh, dt: p0s.enter_context(
                        nc.sbuf_tensor(f"{n}_t{tile}", sh, dt))
                    lnw_bc = A0("lnw_bc", [128, H], F16)
                    xblk = [A0(f"xblk{i}", [128, H], F32) for i in range(2)]
                    xhat_tok = A0("xhat_tok", [128, H], BF16)
                    with nc.named_scope(f"P0_t{tile}"):
                        _p0(nc, C, sec, tok0, x_in, ln_w, lnw_bc, xblk, xhat_tok,
                            xhatF, bnb, smalls, ident_bf, pa, pb)
                        barrier()

                with contextlib.ExitStack() as a1s:
                    A1 = lambda n, sh, dt: a1s.enter_context(
                        nc.sbuf_tensor(f"{n}_t{tile}", sh, dt))
                    mup_bc = A1("mup_bc", [128, PROJ], F16)
                    xbct = [A1(f"xbct{i}", [128, EXT], BF16) for i in range(8)]
                    ztmp = A1("ztmp", [128, 512], F32)
                    bst = A1("bst", [128, 512], BF16)
                    accb = A1("accb", [128, TT], F32)
                    zstg = [A1(f"zstg{i}", [128, 512], BF16) for i in range(2)]
                    xstg = [A1(f"xstg{i}", [128, TT], BF16) for i in range(2)]

                    def mup_load(g):
                        C.dma(g.dma_start(mup_bc[:], bcast_ap(mup, 128, PROJ)), "dw")
                    sec("gpsimd", mup_load)

                    with nc.named_scope(f"A1_t{tile}"):
                        _a1(nc, C, sec, tok0, w_in, xhatF, wbuf, mup_bc, ztmp, bst,
                            xbct, accb, convwF, tailsave, bmcf, dtT, xstg, xsspill,
                            ident_bf, pa, pb)
                        barrier()
                    with nc.named_scope(f"A2_t{tile}"):
                        _a2(nc, C, sec, tok0, w_in, xhatF, wbuf, mup_bc, ztmp,
                            zstg, zspill, pa)
                        barrier()

            with contextlib.ExitStack() as ph:
                Aph = lambda n, sh, dt: ph.enter_context(
                    nc.sbuf_tensor(f"{n}_t{tile}", sh, dt))
                yhatF = Aph("yhatF", [128, 32 * TT], BF16)
                nw_bc = Aph("nw_bc", [128, DS], F16)
                ysb = Aph("ysb", [128, DS], BF16)
                sc = {n: Aph(n, sh, dt) for n, sh, dt in [
                    ("xsT", [128, DS], BF16),
                    ("yhat_tok", [128, DS], BF16),
                    ("vexp", [128, 512], F32),
                    ("ytmp", [128, 512], BF16),
                    ("dxst", [128, 512], BF16),
                    ("yA", [128, 512], BF16),
                    ("xtld", [128, 512], BF16),
                    ("hdec", [128, 512], BF16),
                    ("dtc", [128, NH], F32),
                    ("dta32", [128, NH], F32),
                    ("dta16", [128, NH], F16),
                    ("srow", [1, 64], F32),
                    ("sbc", [128, 64], F32),
                    ("t64a", [128, NH], F32),
                    ("t64b", [128, NH], F32),
                    ("t64c", [128, NH], F32),
                ]}
                sc["cbtm"] = [Aph(f"cbtm{i}", [128, 128], F32) for i in range(2)]
                sc["bmt"] = [Aph(f"bmt{i}", [128, 128], BF16) for i in range(2)]
                sc["vf16"] = [Aph(f"vf16{i}", [64, 128], F16) for i in range(2)]
                sc["expd"] = [Aph(f"expd{i}", [128, 1024], F16) for i in range(2)]
                sc["rhsd"] = [Aph(f"rhsd{i}", [128, 512], F16) for i in range(2)]
                sc["mtil"] = [Aph(f"mtil{i}", [128, 1024], BF16) for i in range(2)]
                sc["xsTd"] = [Aph(f"xsTd{i}", [128, 512], BF16) for i in range(2)]
                x1stg = [Aph(f"x1stg{i}", [128, 512], BF16) for i in range(2)]

                def nw_load(g):
                    C.dma(g.dma_start(nw_bc[:], bcast_ap(norm_w, 128, DS)), "dw")
                sec("gpsimd", nw_load)

                with nc.named_scope(f"SCAN_t{tile}"):
                    C.merge_scalar = False
                    X = {"xrd_free": {0: 0, 1: 0}, "zrd_free": {0: 0, 1: 0},
                         "st_last": 0, "scum": 0, "tcu_prev": 0, "vyx_last": 0,
                         "front_v": 0, "vt2": 0, "ytr": 0, "base_g": 0,
                         "dmm": {}, "expfree": {}, "mfree": {}, "dxs": {},
                         "st": {}, "hev": {}}
                    fr = lambda ch: _scan_front(
                        nc, C, sec, tok0, ch, sc, wbuf, xsspill, zspill, dtT,
                        dbias_bc, nega_bc, bmcf, ones_f16, ones_f32, tri_f16,
                        su_f16, tri_f32, ident_bf, ident_f32, sm64, pa, pb, X)
                    fr(0)
                    for ch in range(NB):
                        _scan_groups(nc, C, sec, tok0, ch, sc, wbuf, dcol_bc,
                                     h_bf, bmcf, ysb, e2_f16, bnb, tri_f16,
                                     su_f16, ident_bf, pa, pdm, pb, X)
                        if ch + 1 < NB:
                            fr(ch + 1)
                        _scan_tail(nc, C, sec, tok0, ch, sc, nw_bc, ysb, yhatF,
                                   ident_bf, smalls, sm64, bnb, pa, pb, X)
                    barrier()
                    C.merge_scalar = True
                with nc.named_scope(f"E_t{tile}"):
                    _e_outproj(nc, C, sec, tok0, w_out, yhatF, wbuf, x1stg,
                               x1spill, pa)
                    barrier()

            with contextlib.ExitStack() as ph:
                Aph = lambda n, sh, dt: ph.enter_context(
                    nc.sbuf_tensor(f"{n}_t{tile}", sh, dt))
                awT = Aph("awT", [128, 32 * 512], BF16)
                bwT = Aph("bwT", [128, 4 * H], BF16)
                mT = Aph("mT", [128, 32 * 64], BF16)
                keysT = Aph("keysT", [16, 64], BF16)
                x2F = Aph("x2F", [128, 32 * 128], BF16)
                x2tok = [Aph(f"x2tok{i}", [128, H], BF16) for i in range(2)]
                bnb2 = [Aph(f"bnb2_{i}", [128, 24], F32) for i in range(2)]
                x3f = Aph("x3f", [128, H], F32)
                xblk2 = Aph("xblk2", [128, H], F32)
                ysb2 = Aph("ysb2", [128, H], F32)
                fscr = Aph("fscr", [128, 512], F32)
                fscr2 = Aph("fscr2", [128, 64], F32)
                gq = Aph("gq", [128, 64], F32)
                gwork = Aph("gwork", [128, 64], F32)
                zhat = Aph("zhat", [128, 512], BF16)
                zhT = Aph("zhT", [128, 512], BF16)
                rcols = [Aph(f"rcol{i}", [128, 1], F32) for i in range(4)]

                with nc.named_scope(f"FP_t{tile}"):
                    _f_prep(nc, C, sec, ffn_a, ffn_b, ffn_wq, ffn_keys, wbuf,
                            awT, bwT, mT, keysT, ident_bf, pa, pb)
                    barrier()
                do_at = {}
                Xf = {"aq": {}, "x2Fev": {}, "x2tr": {}, "s1": {}, "vln": {},
                      "tb": {}}
                with nc.named_scope(f"F_t{tile}"):
                    fx = lambda b: _f_x(nc, C, sec, tok0, b, x_in, x1spill,
                                        xblk2, ysb2, x2F, x2tok, awT, mT, bnb,
                                        smalls, ident_bf, pa, pb, Xf)
                    fy = lambda b: _f_y(nc, C, sec, tok0, b, out_t, x3f, x2tok,
                                        bwT, fscr, fscr2, gq, gwork, wbuf[1], zhat,
                                        zhT, bnb2, smalls, rcols, ident_bf, pa,
                                        pdm, pb, do_at, Xf)
                    fx(0)
                    for b in range(NB):
                        if b + 1 < NB:
                            fx(b + 1)
                        fy(b)
                    barrier()

        def fin(s):
            C.wait(s, "do")
            C.wait(s, "do2")
            C.wait(s, "doa")
            C.wait(s, "dob")
        sec("sync", fin)
        flush()

    return nc



def _ln_stats(C, v, bnb, src_ap, ncols, mu_col, m2_col, sm, c0, want_meansq):
    nst = ncols // 512
    for i in range(nst):
        v.bn_stats(bnb[i % 2][:, (i // 2) * 6:(i // 2) * 6 + 6],
                   src_ap[:, i * 512:(i + 1) * 512])
    v.bn_aggr(sm[:, c0:c0 + 2], bnb[0][:, 0:(nst // 2) * 6])
    v.bn_aggr(sm[:, c0 + 2:c0 + 4], bnb[1][:, 0:(nst // 2) * 6])
    mA, vA = sm[:, c0:c0 + 1], sm[:, c0 + 1:c0 + 2]
    mB, vB = sm[:, c0 + 2:c0 + 3], sm[:, c0 + 3:c0 + 4]
    t0, t1 = sm[:, c0 + 4:c0 + 5], sm[:, c0 + 5:c0 + 6]
    v.tensor_tensor(mu_col, mA, mB, OP.add)
    v.tensor_scalar(mu_col, mu_col, 0.5, None, OP.mult)
    v.tensor_tensor(t0, mA, mB, OP.subtract)
    v.tensor_tensor(t0, t0, t0, OP.mult)
    v.tensor_tensor(t1, vA, vB, OP.add)
    v.tensor_scalar(t1, t1, 0.5, None, OP.mult)
    v.scalar_tensor_tensor(m2_col, t0, 0.25, t1, OP.mult, OP.add)
    if want_meansq:
        v.tensor_tensor(t0, mu_col, mu_col, OP.mult)
        v.tensor_tensor(m2_col, m2_col, t0, OP.add)
    v.tensor_scalar(m2_col, m2_col, EPS, None, OP.add)


def _rstd_finish(s, src_col, dst_col, tmp_col):
    s.activation(tmp_col, src_col, AF.Ln)
    return s.activation(dst_col, tmp_col, AF.Exp, scale=-0.5)


def _p0(nc, C, sec, tok0, x_in, ln_w, lnw_bc, xblk, xhat_tok, xhatF, bnb,
        smalls, ident_bf, pa, pb):
    dx_at = {}
    v_stt = {}
    t_done = {}

    def g_lnw(g):
        C.dma(g.dma_start(lnw_bc[:], bcast_ap(ln_w, 128, H)), "dw")
    sec("gpsimd", g_lnw)
    dw_lnw = C.cnt["dw"]

    def d_pre(s):
        for b in range(2):
            dx_at[b] = C.dma(s.dma_start(
                xblk[b % 2][:], x_in[tok0 + b * 128:tok0 + (b + 1) * 128, :]),
                "dxa" if b % 2 == 0 else "dxb")
    sec("sync", d_pre)

    for b in range(NB):
        def vf(v, b=b, dxf=None):
            C.wait(v, "dxa" if b % 2 == 0 else "dxb")
            if b == 0:
                C.wait(v, "dw", dw_lnw)
            _ln_stats(C, v, bnb, xblk[b % 2][:], H, smalls[:, 8:9], smalls[:, 9:10],
                      smalls, 12, want_meansq=True)
            C.inc(v.tensor_copy(smalls[:, 18:19], smalls[:, 9:10]), "vE")
        sec("vector", vf)
        v_bn = C.cnt["vE"]

        def sf(s, b=b):
            C.wait(s, "vE", v_bn)
            C.inc(_rstd_finish(s, smalls[:, 18:19], smalls[:, 10:11],
                               smalls[:, 11:12]), "sE")
        sec("scalar", sf)
        s_rs = C.cnt["sE"]

        def vg(v, b=b):
            C.wait(v, "sE", s_rs)
            if b >= 1:
                C.wait(v, "tE", t_done[b - 1])
            C.inc(v.scalar_tensor_tensor(xhat_tok[:], xblk[b % 2][:],
                                         smalls[:, 10:11], lnw_bc[:],
                                         OP.mult, OP.mult), "vE")
        sec("vector", vg)
        v_stt[b] = C.cnt["vE"]

        if b + 2 < NB:
            def df(s, b=b):
                C.wait(s, "vE", v_stt[b])
                dx_at[b + 2] = C.dma(s.dma_start(
                    xblk[b % 2][:],
                    x_in[tok0 + (b + 2) * 128:tok0 + (b + 3) * 128, :]),
                    "dxa" if b % 2 == 0 else "dxb")
            sec("sync", df)

        for jb in range(4):
            def tf(t, b=b, jb=jb):
                if jb == 0:
                    C.wait(t, "vE", v_stt[b])
                key = f"pb{jb % 2}"
                C.bank_guard(t, key)
                for i in range(8):
                    c = jb * 8 + i
                    mm = t.transpose(pb[jb % 2][:, i * 128:(i + 1) * 128],
                                     xhat_tok[:, c * 128:(c + 1) * 128], ident_bf[:])
                C.inc(mm, "tE")
            sec("tensor", tf)
            t_tr = C.cnt["tE"]

            def sg(v, b=b, jb=jb, t_tr=t_tr):
                v.wait_ge(C.sems["tE"], t_tr)
                dstap = xhatF[:, jb * 8 * TT:(jb * 8 + 8) * TT] \
                    .rearrange("p (c t) -> p c t", t=TT)[:, :, b * 128:(b + 1) * 128]
                C.inc(v.tensor_copy(
                    dstap, pb[jb % 2][:, 0:1024].rearrange("p (c t) -> p c t", t=128)),
                    "vE")
                C.bank_mark(f"pb{jb % 2}", "vE")
            sec("vector", sg)
        t_done[b] = C.cnt["tE"]


def _a1(nc, C, sec, tok0, w_in, xhatF, wbuf, mup_bc, ztmp, bst, xbct, accb,
        convwF, tailsave, bmcf, dtT, xstg, xsspill, ident_bf, pa, pb):
    n_slab = 18
    dw_at = {}
    mm_done = {}
    ev_done = {}
    tg_at = {}
    conv_tap = {}
    spill_at = {}
    s_silu_at = {}

    def pre(g):
        def f(gp, g=g):
            col0 = 4096 + g * 256
            w = 64 if g == 17 else 256
            if g >= 2:
                gp.wait_ge(C.sems["tE"], mm_done[g - 2])
            dw_at[g] = C.dma(gp.dma_start(
                wbuf[g % 2][:, 0:32 * w].rearrange("p (c f) -> p c f", f=w),
                w_in.rearrange("(c p) f -> p c f", p=128)[:, :, col0:col0 + w]),
                "dwa" if g % 2 == 0 else "dwb")
        sec("gpsimd", f)

    pre(0)
    pre(1)

    for g in range(n_slab):
        col0 = 4096 + g * 256
        w = 64 if g == 17 else 256

        if g < 16:
            def gtail(gp, g=g):
                if 2 * g - 8 >= 0:
                    gp.wait_ge(C.sems["vE"], conv_tap[2 * g + 1 - 8])
                cc0 = 2 * g
                gp.tensor_copy(xbct[cc0 % 8][:, 0:3],
                               tailsave[:, cc0 * 3:cc0 * 3 + 3])
                C.inc(gp.tensor_copy(xbct[(cc0 + 1) % 8][:, 0:3],
                                     tailsave[:, (cc0 + 1) * 3:(cc0 + 1) * 3 + 3]),
                      "gE")
            sec("gpsimd", gtail)
        g_tail_in = C.cnt["gE"]

        for b in range(NB):
            idx = g * NB + b

            def tf(t, g=g, b=b, w=w, idx=idx):
                if b == 0:
                    C.wait(t, "dwa" if g % 2 == 0 else "dwb")
                C.bank_guard(t, f"pa{idx % 3}")
                for c in range(32):
                    mm = t.matmul(pa[idx % 3][:, 0:w],
                                  xhatF[:, c * TT + b * 128:c * TT + (b + 1) * 128],
                                  wbuf[g % 2][:, c * w:(c + 1) * w],
                                  start=(c == 0), stop=(c == 31))
                C.inc(mm, "tE")
                mm_done[g] = C.cnt["tE"]
            sec("tensor", tf)
            t_mm = C.cnt["tE"]

            def vf(v, g=g, b=b, w=w, col0=col0, idx=idx, t_mm=t_mm):
                C.wait(v, "tE", t_mm)
                if idx == 0:
                    C.wait(v, "dw")
                if g < 17 and idx - 1 in tg_at:
                    C.wait(v, "tE", tg_at[idx - 1])
                if g == 17:
                    C.inc(v.tensor_tensor(dtT[:, b * 64:(b + 1) * 64],
                                          pa[idx % 3][:, 0:64],
                                          mup_bc[:, col0:col0 + 64], OP.mult), "vE")
                else:
                    C.inc(v.tensor_tensor(bst[:, 0:w], pa[idx % 3][:, 0:w],
                                          mup_bc[:, col0:col0 + w], OP.mult), "vE")
                C.bank_mark(f"pa{idx % 3}", "vE")
                ev_done[idx] = C.cnt["vE"]
            sec("vector", vf)
            v_ev = C.cnt["vE"]

            if g < 17:
                def tg(t, g=g, b=b, idx=idx, v_ev=v_ev):
                    C.wait(t, "vE", v_ev)
                    key = f"pb{idx % 2}"
                    C.bank_guard(t, key)
                    for i in range(2):
                        mm = t.transpose(pb[idx % 2][:, i * 128:(i + 1) * 128],
                                         bst[:, i * 128:(i + 1) * 128], ident_bf[:])
                    C.inc(mm, "tE")
                    tg_at[idx] = C.cnt["tE"]
                sec("tensor", tg)
                t_tr = C.cnt["tE"]

                def sg(v, g=g, b=b, idx=idx, t_tr=t_tr):
                    v.wait_ge(C.sems["tE"], t_tr)
                    last = None
                    for i in range(2):
                        cc = 2 * g + i
                        if cc < 32:
                            dst = xbct[cc % 8][:, 3 + b * 128:3 + (b + 1) * 128]
                        else:
                            dst = bmcf[:, (cc - 32) * TT + b * 128:
                                       (cc - 32) * TT + (b + 1) * 128]
                        last = v.tensor_copy(dst, pb[idx % 2][:, i * 128:(i + 1) * 128])
                    C.inc(last, "vE")
                    C.bank_mark(f"pb{idx % 2}", "vE")
                sec("vector", sg)

        if g + 2 < n_slab:
            pre(g + 2)

        if g < 16:
            for i in range(2):
                cc = 2 * g + i
                _conv_chunk(nc, C, sec, tok0, cc, xbct[cc % 8], accb, convwF,
                            tailsave, xstg, xsspill, bmcf, conv_tap, spill_at)
        elif g == 16:
            for i in range(2):
                cc = 32 + i
                _conv_chunk(nc, C, sec, tok0, cc, None, accb, convwF,
                            tailsave, xstg, xsspill, bmcf, conv_tap, spill_at)


def _conv_chunk(nc, C, sec, tok0, cc, buf, accb, convwF, tailsave, xstg,
                xsspill, bmcf, conv_tap, spill_at):
    if cc >= 32:
        data = bmcf[:, (cc - 32) * TT:(cc - 32 + 1) * TT]
    else:
        data = buf[:, 3:3 + TT]
    sE_now = C.cnt["sE"]
    vE_now = C.cnt["vE"]
    gE_now = C.cnt["gE"]

    def vf(v, cc=cc, data=data, sE_now=sE_now, vE_now=vE_now, gE_now=gE_now):
        C.wait(v, "sE", sE_now)
        C.wait(v, "vE", vE_now)
        C.wait(v, "gE", gE_now)
        w = lambda k: convwF[:, cc * 4 + k:cc * 4 + k + 1]
        if cc < 32:
            ext = buf
            v.tensor_scalar(accb[:], ext[:, 0:TT], w(0), None, OP.mult)
            for k in range(1, 3):
                v.scalar_tensor_tensor(accb[:], ext[:, k:k + TT], w(k), accb[:],
                                       OP.mult, OP.add)
            last = v.scalar_tensor_tensor(accb[:], ext[:, 3:3 + TT], w(3), accb[:],
                                          OP.mult, OP.add)
        else:
            ts3 = tailsave[:, cc * 3:cc * 3 + 3]
            v.tensor_scalar(accb[:], data[:], w(3), None, OP.mult)
            for k in range(3):
                sh = 3 - k
                v.scalar_tensor_tensor(accb[:, sh:TT], data[:, 0:TT - sh], w(k),
                                       accb[:, sh:TT], OP.mult, OP.add)
                last = v.scalar_tensor_tensor(accb[:, 0:sh], ts3[:, 3 - sh:3], w(k),
                                              accb[:, 0:sh], OP.mult, OP.add)
        C.inc(last, "vE")
        conv_tap[cc] = C.cnt["vE"]
    sec("vector", vf)
    v_tap = C.cnt["vE"]

    def gp2(g, cc=cc, data=data, v_tap=v_tap):
        C.wait(g, "vE", v_tap)
        C.inc(g.tensor_copy(tailsave[:, cc * 3:cc * 3 + 3], data[:, TT - 3:TT]), "gE")
    sec("gpsimd", gp2)
    g_tail = C.cnt["gE"]

    def sf(s, cc=cc, data=data, v_tap=v_tap, g_tail=g_tail):
        C.wait(s, "vE", v_tap)
        C.wait(s, "gE", g_tail)
        if cc < 32:
            if cc >= 2:
                C.wait(s, "doa" if cc % 2 == 0 else "dob")
            dst = xstg[cc % 2][:]
        else:
            dst = data
        C.inc(s.activation(dst, accb[:], AF.Silu), "sE")
    sec("scalar", sf)
    s_silu = C.cnt["sE"]

    if cc < 32:
        def df(sy, cc=cc, s_silu=s_silu):
            C.wait(sy, "sE", s_silu)
            spill_at[cc] = C.dma(sy.dma_start(
                xsspill[cc * 128:(cc + 1) * 128, tok0:tok0 + TT], xstg[cc % 2][:]),
                "doa" if cc % 2 == 0 else "dob")
        sec("sync", df)


def _a2(nc, C, sec, tok0, w_in, xhatF, wbuf, mup_bc, ztmp, zstg, zspill, pa):
    dw_at = {}
    mm_done = {}
    do_at = {}
    sz_at = {}

    def pre(g):
        def f(gp, g=g):
            col0 = g * 256
            if g >= 2:
                gp.wait_ge(C.sems["tE"], mm_done[g - 2])
            dw_at[g] = C.dma(gp.dma_start(
                wbuf[g % 2][:, 0:32 * 256].rearrange("p (c f) -> p c f", f=256),
                w_in.rearrange("(c p) f -> p c f", p=128)[:, :, col0:col0 + 256]),
                "dwa" if g % 2 == 0 else "dwb")
        sec("gpsimd", f)

    pre(0)
    pre(1)
    for g in range(16):
        col0 = g * 256
        for b in range(NB):
            idx = g * NB + b

            def tf(t, g=g, b=b, idx=idx):
                if b == 0:
                    C.wait(t, "dwa" if g % 2 == 0 else "dwb")
                C.bank_guard(t, f"pa{idx % 3}")
                for c in range(32):
                    mm = t.matmul(pa[idx % 3][:, 0:256],
                                  xhatF[:, c * TT + b * 128:c * TT + (b + 1) * 128],
                                  wbuf[g % 2][:, c * 256:(c + 1) * 256],
                                  start=(c == 0), stop=(c == 31))
                C.inc(mm, "tE")
                mm_done[g] = C.cnt["tE"]
            sec("tensor", tf)
            t_mm = C.cnt["tE"]

            def vf(v, col0=col0, idx=idx, t_mm=t_mm):
                C.wait(v, "tE", t_mm)
                if idx == 0:
                    C.wait(v, "dw")
                if idx - 1 in sz_at:
                    C.wait(v, "sE", sz_at[idx - 1])
                C.inc(v.tensor_tensor(ztmp[:, 0:256], pa[idx % 3][:, 0:256],
                                      mup_bc[:, col0:col0 + 256], OP.mult), "vE")
                C.bank_mark(f"pa{idx % 3}", "vE")
            sec("vector", vf)
            v_ev = C.cnt["vE"]

            def sf(s, idx=idx, v_ev=v_ev):
                C.wait(s, "vE", v_ev)
                if idx >= 2:
                    C.wait(s, "doa" if idx % 2 == 0 else "dob")
                C.inc(s.activation(zstg[idx % 2][:, 0:256], ztmp[:, 0:256],
                                   AF.Silu), "sE")
                sz_at[idx] = C.cnt["sE"]
            sec("scalar", sf)
            s_z = C.cnt["sE"]

            def df(sy, b=b, col0=col0, idx=idx, s_z=s_z):
                C.wait(sy, "sE", s_z)
                do_at[idx] = C.dma(sy.dma_start(
                    zspill[tok0 + b * 128:tok0 + (b + 1) * 128, col0:col0 + 256],
                    zstg[idx % 2][:, 0:256]), "doa" if idx % 2 == 0 else "dob")
            sec("sync", df)
        if g + 2 < 16:
            pre(g + 2)


def _bc(ap, dims):
    return bass.AP(ap.tensor, ap.offset, dims)


def _scan_front(nc, C, sec, tok0, ch, sc, wbuf, xsspill, zspill, dtT,
                dbias_bc, nega_bc, bmcf, ones_f16, ones_f32, tri_f16,
                su_f16, tri_f32, ident_bf, ident_f32, sm64, pa, pb, X):
    t0c = tok0 + ch * 128
    par = ch % 2
    xrd = wbuf[0][:, par * 4096:par * 4096 + 4096]
    zrd = wbuf[1][:, par * 4096:par * 4096 + 4096]

    def d0(sy, par=par):
        if X["xrd_free"][par]:
            sy.wait_ge(C.sems["tE"], X["xrd_free"][par])
        if X["zrd_free"][par]:
            sy.wait_ge(C.sems["vE"], X["zrd_free"][par])
        dsem = "dxa" if par == 0 else "dxb"
        C.dma(sy.dma_start(
            xrd.rearrange("p (c t) -> p c t", t=128),
            xsspill.rearrange("(c p) t -> p c t", p=128)[:, :, t0c:t0c + 128]), dsem)
        C.dma(sy.dma_start(zrd, zspill[t0c:t0c + 128, :]), dsem)
    sec("sync", d0)
    dsemn = "dxa" if par == 0 else "dxb"
    dxv = C.cnt[dsemn]

    for jb in range(4):
        def tf(t, jb=jb, dxv=dxv, dsemn=dsemn):
            if jb == 0:
                C.wait(t, dsemn, dxv)
            key = f"pb{jb % 2}"
            C.bank_guard(t, key)
            for i in range(8):
                c = jb * 8 + i
                mm = t.transpose(pb[jb % 2][:, i * 128:(i + 1) * 128],
                                 xrd[:, c * 128:(c + 1) * 128], ident_bf[:])
            C.inc(mm, "tE")
        sec("tensor", tf)
        t_tr = C.cnt["tE"]

        def sg(v, jb=jb, t_tr=t_tr):
            v.wait_ge(C.sems["tE"], t_tr)
            C.inc(v.tensor_copy(sc["xsT"][:, jb * 1024:(jb + 1) * 1024],
                                pb[jb % 2][:, 0:1024]), "vE")
            C.bank_mark(f"pb{jb % 2}", "vE")
        sec("vector", sg)
    X["xrd_free"][par] = C.cnt["tE"]

    def v_dt(v):
        dts = dtT[:, ch * 64:(ch + 1) * 64]
        if X["scum"]:
            C.wait(v, "sE", X["scum"])
        v.tensor_tensor(sc["t64a"][:], dts, dbias_bc[:], OP.add)
        C.inc(v.scalar_tensor_tensor(sc["t64b"][:], sc["t64a"][:], -1.0,
                                     sc["t64a"][:], OP.mult, OP.min), "vE")
    sec("vector", v_dt)
    v_1 = C.cnt["vE"]

    def s_dt(s, v_1=v_1):
        C.wait(s, "vE", v_1)
        s.activation(sc["t64c"][:], sc["t64b"][:], AF.Exp)
        C.inc(s.activation(sc["t64b"][:], sc["t64c"][:], AF.Ln, bias=1.0), "sE")
    sec("scalar", s_dt)
    s_1 = C.cnt["sE"]

    def v_dt2(v, s_1=s_1):
        C.wait(v, "sE", s_1)
        if X["tcu_prev"]:
            v.wait_ge(C.sems["tE"], X["tcu_prev"])
        v.tensor_scalar(sc["t64c"][:], sc["t64a"][:], 0.0, None, OP.max)
        v.tensor_tensor(sc["dtc"][:], sc["t64b"][:], sc["t64c"][:], OP.add)
        v.tensor_tensor(sc["dta32"][:], sc["dtc"][:], nega_bc[:], OP.mult)
        C.inc(v.tensor_copy(sc["dta16"][:], sc["dta32"][:]), "vE")
    sec("vector", v_dt2)
    v_2 = C.cnt["vE"]

    def t_cum(t, v_2=v_2):
        C.wait(t, "vE", v_2)
        C.bank_guard(t, "pa3")
        t.matmul(pa[3][0:64, 0:128], sc["dta16"][:], tri_f16[:], start=True,
                 stop=True)
        t.matmul(pa[3][:, 128:192], su_f16[:], sc["dta16"][:], start=True,
                 stop=True)
        C.inc(t.matmul(pa[3][0:1, 192:256], ones_f16[:, 0:1], sc["dta16"][:],
                       start=True, stop=True), "tE")
    sec("tensor", t_cum)
    t_cu = C.cnt["tE"]
    X["tcu_prev"] = t_cu

    def s_cum(s, t_cu=t_cu, par=par):
        C.wait(s, "tE", t_cu)
        if X["vyx_last"]:
            C.wait(s, "vE", X["vyx_last"])
        s.activation(sc["vf16"][par][:], pa[3][0:64, 0:128], AF.Exp)
        C.inc(s.activation(sc["t64a"][:], pa[3][:, 128:192], AF.Exp), "sE")
        C.bank_mark("pa3", "sE")
    sec("scalar", s_cum)
    s_2 = C.cnt["sE"]
    X["scum"] = s_2

    def v_sb(v, t_cu=t_cu):
        v.wait_ge(C.sems["tE"], t_cu)
        C.inc(v.tensor_copy(sc["srow"][:], pa[3][0:1, 192:256]), "vE")
    sec("vector", v_sb)
    v_sr = C.cnt["vE"]

    def t_sb2(t, v_sr=v_sr):
        C.wait(t, "vE", v_sr)
        C.bank_guard(t, "pa3")
        C.inc(t.matmul(pa[3][:, 448:512], ones_f32[0:1, :], sc["srow"][:],
                       start=True, stop=True), "tE")
    sec("tensor", t_sb2)
    t_sb2a = C.cnt["tE"]

    def s_sb(s, t_sb2a=t_sb2a):
        s.wait_ge(C.sems["tE"], t_sb2a)
        C.inc(s.activation(sc["sbc"][:], pa[3][:, 448:512], AF.Exp), "sE")
        C.bank_mark("pa3", "sE")
    sec("scalar", s_sb)

    def t_cb(t, par=par):
        C.wait(t, "vE")
        C.bank_guard(t, "pa3")
        t.matmul(pa[3][:, 256:384], bmcf[:, ch * 128:(ch + 1) * 128],
                 bmcf[:, TT + ch * 128:TT + (ch + 1) * 128], start=True, stop=True)
        C.bank_guard(t, "pb0")
        C.inc(t.transpose(pb[0][:, 0:128], bmcf[:, ch * 128:(ch + 1) * 128],
                          ident_bf[:]), "tE")
    sec("tensor", t_cb)
    t_cb_at = C.cnt["tE"]

    def v_cb(v, t_cb_at=t_cb_at, par=par):
        v.wait_ge(C.sems["tE"], t_cb_at)
        v.tensor_tensor(sc["cbtm"][par][:], pa[3][:, 256:384], tri_f32[:], OP.mult)
        C.bank_mark("pa3", "vE")
        C.inc(v.tensor_copy(sc["bmt"][par][:], pb[0][:, 0:128]), "vE")
        C.bank_mark("pb0", "vE")
    sec("vector", v_cb)
    X["front_v"] = C.cnt["vE"]


def _scan_groups(nc, C, sec, tok0, ch, sc, wbuf, dcol_bc, h_bf, bmcf, ysb,
                 e2_f16, bnb, tri_f16, su_f16, ident_bf, pa, pdm, pb, X):
    par = ch % 2
    zrd = wbuf[1][:, par * 4096:par * 4096 + 4096]
    front_v = X["front_v"]

    for g in range(8):
        gg = X["base_g"] + g
        heads = list(range(8 * g, 8 * g + 8))

        for half in range(2):
            hh = 2 * gg + half
            h0 = 8 * g + half * 4

            def v_r(v, hh=hh, h0=h0, front_v=front_v):
                C.wait(v, "vE", front_v)
                if X["dmm"].get(hh - 2):
                    v.wait_ge(C.sems["tE"], X["dmm"][hh - 2])
                C.inc(v.tensor_tensor(
                    sc["rhsd"][hh % 2][:].rearrange("p (a b) -> p a b", b=128),
                    _bc(tri_f16[:], [[128, 128], [0, 4], [1, 128]]),
                    _bc(sc["dta32"][:, h0:h0 + 4], [[64, 128], [1, 4], [0, 128]]),
                    OP.mult), "vE")
            sec("vector", v_r)
            v_rh = C.cnt["vE"]

            def t_d(t, hh=hh, v_rh=v_rh):
                C.wait(t, "vE", v_rh)
                key = f"pd{hh % 2}"
                C.bank_guard(t, key)
                half0 = (hh % 2) * 512
                for i in range(4):
                    mm = t.matmul(pdm[:, half0 + i * 128:half0 + (i + 1) * 128],
                                  su_f16[:],
                                  sc["rhsd"][hh % 2][:, i * 128:(i + 1) * 128],
                                  start=(i == 0), stop=(i == 3))
                C.inc(mm, "tE")
            sec("tensor", t_d)
            X["dmm"][hh] = C.cnt["tE"]

        def s_e(s, gg=gg):
            s.wait_ge(C.sems["tE"], X["dmm"][2 * gg + 1])
            if X["expfree"].get(gg - 2):
                C.wait(s, "vE", X["expfree"][gg - 2])
            C.inc(s.activation(sc["expd"][gg % 2][:], pdm[:, 0:1024], AF.Exp),
                  "sE")
            C.bank_mark("pd0", "sE")
            C.bank_mark("pd1", "sE")
        sec("scalar", s_e)
        exp_g = C.cnt["sE"]

        def v_gA(v, g=g, gg=gg, par=par, exp_g=exp_g):
            C.wait(v, "sE", exp_g)
            if X["mfree"].get(gg - 2):
                v.wait_ge(C.sems["tE"], X["mfree"][gg - 2])
            v.tensor_tensor(
                sc["mtil"][gg % 2][:].rearrange("p (a b) -> p a b", b=128),
                sc["expd"][gg % 2][:].rearrange("p (a b) -> p a b", b=128),
                _bc(sc["cbtm"][par][:], [[128, 128], [0, 8], [1, 128]]),
                OP.mult)
            v.tensor_tensor(
                sc["xsTd"][gg % 2][:].rearrange("p (a b) -> p a b", b=64),
                sc["xsT"][:, g * 512:(g + 1) * 512].rearrange("p (a b) -> p a b", b=64),
                _bc(sc["dtc"][:, g * 8:g * 8 + 8], [[64, 128], [1, 8], [0, 64]]),
                OP.mult)
            C.inc(v.tensor_tensor(
                sc["dxst"][:].rearrange("p (a b) -> p a b", b=64),
                sc["xsT"][:, g * 512:(g + 1) * 512].rearrange("p (a b) -> p a b", b=64),
                _bc(dcol_bc[:, g * 8:g * 8 + 8], [[64, 128], [1, 8], [0, 64]]),
                OP.mult), "vE")
        sec("vector", v_gA)
        v_ga = C.cnt["vE"]
        X["expfree"][gg] = v_ga

        def t_iv(t, g=g, gg=gg, par=par, v_ga=v_ga):
            C.wait(t, "vE", v_ga)
            C.bank_guard(t, "pa1")
            for i in range(8):
                t.matmul(pa[1][:, i * 64:(i + 1) * 64],
                         sc["mtil"][gg % 2][:, i * 128:(i + 1) * 128],
                         sc["xsTd"][gg % 2][:, i * 64:(i + 1) * 64],
                         start=(i == 0), stop=(i == 7))
            C.bank_guard(t, "pa0")
            if X["hev"].get(g):
                t.wait_ge(C.sems["vE"], X["hev"][g])
            for i, r in enumerate(heads):
                mm = t.matmul(pa[0][:, i * 64:(i + 1) * 64],
                              bmcf[:, TT + ch * 128:TT + (ch + 1) * 128],
                              h_bf[:, r * 64:(r + 1) * 64],
                              start=(i == 0), stop=(i == 7))
            C.inc(mm, "tE")
            key = f"pd{g % 2}"
            C.bank_guard(t, key)
            C.inc(t.matmul(pdm[:, (g % 2) * 512:(g % 2) * 512 + 512],
                           sc["vf16"][par][:], e2_f16[:, g * 512:(g + 1) * 512],
                           start=True, stop=True), "tE")
        sec("tensor", t_iv)
        t_ve = C.cnt["tE"]
        t_int = t_ve - 1
        X["mfree"][gg] = t_int

        def v_gB(v, g=g, t_ve=t_ve, t_int=t_int):
            v.wait_ge(C.sems["tE"], t_ve)
            v.tensor_copy(sc["vexp"][:], pdm[:, (g % 2) * 512:(g % 2) * 512 + 512])
            C.bank_mark(f"pd{g % 2}", "vE")
            C.inc(v.scalar_tensor_tensor(sc["ytmp"][:], pa[0][:, 0:512], 1.0,
                                         sc["vexp"][:], OP.mult, OP.mult), "vE")
            C.bank_mark("pa0", "vE")
        sec("vector", v_gB)
        v_yt = C.cnt["vE"]

        def v_gC(v, g=g, gg=gg, par=par, t_int=t_int):
            v.scalar_tensor_tensor(sc["yA"][:], pa[1][:, 0:512], 1.0,
                                   sc["ytmp"][:], OP.mult, OP.add)
            C.bank_mark("pa1", "vE")
            v.scalar_tensor_tensor(sc["yA"][:], sc["dxst"][:], 1.0, sc["yA"][:],
                                   OP.mult, OP.add)
            v.tensor_tensor(ysb[:, g * 512:(g + 1) * 512], sc["yA"][:],
                            zrd[:, g * 512:(g + 1) * 512], OP.mult)
            v.bn_stats(bnb[g % 2][:, (g // 2) * 6:(g // 2) * 6 + 6],
                       ysb[:, g * 512:(g + 1) * 512])
            if X["st"].get(gg - 1):
                v.wait_ge(C.sems["tE"], X["st"][gg - 1])
            v.tensor_tensor(
                sc["xtld"][:].rearrange("p (a b) -> p a b", b=64),
                sc["xsTd"][gg % 2][:].rearrange("p (a b) -> p a b", b=64),
                _bc(sc["t64a"][:, g * 8:g * 8 + 8], [[64, 128], [1, 8], [0, 64]]),
                OP.mult)
            C.inc(v.tensor_tensor(
                sc["hdec"][:].rearrange("p (a b) -> p a b", b=64),
                h_bf[:, g * 512:(g + 1) * 512].rearrange("p (a b) -> p a b", b=64),
                _bc(sc["sbc"][:, g * 8:g * 8 + 8], [[64, 128], [1, 8], [0, 64]]),
                OP.mult), "vE")
        sec("vector", v_gC)
        v_xt = C.cnt["vE"]

        def t_st(t, g=g, gg=gg, par=par, v_xt=v_xt):
            C.wait(t, "vE", v_xt)
            C.bank_guard(t, "pa2")
            for i in range(8):
                t.matmul(pa[2][:, i * 64:(i + 1) * 64], sc["bmt"][par][:],
                         sc["xtld"][:, i * 64:(i + 1) * 64],
                         start=(i == 0), stop=False)
            C.inc(t.matmul(pa[2][:, 0:512], ident_bf[:], sc["hdec"][:],
                           start=False, stop=True), "tE")
            X["st"][gg] = C.cnt["tE"]
        sec("tensor", t_st)
        t_hu = C.cnt["tE"]
        if g == 7:
            X["st_last"] = t_hu

        def v_h(v, g=g, t_hu=t_hu):
            v.wait_ge(C.sems["tE"], t_hu)
            C.inc(v.tensor_copy(h_bf[:, g * 512:(g + 1) * 512], pa[2][:, 0:512]),
                  "vE")
            C.bank_mark("pa2", "vE")
            X["hev"][g] = C.cnt["vE"]
        sec("vector", v_h)

    X["vyx_last"] = C.cnt["vE"]
    X["zrd_free"][par] = C.cnt["vE"]
    X["base_g"] += 8


def _scan_tail(nc, C, sec, tok0, ch, sc, nw_bc, ysb, yhatF, ident_bf, smalls,
               sm64, bnb, pa, pb, X):
    def v_tail(v):
        v.bn_aggr(smalls[:, 12:14], bnb[0][:, 0:24])
        v.bn_aggr(smalls[:, 20:22], bnb[1][:, 0:24])
        mA, vA = smalls[:, 12:13], smalls[:, 13:14]
        mB, vB = smalls[:, 20:21], smalls[:, 21:22]
        t0, t1 = smalls[:, 16:17], smalls[:, 17:18]
        v.tensor_tensor(smalls[:, 8:9], mA, mB, OP.add)
        v.tensor_scalar(smalls[:, 8:9], smalls[:, 8:9], 0.5, None, OP.mult)
        v.tensor_tensor(t0, mA, mB, OP.subtract)
        v.tensor_tensor(t0, t0, t0, OP.mult)
        v.tensor_tensor(t1, vA, vB, OP.add)
        v.tensor_scalar(t1, t1, 0.5, None, OP.mult)
        v.scalar_tensor_tensor(smalls[:, 9:10], t0, 0.25, t1, OP.mult, OP.add)
        v.tensor_tensor(t0, smalls[:, 8:9], smalls[:, 8:9], OP.mult)
        v.tensor_tensor(smalls[:, 9:10], smalls[:, 9:10], t0, OP.add)
        v.tensor_scalar(smalls[:, 9:10], smalls[:, 9:10], EPS, None, OP.add)
        C.inc(v.tensor_copy(smalls[:, 18:19], smalls[:, 9:10]), "vE")
    sec("vector", v_tail)
    v_t1 = C.cnt["vE"]

    def s_tail(s, v_t1=v_t1):
        C.wait(s, "vE", v_t1)
        if X["vt2"]:
            C.wait(s, "vE", X["vt2"])
        C.inc(_rstd_finish(s, smalls[:, 18:19], smalls[:, 10:11],
                           smalls[:, 11:12]), "sE")
    sec("scalar", s_tail)
    s_t1 = C.cnt["sE"]

    def v_tail2(v, s_t1=s_t1):
        C.wait(v, "sE", s_t1)
        if not X["ytr"]:
            C.wait(v, "dw")
        if X["ytr"]:
            v.wait_ge(C.sems["tE"], X["ytr"])
        C.inc(v.scalar_tensor_tensor(sc["yhat_tok"][:], ysb[:], smalls[:, 10:11],
                                     nw_bc[:], OP.mult, OP.mult), "vE")
    sec("vector", v_tail2)
    v_t2 = C.cnt["vE"]
    X["vt2"] = v_t2

    for jb in range(4):
        def tf2(t, jb=jb, v_t2=v_t2):
            if jb == 0:
                C.wait(t, "vE", v_t2)
            key = f"pb{jb % 2}"
            C.bank_guard(t, key)
            for i in range(8):
                c = jb * 8 + i
                mm = t.transpose(pb[jb % 2][:, i * 128:(i + 1) * 128],
                                 sc["yhat_tok"][:, c * 128:(c + 1) * 128],
                                 ident_bf[:])
            C.inc(mm, "tE")
        sec("tensor", tf2)
        t_tr2 = C.cnt["tE"]

        def sg2(v, jb=jb, t_tr2=t_tr2):
            v.wait_ge(C.sems["tE"], t_tr2)
            dstap = yhatF[:, jb * 8 * TT:(jb * 8 + 8) * TT] \
                .rearrange("p (c t) -> p c t", t=TT)[:, :, ch * 128:(ch + 1) * 128]
            C.inc(v.tensor_copy(
                dstap, pb[jb % 2][:, 0:1024].rearrange("p (c t) -> p c t", t=128)),
                "vE")
            C.bank_mark(f"pb{jb % 2}", "vE")
        sec("vector", sg2)
    X["ytr"] = C.cnt["tE"]


def _e_outproj(nc, C, sec, tok0, w_out, yhatF, wbuf, x1stg, x1spill, pa):
    dw_at = {}
    mm_done = {}
    do_at = {}

    def pre(g):
        def f(gp, g=g):
            col0 = g * 256
            if g >= 2:
                gp.wait_ge(C.sems["tE"], mm_done[g - 2])
            dw_at[g] = C.dma(gp.dma_start(
                wbuf[g % 2][:, 0:32 * 256].rearrange("p (c f) -> p c f", f=256),
                w_out.rearrange("(c p) f -> p c f", p=128)[:, :, col0:col0 + 256]),
                "dwa" if g % 2 == 0 else "dwb")
        sec("gpsimd", f)

    pre(0)
    pre(1)
    for g in range(16):
        col0 = g * 256
        for b in range(NB):
            idx = g * NB + b

            def tf(t, g=g, b=b, idx=idx):
                if b == 0:
                    C.wait(t, "dwa" if g % 2 == 0 else "dwb")
                C.bank_guard(t, f"pa{idx % 3}")
                for c in range(32):
                    mm = t.matmul(pa[idx % 3][:, 0:256],
                                  yhatF[:, c * TT + b * 128:c * TT + (b + 1) * 128],
                                  wbuf[g % 2][:, c * 256:(c + 1) * 256],
                                  start=(c == 0), stop=(c == 31))
                C.inc(mm, "tE")
                mm_done[g] = C.cnt["tE"]
            sec("tensor", tf)
            t_mm = C.cnt["tE"]

            def sf(v, idx=idx, t_mm=t_mm):
                v.wait_ge(C.sems["tE"], t_mm)
                if idx >= 2:
                    C.wait(v, "doa" if idx % 2 == 0 else "dob")
                C.inc(v.tensor_copy(x1stg[idx % 2][:, 0:256], pa[idx % 3][:, 0:256]),
                      "vE")
                C.bank_mark(f"pa{idx % 3}", "vE")
            sec("vector", sf)
            s_x1 = C.cnt["vE"]

            def df(sy, b=b, col0=col0, idx=idx, s_x1=s_x1):
                C.wait(sy, "vE", s_x1)
                do_at[idx] = C.dma(sy.dma_start(
                    x1spill[tok0 + b * 128:tok0 + (b + 1) * 128, col0:col0 + 256],
                    x1stg[idx % 2][:, 0:256]), "doa" if idx % 2 == 0 else "dob")
            sec("sync", df)
        if g + 2 < 16:
            pre(g + 2)


def _f_prep(nc, C, sec, ffn_a, ffn_b, ffn_wq, ffn_keys, wbuf, awT, bwT,
            mT, keysT, ident_bf, pa, pb):
    def d0(gp):
        for h in range(2):
            C.dma(gp.dma_start(
                wbuf[h][:, 0:8192].rearrange("p (j d) -> p j d", d=4096),
                ffn_a.rearrange("(j p) d -> p j d", p=128)[:, 2 * h:2 * h + 2, :]),
                "dw")
    sec("gpsimd", d0)
    dw_a = C.cnt["dw"]

    for cb in range(16):
        def tf(t, cb=cb):
            if cb == 0:
                C.wait(t, "dw", dw_a)
            key = f"pb{cb % 2}"
            C.bank_guard(t, key)
            for cc in range(2):
                c = cb * 2 + cc
                for j in range(4):
                    mm = t.transpose(
                        pb[cb % 2][:, cc * 512 + j * 128:cc * 512 + (j + 1) * 128],
                        wbuf[j // 2][:, (j % 2) * 4096 + c * 128:
                                     (j % 2) * 4096 + (c + 1) * 128],
                        ident_bf[:])
            C.inc(mm, "tE")
        sec("tensor", tf)
        t_tr = C.cnt["tE"]

        def sg(s, cb=cb, t_tr=t_tr):
            s.wait_ge(C.sems["tE"], t_tr)
            C.inc(s.activation(awT[:, cb * 1024:(cb + 1) * 1024],
                               pb[cb % 2][:, 0:1024], AF.Copy), "sE")
            C.bank_mark(f"pb{cb % 2}", "sE")
        sec("scalar", sg)
    t_a_done = C.cnt["tE"]

    def d1(gp, t_a_done=t_a_done):
        gp.wait_ge(C.sems["tE"], t_a_done)
        for h in range(2):
            C.dma(gp.dma_start(
                wbuf[h][:, 0:8192].rearrange("p (j n) -> p j n", n=512),
                ffn_b.rearrange("(j p) n -> p j n", p=128)[:, 16 * h:16 * h + 16, :]),
                "dw")
    sec("gpsimd", d1)
    dw_b = C.cnt["dw"]

    for nch in range(4):
        for jb in range(4):
            def tf2(t, nch=nch, jb=jb):
                if nch == 0 and jb == 0:
                    C.wait(t, "dw", dw_b)
                key = f"pb{jb % 2}"
                C.bank_guard(t, key)
                for i in range(8):
                    j = jb * 8 + i
                    mm = t.transpose(pb[jb % 2][:, i * 128:(i + 1) * 128],
                                     wbuf[j // 16][:, (j % 16) * 512 + nch * 128:
                                                   (j % 16) * 512 + (nch + 1) * 128],
                                     ident_bf[:])
                C.inc(mm, "tE")
            sec("tensor", tf2)
            t_tr2 = C.cnt["tE"]

            def sg2(s, nch=nch, jb=jb, t_tr2=t_tr2):
                s.wait_ge(C.sems["tE"], t_tr2)
                C.inc(s.activation(
                    bwT[:, nch * 4096 + jb * 1024:nch * 4096 + (jb + 1) * 1024],
                    pb[jb % 2][:, 0:1024], AF.Copy), "sE")
                C.bank_mark(f"pb{jb % 2}", "sE")
            sec("scalar", sg2)
    t_b_done = C.cnt["tE"]

    def d2(gp, t_b_done=t_b_done):
        gp.wait_ge(C.sems["tE"], t_b_done)
        C.dma(gp.dma_start(wbuf[0][0:16, 0:4096], ffn_wq), "dw")
        C.dma(gp.dma_start(wbuf[0][0:64, 4096:4112], ffn_keys), "dw")
    sec("gpsimd", d2)
    dw_q = C.cnt["dw"]

    def tf4(t):
        C.wait(t, "dw", dw_q)
        C.bank_guard(t, "pb0")
        C.inc(t.transpose(pb[0][0:16, 0:64], wbuf[0][0:64, 4096:4112],
                          ident_bf[0:64, 0:64]), "tE")
    sec("tensor", tf4)
    t_k = C.cnt["tE"]

    def sg4(s, t_k=t_k):
        s.wait_ge(C.sems["tE"], t_k)
        C.inc(s.activation(keysT[:], pb[0][0:16, 0:64], AF.Copy), "sE")
        C.bank_mark("pb0", "sE")
    sec("scalar", sg4)
    s_k = C.cnt["sE"]

    for mb in range(4):
        def tm(t, mb=mb, s_k=s_k):
            if mb == 0:
                C.wait(t, "sE", s_k)
            C.bank_guard(t, f"pa{mb % 2}")
            for i in range(8):
                c = mb * 8 + i
                mm = t.matmul(pa[mb % 2][:, i * 64:(i + 1) * 64],
                              wbuf[0][0:16, c * 128:(c + 1) * 128],
                              keysT[:], start=True, stop=True)
            C.inc(mm, "tE")
        sec("tensor", tm)
        t_m = C.cnt["tE"]

        def vm(v, mb=mb, t_m=t_m):
            v.wait_ge(C.sems["tE"], t_m)
            C.inc(v.tensor_copy(mT[:, mb * 512:(mb + 1) * 512],
                                pa[mb % 2][:, 0:512]), "vE")
            C.bank_mark(f"pa{mb % 2}", "vE")
        sec("vector", vm)


def _f_x(nc, C, sec, tok0, b, x_in, x1spill, xblk2, ysb2, x2F, x2tok,
         awT, mT, bnb, smalls, ident_bf, pa, pb, Xf):
    t0b = tok0 + b * 128
    par = b % 2
    zi = 0 if par == 0 else 3
    si = 1 if par == 0 else 2

    def d0(sy, b=b):
        if Xf["aq"].get(b - 1):
            sy.wait_ge(C.sems["tE"], Xf["aq"][b - 1])
        if Xf["x2Fev"].get(b - 1):
            sy.wait_ge(C.sems["sE"], Xf["x2Fev"][b - 1])
        if Xf["vln"].get(b - 1):
            sy.wait_ge(C.sems["vE"], Xf["vln"][b - 1])
        dsem = "dxa" if b % 2 == 0 else "dxb"
        C.dma(sy.dma_start(xblk2[:], x_in[t0b:t0b + 128, :]), dsem)
        C.dma(sy.dma_start(x2F[:], x1spill[t0b:t0b + 128, :]), dsem)
    sec("sync", d0)

    def v0(v):
        C.wait(v, "dxa" if b % 2 == 0 else "dxb")
        if Xf["s1"].get(b - 1):
            v.wait_ge(C.sems["sE"], Xf["s1"][b - 1])
        v.tensor_tensor(ysb2[:], x2F[:, 0:H], xblk2[:], OP.add)
        _ln_stats(C, v, bnb, ysb2[:], H, smalls[:, 8:9], smalls[:, 9:10],
                  smalls, 12, want_meansq=False)
        C.inc(v.tensor_copy(smalls[:, 18:19], smalls[:, 9:10]), "vE")
    sec("vector", v0)
    v_ln = C.cnt["vE"]
    Xf["vln"][b] = v_ln

    def s0(s, v_ln=v_ln, par=par):
        C.wait(s, "vE", v_ln)
        _rstd_finish(s, smalls[:, 18:19], smalls[:, 10:11], smalls[:, 11:12])
        s.activation(smalls[:, 11:12], smalls[:, 8:9], AF.Copy,
                     scale=smalls[:, 10:11])
        s.activation(smalls[:, 19:20], smalls[:, 11:12], AF.Copy, scale=-1.0)
        if Xf["tb"].get(b - 2):
            s.wait_ge(C.sems["tE"], Xf["tb"][b - 2])
        C.inc(s.activation(x2tok[par][:], ysb2[:], AF.Identity,
                           scale=smalls[:, 10:11], bias=smalls[:, 19:20]), "sE")
    sec("scalar", s0)
    s_x2 = C.cnt["sE"]
    Xf["s1"][b] = s_x2

    for jb in range(4):
        def tf(t, jb=jb, s_x2=s_x2, par=par):
            if jb == 0:
                C.wait(t, "sE", s_x2)
            C.bank_guard(t, f"pb{jb % 2}")
            for i in range(8):
                c = jb * 8 + i
                mm = t.transpose(pb[jb % 2][:, i * 128:(i + 1) * 128],
                                 x2tok[par][:, c * 128:(c + 1) * 128], ident_bf[:])
            C.inc(mm, "tE")
        sec("tensor", tf)
        t_tr = C.cnt["tE"]

        def sg(s, jb=jb, t_tr=t_tr):
            s.wait_ge(C.sems["tE"], t_tr)
            C.inc(s.activation(x2F[:, jb * 1024:(jb + 1) * 1024],
                               pb[jb % 2][:, 0:1024], AF.Copy), "sE")
            C.bank_mark(f"pb{jb % 2}", "sE")
        sec("scalar", sg)
    Xf["x2tr"][b] = C.cnt["tE"]
    s_x2f = C.cnt["sE"]
    Xf["x2Fev"][b] = s_x2f

    def t_a(t, s_x2f=s_x2f, zi=zi, si=si):
        C.wait(t, "sE", s_x2f)
        C.bank_guard(t, f"pa{zi}")
        C.bank_guard(t, f"pa{si}")
        last = None
        for c in range(32):
            t.matmul(pa[zi][:, 0:512], x2F[:, c * 128:(c + 1) * 128],
                     awT[:, c * 512:(c + 1) * 512], start=(c == 0), stop=(c == 31))
            last = t.matmul(pa[si][:, 0:64], x2F[:, c * 128:(c + 1) * 128],
                            mT[:, c * 64:(c + 1) * 64], start=(c == 0),
                            stop=(c == 31))
        C.inc(last, "tE")
    sec("tensor", t_a)
    Xf["aq"][b] = C.cnt["tE"]


def _f_y(nc, C, sec, tok0, b, out_t, x3f, x2tok, bwT, fscr, fscr2, gq, gwork,
         fcmp, zhat, zhT, bnb2, smalls, rcols, ident_bf, pa, pdm, pb,
         do_at, Xf):
    t0b = tok0 + b * 128
    par = b % 2
    zi = 0 if par == 0 else 3
    si = 1 if par == 0 else 2
    t_aq = Xf["aq"][b]

    def v_sc(v, t_aq=t_aq, si=si):
        C.wait(v, "tE", t_aq)
        v.tensor_copy(gq[:], pa[si][:, 0:64])
        C.bank_mark(f"pa{si}", "vE")
        v.tensor_tensor(
            fcmp[:, 0:4096].rearrange("p (a b) -> p a b", b=64),
            _bc(gq[:], [[64, 128], [0, 64], [1, 64]]),
            _bc(gq[:], [[64, 128], [1, 64], [0, 64]]),
            OP.is_gt)
        v.reduce_sum(fscr2[:], fcmp[:, 0:4096].rearrange("p (a b) -> p a b", b=64),
                     axis=AX)
        v.reduce_max(rcols[0][:], gq[:], axis=AX)
        C.inc(v.tensor_scalar(rcols[1][:], rcols[0][:], -1.0, None, OP.mult),
              "vE")
    sec("vector", v_sc)
    v_rt = C.cnt["vE"]

    def s_r(s, v_rt=v_rt):
        C.wait(s, "vE", v_rt)
        C.inc(s.activation(fscr[:, 0:64], gq[:], AF.Exp, bias=rcols[1][:]), "sE")
    sec("scalar", s_r)
    s_re = C.cnt["sE"]

    def v_g(v, s_re=s_re):
        C.wait(v, "sE", s_re)
        v.tensor_scalar(gwork[:], fscr2[:], 15.0, None, OP.is_le)
        v.tensor_tensor(gwork[:], gwork[:], fscr[:, 0:64], OP.mult)
        v.reduce_sum(rcols[3][:], gwork[:], axis=AX)
        v.reciprocal(rcols[0][:], rcols[3][:])
        C.inc(v.tensor_scalar(gwork[:], gwork[:], rcols[0][:], 2.0,
                              OP.mult, OP.mult), "vE")
    sec("vector", v_g)

    def v_z(v, zi=zi):
        C.inc(v.scalar_tensor_tensor(
            zhat[:].rearrange("p (a b) -> p a b", b=8),
            pa[zi][:, 0:512].rearrange("p (a b) -> p a b", b=8), 1.0,
            _bc(gwork[:], [[64, 128], [1, 64], [0, 8]]),
            OP.mult, OP.mult), "vE")
        C.bank_mark(f"pa{zi}", "vE")
    sec("vector", v_z)
    v_zh = C.cnt["vE"]

    def t_zt(t, v_zh=v_zh):
        C.wait(t, "vE", v_zh)
        C.bank_guard(t, "pb0")
        last = None
        for i in range(4):
            last = t.transpose(pb[0][:, i * 128:(i + 1) * 128],
                               zhat[:, i * 128:(i + 1) * 128], ident_bf[:])
        C.inc(last, "tE")
    sec("tensor", t_zt)
    t_zT = C.cnt["tE"]

    def s_zt(s, t_zT=t_zT):
        s.wait_ge(C.sems["tE"], t_zT)
        C.inc(s.activation(zhT[:], pb[0][:, 0:512], AF.Copy), "sE")
        C.bank_mark("pb0", "sE")
    sec("scalar", s_zt)
    s_zTT = C.cnt["sE"]

    for og in range(8):
        def t_b(t, og=og, s_zTT=s_zTT, par=par):
            if og == 0:
                C.wait(t, "sE", s_zTT)
            key = f"pd{og % 2}"
            C.bank_guard(t, key)
            half = (og % 2) * 512
            for nch in range(4):
                t.matmul(pdm[:, half:half + 512],
                         zhT[:, nch * 128:(nch + 1) * 128],
                         bwT[:, nch * 4096 + og * 512:
                             nch * 4096 + (og + 1) * 512],
                         start=(nch == 0), stop=False)
            C.inc(t.matmul(pdm[:, half:half + 512], ident_bf[:],
                           x2tok[par][:, og * 512:(og + 1) * 512],
                           start=False, stop=True), "tE")
        sec("tensor", t_b)
        t_bo = C.cnt["tE"]

        def s_vo(s, og=og, t_bo=t_bo, par=par):
            s.wait_ge(C.sems["tE"], t_bo)
            if og == 0:
                C.wait(s, "do" if par == 1 else "do2")
            half = (og % 2) * 512
            C.inc(s.activation(x3f[:, og * 512:(og + 1) * 512],
                               pdm[:, half:half + 512], AF.Copy), "sE")
            C.bank_mark(f"pd{og % 2}", "sE")
        sec("scalar", s_vo)
    Xf["tb"][b] = C.cnt["tE"]
    s_voe = C.cnt["sE"]

    def v_l2(v, s_voe=s_voe):
        C.wait(v, "sE", s_voe)
        _ln_stats(C, v, bnb2, x3f[:], H, smalls[:, 20:21], smalls[:, 21:22],
                  smalls, 24, want_meansq=False)
        C.inc(v.tensor_copy(smalls[:, 30:31], smalls[:, 21:22]), "vE")
    sec("vector", v_l2)
    v_l2a = C.cnt["vE"]

    def s_l2(s, v_l2a=v_l2a):
        C.wait(s, "vE", v_l2a)
        _rstd_finish(s, smalls[:, 30:31], smalls[:, 22:23], smalls[:, 23:24])
        s.activation(smalls[:, 23:24], smalls[:, 20:21], AF.Copy,
                     scale=smalls[:, 22:23])
        s.activation(smalls[:, 26:27], smalls[:, 23:24], AF.Copy, scale=-1.0)
        C.inc(s.activation(x3f[:], x3f[:], AF.Identity,
                           scale=smalls[:, 22:23], bias=smalls[:, 26:27]), "sE")
    sec("scalar", s_l2)
    s_ot = C.cnt["sE"]

    def d_out(sy, b=b, s_ot=s_ot, par=par):
        sy.wait_ge(C.sems["sE"], s_ot)
        do_at[b] = C.dma(sy.dma_start(out_t[t0b:t0b + 128, :], x3f[:]),
                         "do" if par == 0 else "do2")
    sec("sync", d_out)


def kernel(**inputs):
    _ = inputs.get("L0_attn_residual")
    x = np.ascontiguousarray(inputs["L0_attn_hidden_states"], dtype=np.float32)
    nc = build()
    base = {
        "w_in": np.ascontiguousarray(inputs["L0_attn_in_proj_w"], np.float32),
        "conv_w": np.ascontiguousarray(inputs["L0_attn_conv_w"], np.float32),
        "a_log": np.ascontiguousarray(inputs["L0_attn_A_log"], np.float32).reshape(1, NH),
        "dt_bias": np.ascontiguousarray(inputs["L0_attn_dt_bias"], np.float32).reshape(1, NH),
        "d_vec": np.ascontiguousarray(inputs["L0_attn_D"], np.float32).reshape(1, NH),
        "ln_w": np.ascontiguousarray(inputs["L0_attn_ln_w"], np.float32).reshape(1, H),
        "norm_w": np.ascontiguousarray(inputs["L0_attn_norm_w"], np.float32).reshape(1, DS),
        "mup": np.ascontiguousarray(inputs["L0_attn_mup_vector"], np.float32).reshape(1, PROJ),
        "w_out": np.ascontiguousarray(inputs["L0_attn_out_proj_w"], np.float32),
        "ffn_a": np.ascontiguousarray(inputs["L0_ffn_A_w"], np.float32),
        "ffn_b": np.ascontiguousarray(inputs["L0_ffn_B_w"], np.float32),
        "ffn_wq": np.ascontiguousarray(inputs["L0_ffn_Wq_w"], np.float32),
        "ffn_keys": np.ascontiguousarray(inputs["L0_ffn_keys"], np.float32),
    }
    in_maps = [dict(base, x=np.ascontiguousarray(x[c])) for c in range(NCORES)]
    res = run_bass_kernel_spmd(nc, in_maps, core_ids=list(range(NCORES)))
    return np.stack([res.results[c]["out"] for c in range(NCORES)], axis=0)


# version 18
# speedup vs baseline: 1.0075x; 1.0036x over previous
import contextlib
import numpy as np
import concourse.bass as bass
import concourse.mybir as mybir
from concourse.bass_utils import run_bass_kernel_spmd
from concourse.alu_op_type import AluOpType as OP

F32 = mybir.dt.float32
F16 = mybir.dt.float16
BF16 = mybir.dt.bfloat16
AF = mybir.ActivationFunctionType
AX = mybir.AxisListType.X

L = 2048
H = 4096
DS = 4096
PROJ = 8512
CONVD = 4352
NH = 64
TT = 1024
NB = 8
NT = L // TT
EPS = 1e-5
NCORES = 8
EXT = TT + 3

ALL_ENG = ["sync", "gpsimd", "tensor", "vector", "scalar"]
_MERGE_UNTIL = 175

_DRAIN_OPS = frozenset({
    "tensor_scalar", "tensor_tensor", "scalar_tensor_tensor", "tensor_copy",
    "tensor_tensor_scan", "reduce_max", "reduce_sum", "tensor_reduce",
    "bn_stats", "bn_aggr", "reciprocal", "memset", "activation", "select",
    "copy",
})


class _RecCall:
    __slots__ = ("name", "a", "k", "chain")

    def __init__(self, name, a, k):
        self.name, self.a, self.k = name, a, k
        self.chain = []

    def then_inc(self, sem, n):
        self.chain.append((sem, n))
        return self


class _RecEngine:

    def __init__(self, buf):
        self._buf = buf

    def __getattr__(self, name):
        def call(*a, **k):
            rc = _RecCall(name, a, k)
            self._buf.append(rc)
            return rc
        return call


def _replay(eng, calls):
    for rc in calls:
        inst = getattr(eng, rc.name)(*rc.a, **rc.k)
        for sem, n in rc.chain:
            inst = inst.then_inc(sem, n)


def _iv(ap):
    dims = list(ap.ap)
    span = sum(s * (c - 1) for s, c in dims[1:])
    return (ap.tensor.name, ap.offset, ap.offset + span + 1)


def _ovl(iv, ivs):
    n, lo, hi = iv
    return any(n == m and lo < h and l < hi for m, l, h in ivs)


class _AutoDrain:

    def __init__(self, eng, state=None):
        self._e = eng
        self._w, self._r = state if state is not None else ([], [])

    def __getattr__(self, name):
        attr = getattr(self._e, name)
        if name in _DRAIN_OPS:
            def w(*a, **k):
                aps = [x for x in list(a) + list(k.values())
                       if isinstance(x, bass.AP)]
                ivs = [_iv(ap) for ap in aps]
                out, ins = ivs[:1], ivs[1:]
                if (any(_ovl(i, self._w) for i in ins)
                        or any(_ovl(o, self._w) for o in out)
                        or any(_ovl(o, self._r) for o in out)):
                    self._e.drain()
                    self._w.clear()
                    self._r.clear()
                self._w.extend(out)
                self._r.extend(ins)
                return attr(*a, **k)
            return w
        return attr


def bcast_ap(dram_ap, npart, cols):
    return bass.AP(dram_ap.tensor, 0, [[0, npart], [1, cols]])


class Ctx:

    def __init__(self, nc, es):
        self.nc = nc
        self.sems = {}
        self.cnt = {}
        for name in ["dw", "dwa", "dwb", "dx", "dxa", "dxb", "do", "do2",
                     "doa", "dob", "tE", "vE", "sE", "gE"]:
            self.sems[name] = es.enter_context(nc.semaphore(name))
            self.cnt[name] = 0
        self.bank_rd = {}
        self.merge_scalar = True

    def inc(self, instr, name, n=1):
        instr.then_inc(self.sems[name], n)
        self.cnt[name] += n
        return self.cnt[name]

    def dma(self, instr, name):
        return self.inc(instr, name, 16)

    def wait(self, eng, name, val=None):
        v = self.cnt[name] if val is None else val
        if v:
            eng.wait_ge(self.sems[name], v)

    def bank_guard(self, eng, key):
        if key in self.bank_rd:
            sem, val = self.bank_rd[key]
            if val:
                eng.wait_ge(self.sems[sem], val)

    def bank_mark(self, key, sem):
        self.bank_rd[key] = (sem, self.cnt[sem])


def build(debug=()):
    nc = bass.Bass("TRN2", target_bir_lowering=False)
    debug = set(debug)

    x_in = nc.dram_tensor("x", [L, H], F32, kind="ExternalInput").ap()
    w_in = nc.dram_tensor("w_in", [H, PROJ], F32, kind="ExternalInput").ap()
    conv_w = nc.dram_tensor("conv_w", [CONVD, 4], F32, kind="ExternalInput").ap()
    a_log = nc.dram_tensor("a_log", [1, NH], F32, kind="ExternalInput").ap()
    dt_bias = nc.dram_tensor("dt_bias", [1, NH], F32, kind="ExternalInput").ap()
    d_vec = nc.dram_tensor("d_vec", [1, NH], F32, kind="ExternalInput").ap()
    ln_w = nc.dram_tensor("ln_w", [1, H], F32, kind="ExternalInput").ap()
    norm_w = nc.dram_tensor("norm_w", [1, DS], F32, kind="ExternalInput").ap()
    mup = nc.dram_tensor("mup", [1, PROJ], F32, kind="ExternalInput").ap()
    w_out = nc.dram_tensor("w_out", [DS, H], F32, kind="ExternalInput").ap()
    ffn_a = nc.dram_tensor("ffn_a", [512, H], F32, kind="ExternalInput").ap()
    ffn_b = nc.dram_tensor("ffn_b", [H, 512], F32, kind="ExternalInput").ap()
    ffn_wq = nc.dram_tensor("ffn_wq", [16, H], F32, kind="ExternalInput").ap()
    ffn_keys = nc.dram_tensor("ffn_keys", [NH, 16], F32, kind="ExternalInput").ap()

    out_t = nc.dram_tensor("out", [L, H], F32, kind="ExternalOutput").ap()

    def scratch(name, shape, dtype):
        if name in debug:
            return nc.dram_tensor(name, shape, dtype, kind="ExternalOutput").ap()
        return nc.dram_tensor(name, shape, dtype).ap()

    zspill = scratch("zspill", [L, DS], BF16)
    xsspill = scratch("xsspill", [DS, L], BF16)
    x1spill = scratch("x1spill", [L, H], BF16)

    es = contextlib.ExitStack()
    with es:
        C = Ctx(nc, es)
        block = es.enter_context(nc.Block())

        rbuf = {"scalar": []}
        rec = {e: _RecEngine(rbuf[e]) for e in rbuf}
        wrapped = {"scalar": _AutoDrain(rec["scalar"])}
        _vstate = ([], [])

        def flush_one(e):
            calls = rbuf[e]
            if calls:
                rbuf[e] = []
                rec[e]._buf = rbuf[e]
                getattr(block, e)(lambda eng, calls=calls: _replay(eng, calls))

        def sec(eng_name, fn):
            if eng_name in rbuf and C.merge_scalar:
                fn(wrapped[eng_name])
                if len(rbuf[eng_name]) > 48:
                    flush_one(eng_name)
            elif eng_name == "scalar":
                flush_one("scalar")
                getattr(block, eng_name)(lambda e: fn(_AutoDrain(e)))
            elif eng_name == "vector":
                getattr(block, eng_name)(
                    lambda e: fn(_AutoDrain(e, _vstate)))
            else:
                getattr(block, eng_name)(fn)

        def flush():
            for e in rbuf:
                flush_one(e)

        def barrier():
            flush()
            snap = dict(C.cnt)
            for e in ALL_ENG:
                def f(eng, snap=snap):
                    for name, v in snap.items():
                        if v:
                            eng.wait_ge(C.sems[name], v)
                sec(e, f)

        A = lambda n, sh, dt: es.enter_context(nc.sbuf_tensor(n, sh, dt))

        ones_f32 = A("ones_f32", [128, 128], F32)
        ones_f16 = A("ones_f16", [128, 128], F16)
        ident_bf = A("ident_bf", [128, 128], BF16)
        ident_f32 = A("ident_f32", [128, 128], F32)
        tri_f16 = A("tri_f16", [128, 128], F16)
        su_f16 = A("su_f16", [128, 128], F16)
        tri_f32 = A("tri_f32", [128, 128], F32)
        e2_f16 = A("e2_f16", [64, DS], F16)
        e512_bf = A("e512_bf", [64, 512], BF16)
        dbias_bc = A("dbias_bc", [128, NH], F32)
        nega_bc = A("nega_bc", [128, NH], F32)
        dcol_bc = A("dcol_bc", [128, NH], F32)
        convwF = A("convwF", [128, 34 * 4], F32)
        iota_f = A("iota_f", [128, 128], F32)
        iota_c = A("iota_c", [128, 1], F32)
        smalls = A("smalls", [128, 32], F32)
        sm64 = A("sm64", [64, 16], F32)
        dtT = A("dtT", [128, NB * NH], F32)
        h_bf = A("h_bf", [128, DS], BF16)
        tailsave = A("tailsave", [128, 34 * 3], BF16)
        bmcf = A("bmcf", [128, 2 * TT], BF16)
        wbuf = [A(f"wbuf{i}", [128, 8192], BF16) for i in range(2)]
        bnb = [A(f"bnb{i}", [128, 24], F32) for i in range(2)]

        pa = [es.enter_context(nc.psum_tensor(f"pa{i}", [128, 512], F32)) for i in range(4)]
        pdm = es.enter_context(nc.psum_tensor("pdm", [128, 1024], F32))
        pb = [es.enter_context(nc.psum_tensor(f"pb{i}", [128, 1024], BF16)) for i in range(2)]

        with contextlib.ExitStack() as sp:
            e2scr = sp.enter_context(nc.sbuf_tensor("e2scr", [64, DS], F32))

            def setup_sync(s):
                C.dma(s.dma_start(convwF[:].rearrange("p (c k) -> p c k", k=4),
                                  conv_w.rearrange("(c p) k -> p c k", p=128)), "dx")
                C.dma(s.dma_start(dbias_bc[:], bcast_ap(dt_bias, 128, NH)), "dx")
                C.dma(s.dma_start(nega_bc[:], bcast_ap(a_log, 128, NH)), "dx")
                C.dma(s.dma_start(dcol_bc[:], bcast_ap(d_vec, 128, NH)), "dx")
            sec("sync", setup_sync)

            def setup_gp(g):
                g.iota(iota_f[:], [[1, 128]], base=0, channel_multiplier=0,
                       allow_small_or_imprecise_dtypes=True)
                g.iota(iota_c[:], [[0, 1]], base=0, channel_multiplier=1,
                       allow_small_or_imprecise_dtypes=True)
                C.inc(g.iota(e2scr[:, :].rearrange("p (a b) -> p a b", b=64),
                             [[1, 64], [0, 64]], base=0, channel_multiplier=0,
                             allow_small_or_imprecise_dtypes=True), "gE")
            sec("gpsimd", setup_gp)

            def setup_v(v):
                C.wait(v, "gE")
                C.wait(v, "dx")
                v.memset(ones_f32[:], 1.0)
                v.memset(ones_f16[:], 1.0)
                v.memset(h_bf[:], 0.0)
                v.memset(tailsave[:], 0.0)
                v.tensor_scalar(ident_f32[:], iota_f[:], iota_c[:], None, OP.is_equal)
                v.tensor_copy(ident_bf[:], ident_f32[:])
                v.tensor_scalar(tri_f32[:], iota_f[:], iota_c[:], None, OP.is_ge)
                v.tensor_copy(tri_f16[:], tri_f32[:])
                v.tensor_scalar(su_f16[:], iota_f[:], iota_c[:], None, OP.is_lt)
                v.tensor_scalar(e2scr[:, :], e2scr[:, :], iota_c[0:64, :],
                                None, OP.is_equal)
                C.inc(v.tensor_copy(e2_f16[:], e2scr[:, :]), "vE")
            sec("vector", setup_v)
            v_e2 = C.cnt["vE"]

            def setup_gp2(g):
                C.wait(g, "vE", v_e2)
                C.inc(g.iota(e2scr[:, 0:512].rearrange("p (a b) -> p a b", b=8),
                             [[1, 64], [0, 8]], base=0, channel_multiplier=0,
                             allow_small_or_imprecise_dtypes=True), "gE")
            sec("gpsimd", setup_gp2)

            def setup_v2(v):
                C.wait(v, "gE")
                v.tensor_scalar(e2scr[:, 0:512], e2scr[:, 0:512], iota_c[0:64, :],
                                None, OP.is_equal)
                C.inc(v.tensor_copy(e512_bf[:], e2scr[:, 0:512]), "vE")
            sec("vector", setup_v2)
            v_e5 = C.cnt["vE"]

            def setup_s(s):
                C.wait(s, "vE", v_e5)
                C.inc(s.activation(nega_bc[:], nega_bc[:], AF.Exp), "sE")
            sec("scalar", setup_s)

            def setup_v3(v):
                C.wait(v, "sE")
                C.inc(v.tensor_scalar(nega_bc[:], nega_bc[:], -1.0, None, OP.mult),
                      "vE")
            sec("vector", setup_v3)

            barrier()

        for tile in range(NT):
            tok0 = tile * TT

            with contextlib.ExitStack() as ph:
                Aph = lambda n, sh, dt: ph.enter_context(
                    nc.sbuf_tensor(f"{n}_t{tile}", sh, dt))
                xhatF = Aph("xhatF", [128, 32 * TT], BF16)

                with contextlib.ExitStack() as p0s:
                    A0 = lambda n, sh, dt: p0s.enter_context(
                        nc.sbuf_tensor(f"{n}_t{tile}", sh, dt))
                    lnw_bc = A0("lnw_bc", [128, H], F16)
                    xblk = [A0(f"xblk{i}", [128, H], F32) for i in range(2)]
                    xhat_tok = A0("xhat_tok", [128, H], BF16)
                    with nc.named_scope(f"P0_t{tile}"):
                        _p0(nc, C, sec, tok0, x_in, ln_w, lnw_bc, xblk, xhat_tok,
                            xhatF, bnb, smalls, ident_bf, pa, pb)
                        barrier()

                with contextlib.ExitStack() as a1s:
                    A1 = lambda n, sh, dt: a1s.enter_context(
                        nc.sbuf_tensor(f"{n}_t{tile}", sh, dt))
                    mup_bc = A1("mup_bc", [128, PROJ], F16)
                    xbct = [A1(f"xbct{i}", [128, EXT], BF16) for i in range(8)]
                    ztmp = A1("ztmp", [128, 512], F32)
                    bst = A1("bst", [128, 512], BF16)
                    accb = A1("accb", [128, TT], F32)
                    zstg = [A1(f"zstg{i}", [128, 512], BF16) for i in range(2)]
                    xstg = [A1(f"xstg{i}", [128, TT], BF16) for i in range(2)]

                    def mup_load(g):
                        C.dma(g.dma_start(mup_bc[:], bcast_ap(mup, 128, PROJ)), "dw")
                    sec("gpsimd", mup_load)

                    with nc.named_scope(f"A1_t{tile}"):
                        _a1(nc, C, sec, tok0, w_in, xhatF, wbuf, mup_bc, ztmp, bst,
                            xbct, accb, convwF, tailsave, bmcf, dtT, xstg, xsspill,
                            ident_bf, pa, pb)
                        barrier()
                    with nc.named_scope(f"A2_t{tile}"):
                        _a2(nc, C, sec, tok0, w_in, xhatF, wbuf, mup_bc, ztmp,
                            zstg, zspill, pa)
                        barrier()

            with contextlib.ExitStack() as ph:
                Aph = lambda n, sh, dt: ph.enter_context(
                    nc.sbuf_tensor(f"{n}_t{tile}", sh, dt))
                yhatF = Aph("yhatF", [128, 32 * TT], BF16)
                nw_bc = Aph("nw_bc", [128, DS], F16)
                ysb = Aph("ysb", [128, DS], BF16)
                sc = {n: Aph(n, sh, dt) for n, sh, dt in [
                    ("xsT", [128, DS], BF16),
                    ("yhat_tok", [128, DS], BF16),
                    ("vexp", [128, 512], F32),
                    ("ytmp", [128, 512], BF16),
                    ("dxst", [128, 512], BF16),
                    ("yA", [128, 512], BF16),
                    ("xtld", [128, 512], BF16),
                    ("hdec", [128, 512], BF16),
                    ("dtc", [128, NH], F32),
                    ("dta32", [128, NH], F32),
                    ("dta16", [128, NH], F16),
                    ("srow", [1, 64], F32),
                    ("sbc", [128, 64], F32),
                    ("t64a", [128, NH], F32),
                    ("t64b", [128, NH], F32),
                    ("t64c", [128, NH], F32),
                ]}
                sc["cbtm"] = [Aph(f"cbtm{i}", [128, 128], F32) for i in range(2)]
                sc["bmt"] = [Aph(f"bmt{i}", [128, 128], BF16) for i in range(2)]
                sc["vf16"] = [Aph(f"vf16{i}", [64, 128], F16) for i in range(2)]
                sc["expd"] = [Aph(f"expd{i}", [128, 1024], F16) for i in range(2)]
                sc["rhsd"] = [Aph(f"rhsd{i}", [128, 512], F16) for i in range(2)]
                sc["mtil"] = [Aph(f"mtil{i}", [128, 1024], BF16) for i in range(2)]
                sc["xsTd"] = [Aph(f"xsTd{i}", [128, 512], BF16) for i in range(2)]
                x1stg = [Aph(f"x1stg{i}", [128, 512], BF16) for i in range(2)]

                def nw_load(g):
                    C.dma(g.dma_start(nw_bc[:], bcast_ap(norm_w, 128, DS)), "dw")
                sec("gpsimd", nw_load)

                with nc.named_scope(f"SCAN_t{tile}"):
                    C.merge_scalar = False
                    X = {"xrd_free": {0: 0, 1: 0}, "zrd_free": {0: 0, 1: 0},
                         "st_last": 0, "scum": 0, "tcu_prev": 0, "vyx_last": 0,
                         "front_v": 0, "vt2": 0, "ytr": 0, "base_g": 0,
                         "dmm": {}, "expfree": {}, "mfree": {}, "dxs": {},
                         "st": {}, "hev": {}}
                    fr = lambda ch: _scan_front(
                        nc, C, sec, tok0, ch, sc, wbuf, xsspill, zspill, dtT,
                        dbias_bc, nega_bc, bmcf, ones_f16, ones_f32, tri_f16,
                        su_f16, tri_f32, ident_bf, ident_f32, sm64, pa, pb, X)
                    fr(0)
                    for ch in range(NB):
                        _scan_groups(nc, C, sec, tok0, ch, sc, wbuf, dcol_bc,
                                     h_bf, bmcf, ysb, e2_f16, bnb, tri_f16,
                                     su_f16, ident_bf, pa, pdm, pb, X)
                        if ch + 1 < NB:
                            fr(ch + 1)
                        _scan_tail(nc, C, sec, tok0, ch, sc, nw_bc, ysb, yhatF,
                                   ident_bf, smalls, sm64, bnb, pa, pb, X)
                    barrier()
                    C.merge_scalar = True
                with nc.named_scope(f"E_t{tile}"):
                    _e_outproj(nc, C, sec, tok0, w_out, yhatF, wbuf, x1stg,
                               x1spill, pa)
                    barrier()

            with contextlib.ExitStack() as ph:
                Aph = lambda n, sh, dt: ph.enter_context(
                    nc.sbuf_tensor(f"{n}_t{tile}", sh, dt))
                awT = Aph("awT", [128, 32 * 512], BF16)
                bwT = Aph("bwT", [128, 4 * H], BF16)
                mT = Aph("mT", [128, 32 * 64], BF16)
                keysT = Aph("keysT", [16, 64], BF16)
                x2F = Aph("x2F", [128, 32 * 128], BF16)
                x2tok = [Aph(f"x2tok{i}", [128, H], BF16) for i in range(2)]
                bnb2 = [Aph(f"bnb2_{i}", [128, 24], F32) for i in range(2)]
                x3f = Aph("x3f", [128, H], F32)
                xblk2 = Aph("xblk2", [128, H], F32)
                ysb2 = Aph("ysb2", [128, H], F32)
                fscr = Aph("fscr", [128, 512], F32)
                fscr2 = Aph("fscr2", [128, 64], F32)
                gq = Aph("gq", [128, 64], F32)
                gwork = Aph("gwork", [128, 64], F32)
                zhat = Aph("zhat", [128, 512], BF16)
                zhT = Aph("zhT", [128, 512], BF16)
                rcols = [Aph(f"rcol{i}", [128, 1], F32) for i in range(4)]

                with nc.named_scope(f"FP_t{tile}"):
                    _f_prep(nc, C, sec, ffn_a, ffn_b, ffn_wq, ffn_keys, wbuf,
                            awT, bwT, mT, keysT, ident_bf, pa, pb)
                    barrier()
                do_at = {}
                Xf = {"aq": {}, "x2Fev": {}, "x2tr": {}, "s1": {}, "vln": {},
                      "tb": {}}
                with nc.named_scope(f"F_t{tile}"):
                    fx = lambda b: _f_x(nc, C, sec, tok0, b, x_in, x1spill,
                                        xblk2, ysb2, x2F, x2tok, awT, mT, bnb,
                                        smalls, ident_bf, pa, pb, Xf)
                    fy = lambda b: _f_y(nc, C, sec, tok0, b, out_t, x3f, x2tok,
                                        bwT, fscr, fscr2, gq, gwork, wbuf[1], zhat,
                                        zhT, bnb2, smalls, rcols, ident_bf, pa,
                                        pdm, pb, do_at, Xf)
                    fx(0)
                    for b in range(NB):
                        if b + 1 < NB:
                            fx(b + 1)
                        fy(b)
                    barrier()

        def fin(s):
            C.wait(s, "do")
            C.wait(s, "do2")
            C.wait(s, "doa")
            C.wait(s, "dob")
        sec("sync", fin)
        flush()

    return nc



def _ln_stats(C, v, bnb, src_ap, ncols, mu_col, m2_col, sm, c0, want_meansq):
    nst = ncols // 512
    for i in range(nst):
        v.bn_stats(bnb[i % 2][:, (i // 2) * 6:(i // 2) * 6 + 6],
                   src_ap[:, i * 512:(i + 1) * 512])
    v.bn_aggr(sm[:, c0:c0 + 2], bnb[0][:, 0:(nst // 2) * 6])
    v.bn_aggr(sm[:, c0 + 2:c0 + 4], bnb[1][:, 0:(nst // 2) * 6])
    mA, vA = sm[:, c0:c0 + 1], sm[:, c0 + 1:c0 + 2]
    mB, vB = sm[:, c0 + 2:c0 + 3], sm[:, c0 + 3:c0 + 4]
    t0, t1 = sm[:, c0 + 4:c0 + 5], sm[:, c0 + 5:c0 + 6]
    v.tensor_tensor(mu_col, mA, mB, OP.add)
    v.tensor_scalar(mu_col, mu_col, 0.5, None, OP.mult)
    v.tensor_tensor(t0, mA, mB, OP.subtract)
    v.tensor_tensor(t0, t0, t0, OP.mult)
    v.tensor_tensor(t1, vA, vB, OP.add)
    v.tensor_scalar(t1, t1, 0.5, None, OP.mult)
    v.scalar_tensor_tensor(m2_col, t0, 0.25, t1, OP.mult, OP.add)
    if want_meansq:
        v.tensor_tensor(t0, mu_col, mu_col, OP.mult)
        v.tensor_tensor(m2_col, m2_col, t0, OP.add)
    v.tensor_scalar(m2_col, m2_col, EPS, None, OP.add)


def _rstd_finish(s, src_col, dst_col, tmp_col):
    s.activation(tmp_col, src_col, AF.Ln)
    return s.activation(dst_col, tmp_col, AF.Exp, scale=-0.5)


def _p0(nc, C, sec, tok0, x_in, ln_w, lnw_bc, xblk, xhat_tok, xhatF, bnb,
        smalls, ident_bf, pa, pb):
    dx_at = {}
    v_stt = {}
    t_done = {}

    def g_lnw(g):
        C.dma(g.dma_start(lnw_bc[:], bcast_ap(ln_w, 128, H)), "dw")
    sec("gpsimd", g_lnw)
    dw_lnw = C.cnt["dw"]

    def d_pre(s):
        for b in range(2):
            dx_at[b] = C.dma(s.dma_start(
                xblk[b % 2][:], x_in[tok0 + b * 128:tok0 + (b + 1) * 128, :]),
                "dxa" if b % 2 == 0 else "dxb")
    sec("sync", d_pre)

    for b in range(NB):
        def vf(v, b=b, dxf=None):
            C.wait(v, "dxa" if b % 2 == 0 else "dxb")
            if b == 0:
                C.wait(v, "dw", dw_lnw)
            _ln_stats(C, v, bnb, xblk[b % 2][:], H, smalls[:, 8:9], smalls[:, 9:10],
                      smalls, 12, want_meansq=True)
            C.inc(v.tensor_copy(smalls[:, 18:19], smalls[:, 9:10]), "vE")
        sec("vector", vf)
        v_bn = C.cnt["vE"]

        def sf(s, b=b):
            C.wait(s, "vE", v_bn)
            C.inc(_rstd_finish(s, smalls[:, 18:19], smalls[:, 10:11],
                               smalls[:, 11:12]), "sE")
        sec("scalar", sf)
        s_rs = C.cnt["sE"]

        def vg(v, b=b):
            C.wait(v, "sE", s_rs)
            if b >= 1:
                C.wait(v, "tE", t_done[b - 1])
            C.inc(v.scalar_tensor_tensor(xhat_tok[:], xblk[b % 2][:],
                                         smalls[:, 10:11], lnw_bc[:],
                                         OP.mult, OP.mult), "vE")
        sec("vector", vg)
        v_stt[b] = C.cnt["vE"]

        if b + 2 < NB:
            def df(s, b=b):
                C.wait(s, "vE", v_stt[b])
                dx_at[b + 2] = C.dma(s.dma_start(
                    xblk[b % 2][:],
                    x_in[tok0 + (b + 2) * 128:tok0 + (b + 3) * 128, :]),
                    "dxa" if b % 2 == 0 else "dxb")
            sec("sync", df)

        for jb in range(4):
            def tf(t, b=b, jb=jb):
                if jb == 0:
                    C.wait(t, "vE", v_stt[b])
                key = f"pb{jb % 2}"
                C.bank_guard(t, key)
                for i in range(8):
                    c = jb * 8 + i
                    mm = t.transpose(pb[jb % 2][:, i * 128:(i + 1) * 128],
                                     xhat_tok[:, c * 128:(c + 1) * 128], ident_bf[:])
                C.inc(mm, "tE")
            sec("tensor", tf)
            t_tr = C.cnt["tE"]

            def sg(v, b=b, jb=jb, t_tr=t_tr):
                v.wait_ge(C.sems["tE"], t_tr)
                dstap = xhatF[:, jb * 8 * TT:(jb * 8 + 8) * TT] \
                    .rearrange("p (c t) -> p c t", t=TT)[:, :, b * 128:(b + 1) * 128]
                C.inc(v.tensor_copy(
                    dstap, pb[jb % 2][:, 0:1024].rearrange("p (c t) -> p c t", t=128)),
                    "vE")
                C.bank_mark(f"pb{jb % 2}", "vE")
            sec("vector", sg)
        t_done[b] = C.cnt["tE"]


def _a1(nc, C, sec, tok0, w_in, xhatF, wbuf, mup_bc, ztmp, bst, xbct, accb,
        convwF, tailsave, bmcf, dtT, xstg, xsspill, ident_bf, pa, pb):
    n_slab = 18
    dw_at = {}
    mm_done = {}
    ev_done = {}
    tg_at = {}
    conv_tap = {}
    spill_at = {}
    s_silu_at = {}

    def pre(g):
        def f(gp, g=g):
            col0 = 4096 + g * 256
            w = 64 if g == 17 else 256
            if g >= 2:
                gp.wait_ge(C.sems["tE"], mm_done[g - 2])
            dw_at[g] = C.dma(gp.dma_start(
                wbuf[g % 2][:, 0:32 * w].rearrange("p (c f) -> p c f", f=w),
                w_in.rearrange("(c p) f -> p c f", p=128)[:, :, col0:col0 + w]),
                "dwa" if g % 2 == 0 else "dwb")
        sec("gpsimd", f)

    pre(0)
    pre(1)

    for g in range(n_slab):
        col0 = 4096 + g * 256
        w = 64 if g == 17 else 256

        if g < 16:
            def gtail(gp, g=g):
                if 2 * g - 8 >= 0:
                    gp.wait_ge(C.sems["vE"], conv_tap[2 * g + 1 - 8])
                cc0 = 2 * g
                gp.tensor_copy(xbct[cc0 % 8][:, 0:3],
                               tailsave[:, cc0 * 3:cc0 * 3 + 3])
                C.inc(gp.tensor_copy(xbct[(cc0 + 1) % 8][:, 0:3],
                                     tailsave[:, (cc0 + 1) * 3:(cc0 + 1) * 3 + 3]),
                      "gE")
            sec("gpsimd", gtail)
        g_tail_in = C.cnt["gE"]

        for b in range(NB):
            idx = g * NB + b

            def tf(t, g=g, b=b, w=w, idx=idx):
                if b == 0:
                    C.wait(t, "dwa" if g % 2 == 0 else "dwb")
                C.bank_guard(t, f"pa{idx % 3}")
                for c in range(32):
                    mm = t.matmul(pa[idx % 3][:, 0:w],
                                  xhatF[:, c * TT + b * 128:c * TT + (b + 1) * 128],
                                  wbuf[g % 2][:, c * w:(c + 1) * w],
                                  start=(c == 0), stop=(c == 31))
                C.inc(mm, "tE")
                mm_done[g] = C.cnt["tE"]
            sec("tensor", tf)
            t_mm = C.cnt["tE"]

            def vf(v, g=g, b=b, w=w, col0=col0, idx=idx, t_mm=t_mm):
                C.wait(v, "tE", t_mm)
                if idx == 0:
                    C.wait(v, "dw")
                if g < 17 and idx - 1 in tg_at:
                    C.wait(v, "tE", tg_at[idx - 1])
                if g == 17:
                    C.inc(v.tensor_tensor(dtT[:, b * 64:(b + 1) * 64],
                                          pa[idx % 3][:, 0:64],
                                          mup_bc[:, col0:col0 + 64], OP.mult), "vE")
                else:
                    C.inc(v.tensor_tensor(bst[:, 0:w], pa[idx % 3][:, 0:w],
                                          mup_bc[:, col0:col0 + w], OP.mult), "vE")
                C.bank_mark(f"pa{idx % 3}", "vE")
                ev_done[idx] = C.cnt["vE"]
            sec("vector", vf)
            v_ev = C.cnt["vE"]

            if g < 17:
                def tg(t, g=g, b=b, idx=idx, v_ev=v_ev):
                    C.wait(t, "vE", v_ev)
                    key = f"pb{idx % 2}"
                    C.bank_guard(t, key)
                    for i in range(2):
                        mm = t.transpose(pb[idx % 2][:, i * 128:(i + 1) * 128],
                                         bst[:, i * 128:(i + 1) * 128], ident_bf[:])
                    C.inc(mm, "tE")
                    tg_at[idx] = C.cnt["tE"]
                sec("tensor", tg)
                t_tr = C.cnt["tE"]

                def sg(v, g=g, b=b, idx=idx, t_tr=t_tr):
                    v.wait_ge(C.sems["tE"], t_tr)
                    last = None
                    for i in range(2):
                        cc = 2 * g + i
                        if cc < 32:
                            dst = xbct[cc % 8][:, 3 + b * 128:3 + (b + 1) * 128]
                        else:
                            dst = bmcf[:, (cc - 32) * TT + b * 128:
                                       (cc - 32) * TT + (b + 1) * 128]
                        last = v.tensor_copy(dst, pb[idx % 2][:, i * 128:(i + 1) * 128])
                    C.inc(last, "vE")
                    C.bank_mark(f"pb{idx % 2}", "vE")
                sec("vector", sg)

        if g + 2 < n_slab:
            pre(g + 2)

        if g < 16:
            for i in range(2):
                cc = 2 * g + i
                _conv_chunk(nc, C, sec, tok0, cc, xbct[cc % 8], accb, convwF,
                            tailsave, xstg, xsspill, bmcf, conv_tap, spill_at)
        elif g == 16:
            for i in range(2):
                cc = 32 + i
                _conv_chunk(nc, C, sec, tok0, cc, None, accb, convwF,
                            tailsave, xstg, xsspill, bmcf, conv_tap, spill_at)


def _conv_chunk(nc, C, sec, tok0, cc, buf, accb, convwF, tailsave, xstg,
                xsspill, bmcf, conv_tap, spill_at):
    if cc >= 32:
        data = bmcf[:, (cc - 32) * TT:(cc - 32 + 1) * TT]
    else:
        data = buf[:, 3:3 + TT]
    sE_now = C.cnt["sE"]
    vE_now = C.cnt["vE"]
    gE_now = C.cnt["gE"]

    def vf(v, cc=cc, data=data, sE_now=sE_now, vE_now=vE_now, gE_now=gE_now):
        C.wait(v, "sE", sE_now)
        C.wait(v, "vE", vE_now)
        C.wait(v, "gE", gE_now)
        w = lambda k: convwF[:, cc * 4 + k:cc * 4 + k + 1]
        if cc < 32:
            ext = buf
            v.tensor_scalar(accb[:], ext[:, 0:TT], w(0), None, OP.mult)
            for k in range(1, 3):
                v.scalar_tensor_tensor(accb[:], ext[:, k:k + TT], w(k), accb[:],
                                       OP.mult, OP.add)
            last = v.scalar_tensor_tensor(accb[:], ext[:, 3:3 + TT], w(3), accb[:],
                                          OP.mult, OP.add)
        else:
            ts3 = tailsave[:, cc * 3:cc * 3 + 3]
            v.tensor_scalar(accb[:], data[:], w(3), None, OP.mult)
            for k in range(3):
                sh = 3 - k
                v.scalar_tensor_tensor(accb[:, sh:TT], data[:, 0:TT - sh], w(k),
                                       accb[:, sh:TT], OP.mult, OP.add)
                last = v.scalar_tensor_tensor(accb[:, 0:sh], ts3[:, 3 - sh:3], w(k),
                                              accb[:, 0:sh], OP.mult, OP.add)
        C.inc(last, "vE")
        conv_tap[cc] = C.cnt["vE"]
    sec("vector", vf)
    v_tap = C.cnt["vE"]

    def gp2(g, cc=cc, data=data, v_tap=v_tap):
        C.wait(g, "vE", v_tap)
        C.inc(g.tensor_copy(tailsave[:, cc * 3:cc * 3 + 3], data[:, TT - 3:TT]), "gE")
    sec("gpsimd", gp2)
    g_tail = C.cnt["gE"]

    def sf(s, cc=cc, data=data, v_tap=v_tap, g_tail=g_tail):
        C.wait(s, "vE", v_tap)
        C.wait(s, "gE", g_tail)
        if cc < 32:
            if cc >= 2:
                C.wait(s, "doa" if cc % 2 == 0 else "dob")
            dst = xstg[cc % 2][:]
        else:
            dst = data
        C.inc(s.activation(dst, accb[:], AF.Silu), "sE")
    sec("scalar", sf)
    s_silu = C.cnt["sE"]

    if cc < 32:
        def df(sy, cc=cc, s_silu=s_silu):
            C.wait(sy, "sE", s_silu)
            spill_at[cc] = C.dma(sy.dma_start(
                xsspill[cc * 128:(cc + 1) * 128, tok0:tok0 + TT], xstg[cc % 2][:]),
                "doa" if cc % 2 == 0 else "dob")
        sec("sync", df)


def _a2(nc, C, sec, tok0, w_in, xhatF, wbuf, mup_bc, ztmp, zstg, zspill, pa):
    dw_at = {}
    mm_done = {}
    do_at = {}
    sz_at = {}

    def pre(g):
        def f(gp, g=g):
            col0 = g * 256
            if g >= 2:
                gp.wait_ge(C.sems["tE"], mm_done[g - 2])
            dw_at[g] = C.dma(gp.dma_start(
                wbuf[g % 2][:, 0:32 * 256].rearrange("p (c f) -> p c f", f=256),
                w_in.rearrange("(c p) f -> p c f", p=128)[:, :, col0:col0 + 256]),
                "dwa" if g % 2 == 0 else "dwb")
        sec("gpsimd", f)

    pre(0)
    pre(1)
    for g in range(16):
        col0 = g * 256
        for b in range(NB):
            idx = g * NB + b

            def tf(t, g=g, b=b, idx=idx):
                if b == 0:
                    C.wait(t, "dwa" if g % 2 == 0 else "dwb")
                C.bank_guard(t, f"pa{idx % 3}")
                for c in range(32):
                    mm = t.matmul(pa[idx % 3][:, 0:256],
                                  xhatF[:, c * TT + b * 128:c * TT + (b + 1) * 128],
                                  wbuf[g % 2][:, c * 256:(c + 1) * 256],
                                  start=(c == 0), stop=(c == 31))
                C.inc(mm, "tE")
                mm_done[g] = C.cnt["tE"]
            sec("tensor", tf)
            t_mm = C.cnt["tE"]

            def vf(v, col0=col0, idx=idx, t_mm=t_mm):
                C.wait(v, "tE", t_mm)
                if idx == 0:
                    C.wait(v, "dw")
                if idx - 1 in sz_at:
                    C.wait(v, "sE", sz_at[idx - 1])
                C.inc(v.tensor_tensor(ztmp[:, 0:256], pa[idx % 3][:, 0:256],
                                      mup_bc[:, col0:col0 + 256], OP.mult), "vE")
                C.bank_mark(f"pa{idx % 3}", "vE")
            sec("vector", vf)
            v_ev = C.cnt["vE"]

            def sf(s, idx=idx, v_ev=v_ev):
                C.wait(s, "vE", v_ev)
                if idx >= 2:
                    C.wait(s, "doa" if idx % 2 == 0 else "dob")
                C.inc(s.activation(zstg[idx % 2][:, 0:256], ztmp[:, 0:256],
                                   AF.Silu), "sE")
                sz_at[idx] = C.cnt["sE"]
            sec("scalar", sf)
            s_z = C.cnt["sE"]

            def df(sy, b=b, col0=col0, idx=idx, s_z=s_z):
                C.wait(sy, "sE", s_z)
                do_at[idx] = C.dma(sy.dma_start(
                    zspill[tok0 + b * 128:tok0 + (b + 1) * 128, col0:col0 + 256],
                    zstg[idx % 2][:, 0:256]), "doa" if idx % 2 == 0 else "dob")
            sec("sync", df)
        if g + 2 < 16:
            pre(g + 2)


def _bc(ap, dims):
    return bass.AP(ap.tensor, ap.offset, dims)


def _scan_front(nc, C, sec, tok0, ch, sc, wbuf, xsspill, zspill, dtT,
                dbias_bc, nega_bc, bmcf, ones_f16, ones_f32, tri_f16,
                su_f16, tri_f32, ident_bf, ident_f32, sm64, pa, pb, X):
    t0c = tok0 + ch * 128
    par = ch % 2
    xrd = wbuf[0][:, par * 4096:par * 4096 + 4096]
    zrd = wbuf[1][:, par * 4096:par * 4096 + 4096]

    def d0(sy, par=par):
        if X["xrd_free"][par]:
            sy.wait_ge(C.sems["tE"], X["xrd_free"][par])
        if X["zrd_free"][par]:
            sy.wait_ge(C.sems["vE"], X["zrd_free"][par])
        dsem = "dxa" if par == 0 else "dxb"
        C.dma(sy.dma_start(
            xrd.rearrange("p (c t) -> p c t", t=128),
            xsspill.rearrange("(c p) t -> p c t", p=128)[:, :, t0c:t0c + 128]), dsem)
        C.dma(sy.dma_start(zrd, zspill[t0c:t0c + 128, :]), dsem)
    sec("sync", d0)
    dsemn = "dxa" if par == 0 else "dxb"
    dxv = C.cnt[dsemn]

    for jb in range(4):
        def tf(t, jb=jb, dxv=dxv, dsemn=dsemn):
            if jb == 0:
                C.wait(t, dsemn, dxv)
            key = f"pb{jb % 2}"
            C.bank_guard(t, key)
            for i in range(8):
                c = jb * 8 + i
                mm = t.transpose(pb[jb % 2][:, i * 128:(i + 1) * 128],
                                 xrd[:, c * 128:(c + 1) * 128], ident_bf[:])
            C.inc(mm, "tE")
        sec("tensor", tf)
        t_tr = C.cnt["tE"]

        def sg(v, jb=jb, t_tr=t_tr):
            v.wait_ge(C.sems["tE"], t_tr)
            C.inc(v.tensor_copy(sc["xsT"][:, jb * 1024:(jb + 1) * 1024],
                                pb[jb % 2][:, 0:1024]), "vE")
            C.bank_mark(f"pb{jb % 2}", "vE")
        sec("vector", sg)
    X["xrd_free"][par] = C.cnt["tE"]

    def v_dt(v):
        dts = dtT[:, ch * 64:(ch + 1) * 64]
        if X["scum"]:
            C.wait(v, "sE", X["scum"])
        v.tensor_tensor(sc["t64a"][:], dts, dbias_bc[:], OP.add)
        C.inc(v.scalar_tensor_tensor(sc["t64b"][:], sc["t64a"][:], -1.0,
                                     sc["t64a"][:], OP.mult, OP.min), "vE")
    sec("vector", v_dt)
    v_1 = C.cnt["vE"]

    def s_dt(s, v_1=v_1):
        C.wait(s, "vE", v_1)
        s.activation(sc["t64c"][:], sc["t64b"][:], AF.Exp)
        C.inc(s.activation(sc["t64b"][:], sc["t64c"][:], AF.Ln, bias=1.0), "sE")
    sec("scalar", s_dt)
    s_1 = C.cnt["sE"]

    def v_dt2(v, s_1=s_1):
        C.wait(v, "sE", s_1)
        if X["tcu_prev"]:
            v.wait_ge(C.sems["tE"], X["tcu_prev"])
        v.tensor_scalar(sc["t64c"][:], sc["t64a"][:], 0.0, None, OP.max)
        v.tensor_tensor(sc["dtc"][:], sc["t64b"][:], sc["t64c"][:], OP.add)
        v.tensor_tensor(sc["dta32"][:], sc["dtc"][:], nega_bc[:], OP.mult)
        C.inc(v.tensor_copy(sc["dta16"][:], sc["dta32"][:]), "vE")
    sec("vector", v_dt2)
    v_2 = C.cnt["vE"]

    def t_cum(t, v_2=v_2):
        C.wait(t, "vE", v_2)
        C.bank_guard(t, "pa3")
        t.matmul(pa[3][0:64, 0:128], sc["dta16"][:], tri_f16[:], start=True,
                 stop=True)
        t.matmul(pa[3][:, 128:192], su_f16[:], sc["dta16"][:], start=True,
                 stop=True)
        C.inc(t.matmul(pa[3][0:1, 192:256], ones_f16[:, 0:1], sc["dta16"][:],
                       start=True, stop=True), "tE")
    sec("tensor", t_cum)
    t_cu = C.cnt["tE"]
    X["tcu_prev"] = t_cu

    def s_cum(s, t_cu=t_cu, par=par):
        C.wait(s, "tE", t_cu)
        if X["vyx_last"]:
            C.wait(s, "vE", X["vyx_last"])
        s.activation(sc["vf16"][par][:], pa[3][0:64, 0:128], AF.Exp)
        C.inc(s.activation(sc["t64a"][:], pa[3][:, 128:192], AF.Exp), "sE")
        C.bank_mark("pa3", "sE")
    sec("scalar", s_cum)
    s_2 = C.cnt["sE"]
    X["scum"] = s_2

    def v_sb(v, t_cu=t_cu):
        v.wait_ge(C.sems["tE"], t_cu)
        C.inc(v.tensor_copy(sc["srow"][:], pa[3][0:1, 192:256]), "vE")
    sec("vector", v_sb)
    v_sr = C.cnt["vE"]

    def t_sb2(t, v_sr=v_sr):
        C.wait(t, "vE", v_sr)
        C.bank_guard(t, "pa3")
        C.inc(t.matmul(pa[3][:, 448:512], ones_f32[0:1, :], sc["srow"][:],
                       start=True, stop=True), "tE")
    sec("tensor", t_sb2)
    t_sb2a = C.cnt["tE"]

    def s_sb(s, t_sb2a=t_sb2a):
        s.wait_ge(C.sems["tE"], t_sb2a)
        C.inc(s.activation(sc["sbc"][:], pa[3][:, 448:512], AF.Exp), "sE")
        C.bank_mark("pa3", "sE")
    sec("scalar", s_sb)

    def t_cb(t, par=par):
        C.wait(t, "vE")
        C.bank_guard(t, "pa3")
        t.matmul(pa[3][:, 256:384], bmcf[:, ch * 128:(ch + 1) * 128],
                 bmcf[:, TT + ch * 128:TT + (ch + 1) * 128], start=True, stop=True)
        C.bank_guard(t, "pb0")
        C.inc(t.transpose(pb[0][:, 0:128], bmcf[:, ch * 128:(ch + 1) * 128],
                          ident_bf[:]), "tE")
    sec("tensor", t_cb)
    t_cb_at = C.cnt["tE"]

    def v_cb(v, t_cb_at=t_cb_at, par=par):
        v.wait_ge(C.sems["tE"], t_cb_at)
        v.tensor_tensor(sc["cbtm"][par][:], pa[3][:, 256:384], tri_f32[:], OP.mult)
        C.bank_mark("pa3", "vE")
        C.inc(v.tensor_copy(sc["bmt"][par][:], pb[0][:, 0:128]), "vE")
        C.bank_mark("pb0", "vE")
    sec("vector", v_cb)
    X["front_v"] = C.cnt["vE"]


def _scan_groups(nc, C, sec, tok0, ch, sc, wbuf, dcol_bc, h_bf, bmcf, ysb,
                 e2_f16, bnb, tri_f16, su_f16, ident_bf, pa, pdm, pb, X):
    par = ch % 2
    zrd = wbuf[1][:, par * 4096:par * 4096 + 4096]
    front_v = X["front_v"]

    for g in range(8):
        gg = X["base_g"] + g
        heads = list(range(8 * g, 8 * g + 8))

        for half in range(2):
            hh = 2 * gg + half
            h0 = 8 * g + half * 4

            def v_r(v, hh=hh, h0=h0, front_v=front_v):
                C.wait(v, "vE", front_v)
                if X["dmm"].get(hh - 2):
                    v.wait_ge(C.sems["tE"], X["dmm"][hh - 2])
                C.inc(v.tensor_tensor(
                    sc["rhsd"][hh % 2][:].rearrange("p (a b) -> p a b", b=128),
                    _bc(tri_f16[:], [[128, 128], [0, 4], [1, 128]]),
                    _bc(sc["dta32"][:, h0:h0 + 4], [[64, 128], [1, 4], [0, 128]]),
                    OP.mult), "vE")
            sec("vector", v_r)
            v_rh = C.cnt["vE"]

            def t_d(t, hh=hh, v_rh=v_rh):
                C.wait(t, "vE", v_rh)
                key = f"pd{hh % 2}"
                C.bank_guard(t, key)
                half0 = (hh % 2) * 512
                for i in range(4):
                    mm = t.matmul(pdm[:, half0 + i * 128:half0 + (i + 1) * 128],
                                  su_f16[:],
                                  sc["rhsd"][hh % 2][:, i * 128:(i + 1) * 128],
                                  start=(i == 0), stop=(i == 3))
                C.inc(mm, "tE")
            sec("tensor", t_d)
            X["dmm"][hh] = C.cnt["tE"]

        def s_e(s, gg=gg):
            s.wait_ge(C.sems["tE"], X["dmm"][2 * gg + 1])
            if X["expfree"].get(gg - 2):
                C.wait(s, "vE", X["expfree"][gg - 2])
            C.inc(s.activation(sc["expd"][gg % 2][:], pdm[:, 0:1024], AF.Exp),
                  "sE")
            C.bank_mark("pd0", "sE")
            C.bank_mark("pd1", "sE")
        sec("scalar", s_e)
        exp_g = C.cnt["sE"]

        def v_gA(v, g=g, gg=gg, par=par, exp_g=exp_g):
            C.wait(v, "sE", exp_g)
            if X["mfree"].get(gg - 2):
                v.wait_ge(C.sems["tE"], X["mfree"][gg - 2])
            v.tensor_tensor(
                sc["mtil"][gg % 2][:].rearrange("p (a b) -> p a b", b=128),
                sc["expd"][gg % 2][:].rearrange("p (a b) -> p a b", b=128),
                _bc(sc["cbtm"][par][:], [[128, 128], [0, 8], [1, 128]]),
                OP.mult)
            v.tensor_tensor(
                sc["xsTd"][gg % 2][:].rearrange("p (a b) -> p a b", b=64),
                sc["xsT"][:, g * 512:(g + 1) * 512].rearrange("p (a b) -> p a b", b=64),
                _bc(sc["dtc"][:, g * 8:g * 8 + 8], [[64, 128], [1, 8], [0, 64]]),
                OP.mult)
            C.inc(v.tensor_tensor(
                sc["dxst"][:].rearrange("p (a b) -> p a b", b=64),
                sc["xsT"][:, g * 512:(g + 1) * 512].rearrange("p (a b) -> p a b", b=64),
                _bc(dcol_bc[:, g * 8:g * 8 + 8], [[64, 128], [1, 8], [0, 64]]),
                OP.mult), "vE")
        sec("vector", v_gA)
        v_ga = C.cnt["vE"]
        X["expfree"][gg] = v_ga

        def t_iv(t, g=g, gg=gg, par=par, v_ga=v_ga):
            C.wait(t, "vE", v_ga)
            C.bank_guard(t, "pa1")
            for i in range(8):
                t.matmul(pa[1][:, i * 64:(i + 1) * 64],
                         sc["mtil"][gg % 2][:, i * 128:(i + 1) * 128],
                         sc["xsTd"][gg % 2][:, i * 64:(i + 1) * 64],
                         start=(i == 0), stop=(i == 7))
            C.bank_guard(t, "pa0")
            if X["hev"].get(g):
                t.wait_ge(C.sems["vE"], X["hev"][g])
            for i, r in enumerate(heads):
                mm = t.matmul(pa[0][:, i * 64:(i + 1) * 64],
                              bmcf[:, TT + ch * 128:TT + (ch + 1) * 128],
                              h_bf[:, r * 64:(r + 1) * 64],
                              start=(i == 0), stop=(i == 7))
            C.inc(mm, "tE")
            key = f"pd{g % 2}"
            C.bank_guard(t, key)
            C.inc(t.matmul(pdm[:, (g % 2) * 512:(g % 2) * 512 + 512],
                           sc["vf16"][par][:], e2_f16[:, g * 512:(g + 1) * 512],
                           start=True, stop=True), "tE")
        sec("tensor", t_iv)
        t_ve = C.cnt["tE"]
        t_int = t_ve - 1
        X["mfree"][gg] = t_int

        def v_gB(v, g=g, t_ve=t_ve, t_int=t_int):
            v.wait_ge(C.sems["tE"], t_ve)
            v.tensor_copy(sc["vexp"][:], pdm[:, (g % 2) * 512:(g % 2) * 512 + 512])
            C.bank_mark(f"pd{g % 2}", "vE")
            C.inc(v.scalar_tensor_tensor(sc["ytmp"][:], pa[0][:, 0:512], 1.0,
                                         sc["vexp"][:], OP.mult, OP.mult), "vE")
            C.bank_mark("pa0", "vE")
        sec("vector", v_gB)
        v_yt = C.cnt["vE"]

        def v_gC(v, g=g, gg=gg, par=par, t_int=t_int):
            v.scalar_tensor_tensor(sc["yA"][:], pa[1][:, 0:512], 1.0,
                                   sc["ytmp"][:], OP.mult, OP.add)
            C.bank_mark("pa1", "vE")
            v.scalar_tensor_tensor(sc["yA"][:], sc["dxst"][:], 1.0, sc["yA"][:],
                                   OP.mult, OP.add)
            v.tensor_tensor(ysb[:, g * 512:(g + 1) * 512], sc["yA"][:],
                            zrd[:, g * 512:(g + 1) * 512], OP.mult)
            v.bn_stats(bnb[g % 2][:, (g // 2) * 6:(g // 2) * 6 + 6],
                       ysb[:, g * 512:(g + 1) * 512])
            if X["st"].get(gg - 1):
                v.wait_ge(C.sems["tE"], X["st"][gg - 1])
            v.tensor_tensor(
                sc["xtld"][:].rearrange("p (a b) -> p a b", b=64),
                sc["xsTd"][gg % 2][:].rearrange("p (a b) -> p a b", b=64),
                _bc(sc["t64a"][:, g * 8:g * 8 + 8], [[64, 128], [1, 8], [0, 64]]),
                OP.mult)
            C.inc(v.tensor_tensor(
                sc["hdec"][:].rearrange("p (a b) -> p a b", b=64),
                h_bf[:, g * 512:(g + 1) * 512].rearrange("p (a b) -> p a b", b=64),
                _bc(sc["sbc"][:, g * 8:g * 8 + 8], [[64, 128], [1, 8], [0, 64]]),
                OP.mult), "vE")
        sec("vector", v_gC)
        v_xt = C.cnt["vE"]

        def t_st(t, g=g, gg=gg, par=par, v_xt=v_xt):
            C.wait(t, "vE", v_xt)
            C.bank_guard(t, "pa2")
            for i in range(8):
                t.matmul(pa[2][:, i * 64:(i + 1) * 64], sc["bmt"][par][:],
                         sc["xtld"][:, i * 64:(i + 1) * 64],
                         start=(i == 0), stop=False)
            C.inc(t.matmul(pa[2][:, 0:512], ident_bf[:], sc["hdec"][:],
                           start=False, stop=True), "tE")
            X["st"][gg] = C.cnt["tE"]
        sec("tensor", t_st)
        t_hu = C.cnt["tE"]
        if g == 7:
            X["st_last"] = t_hu

        def v_h(v, g=g, t_hu=t_hu):
            v.wait_ge(C.sems["tE"], t_hu)
            C.inc(v.tensor_copy(h_bf[:, g * 512:(g + 1) * 512], pa[2][:, 0:512]),
                  "vE")
            C.bank_mark("pa2", "vE")
            X["hev"][g] = C.cnt["vE"]
        sec("vector", v_h)

    X["vyx_last"] = C.cnt["vE"]
    X["zrd_free"][par] = C.cnt["vE"]
    X["base_g"] += 8


def _scan_tail(nc, C, sec, tok0, ch, sc, nw_bc, ysb, yhatF, ident_bf, smalls,
               sm64, bnb, pa, pb, X):
    def v_tail(v):
        v.bn_aggr(smalls[:, 12:14], bnb[0][:, 0:24])
        v.bn_aggr(smalls[:, 20:22], bnb[1][:, 0:24])
        mA, vA = smalls[:, 12:13], smalls[:, 13:14]
        mB, vB = smalls[:, 20:21], smalls[:, 21:22]
        t0, t1 = smalls[:, 16:17], smalls[:, 17:18]
        v.tensor_tensor(smalls[:, 8:9], mA, mB, OP.add)
        v.tensor_scalar(smalls[:, 8:9], smalls[:, 8:9], 0.5, None, OP.mult)
        v.tensor_tensor(t0, mA, mB, OP.subtract)
        v.tensor_tensor(t0, t0, t0, OP.mult)
        v.tensor_tensor(t1, vA, vB, OP.add)
        v.tensor_scalar(t1, t1, 0.5, None, OP.mult)
        v.scalar_tensor_tensor(smalls[:, 9:10], t0, 0.25, t1, OP.mult, OP.add)
        v.tensor_tensor(t0, smalls[:, 8:9], smalls[:, 8:9], OP.mult)
        v.tensor_tensor(smalls[:, 9:10], smalls[:, 9:10], t0, OP.add)
        v.tensor_scalar(smalls[:, 9:10], smalls[:, 9:10], EPS, None, OP.add)
        C.inc(v.tensor_copy(smalls[:, 18:19], smalls[:, 9:10]), "vE")
    sec("vector", v_tail)
    v_t1 = C.cnt["vE"]

    def s_tail(s, v_t1=v_t1):
        C.wait(s, "vE", v_t1)
        if X["vt2"]:
            C.wait(s, "vE", X["vt2"])
        C.inc(_rstd_finish(s, smalls[:, 18:19], smalls[:, 10:11],
                           smalls[:, 11:12]), "sE")
    sec("scalar", s_tail)
    s_t1 = C.cnt["sE"]

    def v_tail2(v, s_t1=s_t1):
        C.wait(v, "sE", s_t1)
        if not X["ytr"]:
            C.wait(v, "dw")
        if X["ytr"]:
            v.wait_ge(C.sems["tE"], X["ytr"])
        C.inc(v.scalar_tensor_tensor(sc["yhat_tok"][:], ysb[:], smalls[:, 10:11],
                                     nw_bc[:], OP.mult, OP.mult), "vE")
    sec("vector", v_tail2)
    v_t2 = C.cnt["vE"]
    X["vt2"] = v_t2

    for jb in range(4):
        def tf2(t, jb=jb, v_t2=v_t2):
            if jb == 0:
                C.wait(t, "vE", v_t2)
            key = f"pb{jb % 2}"
            C.bank_guard(t, key)
            for i in range(8):
                c = jb * 8 + i
                mm = t.transpose(pb[jb % 2][:, i * 128:(i + 1) * 128],
                                 sc["yhat_tok"][:, c * 128:(c + 1) * 128],
                                 ident_bf[:])
            C.inc(mm, "tE")
        sec("tensor", tf2)
        t_tr2 = C.cnt["tE"]

        def sg2(v, jb=jb, t_tr2=t_tr2):
            v.wait_ge(C.sems["tE"], t_tr2)
            dstap = yhatF[:, jb * 8 * TT:(jb * 8 + 8) * TT] \
                .rearrange("p (c t) -> p c t", t=TT)[:, :, ch * 128:(ch + 1) * 128]
            C.inc(v.tensor_copy(
                dstap, pb[jb % 2][:, 0:1024].rearrange("p (c t) -> p c t", t=128)),
                "vE")
            C.bank_mark(f"pb{jb % 2}", "vE")
        sec("vector", sg2)
    X["ytr"] = C.cnt["tE"]


def _e_outproj(nc, C, sec, tok0, w_out, yhatF, wbuf, x1stg, x1spill, pa):
    dw_at = {}
    mm_done = {}
    do_at = {}

    def pre(g):
        def f(gp, g=g):
            col0 = g * 256
            if g >= 2:
                gp.wait_ge(C.sems["tE"], mm_done[g - 2])
            dw_at[g] = C.dma(gp.dma_start(
                wbuf[g % 2][:, 0:32 * 256].rearrange("p (c f) -> p c f", f=256),
                w_out.rearrange("(c p) f -> p c f", p=128)[:, :, col0:col0 + 256]),
                "dwa" if g % 2 == 0 else "dwb")
        sec("gpsimd", f)

    pre(0)
    pre(1)
    for g in range(16):
        col0 = g * 256
        for b in range(NB):
            idx = g * NB + b

            def tf(t, g=g, b=b, idx=idx):
                if b == 0:
                    C.wait(t, "dwa" if g % 2 == 0 else "dwb")
                C.bank_guard(t, f"pa{idx % 3}")
                for c in range(32):
                    mm = t.matmul(pa[idx % 3][:, 0:256],
                                  yhatF[:, c * TT + b * 128:c * TT + (b + 1) * 128],
                                  wbuf[g % 2][:, c * 256:(c + 1) * 256],
                                  start=(c == 0), stop=(c == 31))
                C.inc(mm, "tE")
                mm_done[g] = C.cnt["tE"]
            sec("tensor", tf)
            t_mm = C.cnt["tE"]

            def sf(v, idx=idx, t_mm=t_mm):
                v.wait_ge(C.sems["tE"], t_mm)
                if idx >= 2:
                    C.wait(v, "doa" if idx % 2 == 0 else "dob")
                C.inc(v.tensor_copy(x1stg[idx % 2][:, 0:256], pa[idx % 3][:, 0:256]),
                      "vE")
                C.bank_mark(f"pa{idx % 3}", "vE")
            sec("vector", sf)
            s_x1 = C.cnt["vE"]

            def df(sy, b=b, col0=col0, idx=idx, s_x1=s_x1):
                C.wait(sy, "vE", s_x1)
                do_at[idx] = C.dma(sy.dma_start(
                    x1spill[tok0 + b * 128:tok0 + (b + 1) * 128, col0:col0 + 256],
                    x1stg[idx % 2][:, 0:256]), "doa" if idx % 2 == 0 else "dob")
            sec("sync", df)
        if g + 2 < 16:
            pre(g + 2)


def _f_prep(nc, C, sec, ffn_a, ffn_b, ffn_wq, ffn_keys, wbuf, awT, bwT,
            mT, keysT, ident_bf, pa, pb):
    def d0(gp):
        for h in range(2):
            C.dma(gp.dma_start(
                wbuf[h][:, 0:8192].rearrange("p (j d) -> p j d", d=4096),
                ffn_a.rearrange("(j p) d -> p j d", p=128)[:, 2 * h:2 * h + 2, :]),
                "dw")
    sec("gpsimd", d0)
    dw_a = C.cnt["dw"]

    for cb in range(16):
        def tf(t, cb=cb):
            if cb == 0:
                C.wait(t, "dw", dw_a)
            key = f"pb{cb % 2}"
            C.bank_guard(t, key)
            for cc in range(2):
                c = cb * 2 + cc
                for j in range(4):
                    mm = t.transpose(
                        pb[cb % 2][:, cc * 512 + j * 128:cc * 512 + (j + 1) * 128],
                        wbuf[j // 2][:, (j % 2) * 4096 + c * 128:
                                     (j % 2) * 4096 + (c + 1) * 128],
                        ident_bf[:])
            C.inc(mm, "tE")
        sec("tensor", tf)
        t_tr = C.cnt["tE"]

        def sg(s, cb=cb, t_tr=t_tr):
            s.wait_ge(C.sems["tE"], t_tr)
            C.inc(s.activation(awT[:, cb * 1024:(cb + 1) * 1024],
                               pb[cb % 2][:, 0:1024], AF.Copy), "sE")
            C.bank_mark(f"pb{cb % 2}", "sE")
        sec("scalar", sg)
    t_a_done = C.cnt["tE"]

    def d1(gp, t_a_done=t_a_done):
        gp.wait_ge(C.sems["tE"], t_a_done)
        for h in range(2):
            C.dma(gp.dma_start(
                wbuf[h][:, 0:8192].rearrange("p (j n) -> p j n", n=512),
                ffn_b.rearrange("(j p) n -> p j n", p=128)[:, 16 * h:16 * h + 16, :]),
                "dw")
    sec("gpsimd", d1)
    dw_b = C.cnt["dw"]

    for nch in range(4):
        for jb in range(4):
            def tf2(t, nch=nch, jb=jb):
                if nch == 0 and jb == 0:
                    C.wait(t, "dw", dw_b)
                key = f"pb{jb % 2}"
                C.bank_guard(t, key)
                for i in range(8):
                    j = jb * 8 + i
                    mm = t.transpose(pb[jb % 2][:, i * 128:(i + 1) * 128],
                                     wbuf[j // 16][:, (j % 16) * 512 + nch * 128:
                                                   (j % 16) * 512 + (nch + 1) * 128],
                                     ident_bf[:])
                C.inc(mm, "tE")
            sec("tensor", tf2)
            t_tr2 = C.cnt["tE"]

            def sg2(s, nch=nch, jb=jb, t_tr2=t_tr2):
                s.wait_ge(C.sems["tE"], t_tr2)
                C.inc(s.activation(
                    bwT[:, nch * 4096 + jb * 1024:nch * 4096 + (jb + 1) * 1024],
                    pb[jb % 2][:, 0:1024], AF.Copy), "sE")
                C.bank_mark(f"pb{jb % 2}", "sE")
            sec("scalar", sg2)
    t_b_done = C.cnt["tE"]

    def d2(gp, t_b_done=t_b_done):
        gp.wait_ge(C.sems["tE"], t_b_done)
        C.dma(gp.dma_start(wbuf[0][0:16, 0:4096], ffn_wq), "dw")
        C.dma(gp.dma_start(wbuf[0][0:64, 4096:4112], ffn_keys), "dw")
    sec("gpsimd", d2)
    dw_q = C.cnt["dw"]

    def tf4(t):
        C.wait(t, "dw", dw_q)
        C.bank_guard(t, "pb0")
        C.inc(t.transpose(pb[0][0:16, 0:64], wbuf[0][0:64, 4096:4112],
                          ident_bf[0:64, 0:64]), "tE")
    sec("tensor", tf4)
    t_k = C.cnt["tE"]

    def sg4(s, t_k=t_k):
        s.wait_ge(C.sems["tE"], t_k)
        C.inc(s.activation(keysT[:], pb[0][0:16, 0:64], AF.Copy), "sE")
        C.bank_mark("pb0", "sE")
    sec("scalar", sg4)
    s_k = C.cnt["sE"]

    for mb in range(4):
        def tm(t, mb=mb, s_k=s_k):
            if mb == 0:
                C.wait(t, "sE", s_k)
            C.bank_guard(t, f"pa{mb % 2}")
            for i in range(8):
                c = mb * 8 + i
                mm = t.matmul(pa[mb % 2][:, i * 64:(i + 1) * 64],
                              wbuf[0][0:16, c * 128:(c + 1) * 128],
                              keysT[:], start=True, stop=True)
            C.inc(mm, "tE")
        sec("tensor", tm)
        t_m = C.cnt["tE"]

        def vm(v, mb=mb, t_m=t_m):
            v.wait_ge(C.sems["tE"], t_m)
            C.inc(v.tensor_copy(mT[:, mb * 512:(mb + 1) * 512],
                                pa[mb % 2][:, 0:512]), "vE")
            C.bank_mark(f"pa{mb % 2}", "vE")
        sec("vector", vm)


def _f_x(nc, C, sec, tok0, b, x_in, x1spill, xblk2, ysb2, x2F, x2tok,
         awT, mT, bnb, smalls, ident_bf, pa, pb, Xf):
    t0b = tok0 + b * 128
    par = b % 2
    zi = 0 if par == 0 else 3
    si = 1 if par == 0 else 2

    def d0(sy, b=b):
        if Xf["aq"].get(b - 1):
            sy.wait_ge(C.sems["tE"], Xf["aq"][b - 1])
        if Xf["x2Fev"].get(b - 1):
            sy.wait_ge(C.sems["sE"], Xf["x2Fev"][b - 1])
        if Xf["vln"].get(b - 1):
            sy.wait_ge(C.sems["vE"], Xf["vln"][b - 1])
        dsem = "dxa" if b % 2 == 0 else "dxb"
        C.dma(sy.dma_start(xblk2[:], x_in[t0b:t0b + 128, :]), dsem)
        C.dma(sy.dma_start(x2F[:], x1spill[t0b:t0b + 128, :]), dsem)
    sec("sync", d0)

    def v0(v):
        C.wait(v, "dxa" if b % 2 == 0 else "dxb")
        if Xf["s1"].get(b - 1):
            v.wait_ge(C.sems["sE"], Xf["s1"][b - 1])
        v.tensor_tensor(ysb2[:], x2F[:, 0:H], xblk2[:], OP.add)
        _ln_stats(C, v, bnb, ysb2[:], H, smalls[:, 8:9], smalls[:, 9:10],
                  smalls, 12, want_meansq=False)
        C.inc(v.tensor_copy(smalls[:, 18:19], smalls[:, 9:10]), "vE")
    sec("vector", v0)
    v_ln = C.cnt["vE"]
    Xf["vln"][b] = v_ln

    def s0(s, v_ln=v_ln, par=par):
        C.wait(s, "vE", v_ln)
        _rstd_finish(s, smalls[:, 18:19], smalls[:, 10:11], smalls[:, 11:12])
        s.activation(smalls[:, 11:12], smalls[:, 8:9], AF.Copy,
                     scale=smalls[:, 10:11])
        s.activation(smalls[:, 19:20], smalls[:, 11:12], AF.Copy, scale=-1.0)
        if Xf["tb"].get(b - 2):
            s.wait_ge(C.sems["tE"], Xf["tb"][b - 2])
        C.inc(s.activation(x2tok[par][:], ysb2[:], AF.Identity,
                           scale=smalls[:, 10:11], bias=smalls[:, 19:20]), "sE")
    sec("scalar", s0)
    s_x2 = C.cnt["sE"]
    Xf["s1"][b] = s_x2

    for jb in range(4):
        def tf(t, jb=jb, s_x2=s_x2, par=par):
            if jb == 0:
                C.wait(t, "sE", s_x2)
            C.bank_guard(t, f"pb{jb % 2}")
            for i in range(8):
                c = jb * 8 + i
                mm = t.transpose(pb[jb % 2][:, i * 128:(i + 1) * 128],
                                 x2tok[par][:, c * 128:(c + 1) * 128], ident_bf[:])
            C.inc(mm, "tE")
        sec("tensor", tf)
        t_tr = C.cnt["tE"]

        def sg(s, jb=jb, t_tr=t_tr):
            s.wait_ge(C.sems["tE"], t_tr)
            C.inc(s.activation(x2F[:, jb * 1024:(jb + 1) * 1024],
                               pb[jb % 2][:, 0:1024], AF.Copy), "sE")
            C.bank_mark(f"pb{jb % 2}", "sE")
        sec("scalar", sg)
    Xf["x2tr"][b] = C.cnt["tE"]
    s_x2f = C.cnt["sE"]
    Xf["x2Fev"][b] = s_x2f

    def t_a(t, s_x2f=s_x2f, zi=zi, si=si):
        C.wait(t, "sE", s_x2f)
        C.bank_guard(t, f"pa{zi}")
        C.bank_guard(t, f"pa{si}")
        last = None
        for c in range(32):
            t.matmul(pa[zi][:, 0:512], x2F[:, c * 128:(c + 1) * 128],
                     awT[:, c * 512:(c + 1) * 512], start=(c == 0), stop=(c == 31))
            last = t.matmul(pa[si][:, 0:64], x2F[:, c * 128:(c + 1) * 128],
                            mT[:, c * 64:(c + 1) * 64], start=(c == 0),
                            stop=(c == 31))
        C.inc(last, "tE")
    sec("tensor", t_a)
    Xf["aq"][b] = C.cnt["tE"]


def _f_y(nc, C, sec, tok0, b, out_t, x3f, x2tok, bwT, fscr, fscr2, gq, gwork,
         fcmp, zhat, zhT, bnb2, smalls, rcols, ident_bf, pa, pdm, pb,
         do_at, Xf):
    t0b = tok0 + b * 128
    par = b % 2
    zi = 0 if par == 0 else 3
    si = 1 if par == 0 else 2
    t_aq = Xf["aq"][b]

    def v_sc(v, t_aq=t_aq, si=si):
        C.wait(v, "tE", t_aq)
        v.tensor_copy(gq[:], pa[si][:, 0:64])
        C.bank_mark(f"pa{si}", "vE")
        v.tensor_tensor(
            fcmp[:, 0:4096].rearrange("p (a b) -> p a b", b=64),
            _bc(gq[:], [[64, 128], [0, 64], [1, 64]]),
            _bc(gq[:], [[64, 128], [1, 64], [0, 64]]),
            OP.is_gt)
        v.reduce_sum(fscr2[:], fcmp[:, 0:4096].rearrange("p (a b) -> p a b", b=64),
                     axis=AX)
        v.reduce_max(rcols[0][:], gq[:], axis=AX)
        C.inc(v.tensor_scalar(rcols[1][:], rcols[0][:], -1.0, None, OP.mult),
              "vE")
    sec("vector", v_sc)
    v_rt = C.cnt["vE"]

    def s_r(s, v_rt=v_rt):
        C.wait(s, "vE", v_rt)
        C.inc(s.activation(fscr[:, 0:64], gq[:], AF.Exp, bias=rcols[1][:]), "sE")
    sec("scalar", s_r)
    s_re = C.cnt["sE"]

    def v_g(v, s_re=s_re):
        C.wait(v, "sE", s_re)
        v.tensor_scalar(gwork[:], fscr2[:], 15.0, None, OP.is_le)
        v.tensor_tensor(gwork[:], gwork[:], fscr[:, 0:64], OP.mult)
        v.reduce_sum(rcols[3][:], gwork[:], axis=AX)
        v.reciprocal(rcols[0][:], rcols[3][:])
        C.inc(v.tensor_scalar(gwork[:], gwork[:], rcols[0][:], 2.0,
                              OP.mult, OP.mult), "vE")
    sec("vector", v_g)

    def v_z(v, zi=zi):
        C.inc(v.scalar_tensor_tensor(
            zhat[:].rearrange("p (a b) -> p a b", b=8),
            pa[zi][:, 0:512].rearrange("p (a b) -> p a b", b=8), 1.0,
            _bc(gwork[:], [[64, 128], [1, 64], [0, 8]]),
            OP.mult, OP.mult), "vE")
        C.bank_mark(f"pa{zi}", "vE")
    sec("vector", v_z)
    v_zh = C.cnt["vE"]

    def t_zt(t, v_zh=v_zh):
        C.wait(t, "vE", v_zh)
        C.bank_guard(t, "pb0")
        last = None
        for i in range(4):
            last = t.transpose(pb[0][:, i * 128:(i + 1) * 128],
                               zhat[:, i * 128:(i + 1) * 128], ident_bf[:])
        C.inc(last, "tE")
    sec("tensor", t_zt)
    t_zT = C.cnt["tE"]

    def s_zt(s, t_zT=t_zT):
        s.wait_ge(C.sems["tE"], t_zT)
        C.inc(s.activation(zhT[:], pb[0][:, 0:512], AF.Copy), "sE")
        C.bank_mark("pb0", "sE")
    sec("scalar", s_zt)
    s_zTT = C.cnt["sE"]

    for og in range(8):
        def t_b(t, og=og, s_zTT=s_zTT, par=par):
            if og == 0:
                C.wait(t, "sE", s_zTT)
            key = f"pd{og % 2}"
            C.bank_guard(t, key)
            half = (og % 2) * 512
            for nch in range(4):
                t.matmul(pdm[:, half:half + 512],
                         zhT[:, nch * 128:(nch + 1) * 128],
                         bwT[:, nch * 4096 + og * 512:
                             nch * 4096 + (og + 1) * 512],
                         start=(nch == 0), stop=False)
            C.inc(t.matmul(pdm[:, half:half + 512], ident_bf[:],
                           x2tok[par][:, og * 512:(og + 1) * 512],
                           start=False, stop=True), "tE")
        sec("tensor", t_b)
        t_bo = C.cnt["tE"]

        def s_vo(s, og=og, t_bo=t_bo, par=par):
            s.wait_ge(C.sems["tE"], t_bo)
            if og == 0:
                C.wait(s, "do" if par == 1 else "do2")
            half = (og % 2) * 512
            C.inc(s.activation(x3f[:, og * 512:(og + 1) * 512],
                               pdm[:, half:half + 512], AF.Copy), "sE")
            C.bank_mark(f"pd{og % 2}", "sE")
        sec("scalar", s_vo)
    Xf["tb"][b] = C.cnt["tE"]
    s_voe = C.cnt["sE"]

    def v_l2(v, s_voe=s_voe):
        C.wait(v, "sE", s_voe)
        _ln_stats(C, v, bnb2, x3f[:], H, smalls[:, 20:21], smalls[:, 21:22],
                  smalls, 24, want_meansq=False)
        C.inc(v.tensor_copy(smalls[:, 30:31], smalls[:, 21:22]), "vE")
    sec("vector", v_l2)
    v_l2a = C.cnt["vE"]

    def s_l2(s, v_l2a=v_l2a):
        C.wait(s, "vE", v_l2a)
        _rstd_finish(s, smalls[:, 30:31], smalls[:, 22:23], smalls[:, 23:24])
        s.activation(smalls[:, 23:24], smalls[:, 20:21], AF.Copy,
                     scale=smalls[:, 22:23])
        s.activation(smalls[:, 26:27], smalls[:, 23:24], AF.Copy, scale=-1.0)
        C.inc(s.activation(x3f[:], x3f[:], AF.Identity,
                           scale=smalls[:, 22:23], bias=smalls[:, 26:27]), "sE")
    sec("scalar", s_l2)
    s_ot = C.cnt["sE"]

    def d_out(sy, b=b, s_ot=s_ot, par=par):
        sy.wait_ge(C.sems["sE"], s_ot)
        do_at[b] = C.dma(sy.dma_start(out_t[t0b:t0b + 128, :], x3f[:]),
                         "do" if par == 0 else "do2")
    sec("sync", d_out)


def kernel(**inputs):
    _ = inputs.get("L0_attn_residual")
    x = np.ascontiguousarray(inputs["L0_attn_hidden_states"], dtype=np.float32)
    nc = build()
    base = {
        "w_in": np.ascontiguousarray(inputs["L0_attn_in_proj_w"], np.float32),
        "conv_w": np.ascontiguousarray(inputs["L0_attn_conv_w"], np.float32),
        "a_log": np.ascontiguousarray(inputs["L0_attn_A_log"], np.float32).reshape(1, NH),
        "dt_bias": np.ascontiguousarray(inputs["L0_attn_dt_bias"], np.float32).reshape(1, NH),
        "d_vec": np.ascontiguousarray(inputs["L0_attn_D"], np.float32).reshape(1, NH),
        "ln_w": np.ascontiguousarray(inputs["L0_attn_ln_w"], np.float32).reshape(1, H),
        "norm_w": np.ascontiguousarray(inputs["L0_attn_norm_w"], np.float32).reshape(1, DS),
        "mup": np.ascontiguousarray(inputs["L0_attn_mup_vector"], np.float32).reshape(1, PROJ),
        "w_out": np.ascontiguousarray(inputs["L0_attn_out_proj_w"], np.float32),
        "ffn_a": np.ascontiguousarray(inputs["L0_ffn_A_w"], np.float32),
        "ffn_b": np.ascontiguousarray(inputs["L0_ffn_B_w"], np.float32),
        "ffn_wq": np.ascontiguousarray(inputs["L0_ffn_Wq_w"], np.float32),
        "ffn_keys": np.ascontiguousarray(inputs["L0_ffn_keys"], np.float32),
    }
    in_maps = [dict(base, x=np.ascontiguousarray(x[c])) for c in range(NCORES)]
    res = run_bass_kernel_spmd(nc, in_maps, core_ids=list(range(NCORES)))
    return np.stack([res.results[c]["out"] for c in range(NCORES)], axis=0)


# version 19 (final)
# speedup vs baseline: 1.0142x; 1.0067x over previous
import contextlib
import numpy as np
import concourse.bass as bass
import concourse.mybir as mybir
from concourse.bass_utils import run_bass_kernel_spmd
from concourse.alu_op_type import AluOpType as OP

F32 = mybir.dt.float32
F16 = mybir.dt.float16
BF16 = mybir.dt.bfloat16
AF = mybir.ActivationFunctionType
AX = mybir.AxisListType.X

L = 2048
H = 4096
DS = 4096
PROJ = 8512
CONVD = 4352
NH = 64
TT = 1024
NB = 8
NT = L // TT
EPS = 1e-5
NCORES = 8
EXT = TT + 3

ALL_ENG = ["sync", "gpsimd", "tensor", "vector", "scalar"]
_MERGE_UNTIL = 175

_DRAIN_OPS = frozenset({
    "tensor_scalar", "tensor_tensor", "scalar_tensor_tensor", "tensor_copy",
    "tensor_tensor_scan", "reduce_max", "reduce_sum", "tensor_reduce",
    "bn_stats", "bn_aggr", "reciprocal", "memset", "activation", "select",
    "copy",
})


class _RecCall:
    __slots__ = ("name", "a", "k", "chain")

    def __init__(self, name, a, k):
        self.name, self.a, self.k = name, a, k
        self.chain = []

    def then_inc(self, sem, n):
        self.chain.append((sem, n))
        return self


class _RecEngine:

    def __init__(self, buf):
        self._buf = buf

    def __getattr__(self, name):
        def call(*a, **k):
            rc = _RecCall(name, a, k)
            self._buf.append(rc)
            return rc
        return call


def _replay(eng, calls):
    for rc in calls:
        inst = getattr(eng, rc.name)(*rc.a, **rc.k)
        for sem, n in rc.chain:
            inst = inst.then_inc(sem, n)


def _iv(ap):
    dims = list(ap.ap)
    span = sum(s * (c - 1) for s, c in dims[1:])
    return (ap.tensor.name, ap.offset, ap.offset + span + 1)


def _ovl(iv, ivs):
    n, lo, hi = iv
    return any(n == m and lo < h and l < hi for m, l, h in ivs)


class _AutoDrain:

    def __init__(self, eng, state=None):
        self._e = eng
        self._w, self._r = state if state is not None else ([], [])

    def __getattr__(self, name):
        attr = getattr(self._e, name)
        if name in _DRAIN_OPS:
            def w(*a, **k):
                aps = [x for x in list(a) + list(k.values())
                       if isinstance(x, bass.AP)]
                ivs = [_iv(ap) for ap in aps]
                out, ins = ivs[:1], ivs[1:]
                if (any(_ovl(i, self._w) for i in ins)
                        or any(_ovl(o, self._w) for o in out)
                        or any(_ovl(o, self._r) for o in out)):
                    self._e.drain()
                    self._w.clear()
                    self._r.clear()
                self._w.extend(out)
                self._r.extend(ins)
                return attr(*a, **k)
            return w
        return attr


def bcast_ap(dram_ap, npart, cols):
    return bass.AP(dram_ap.tensor, 0, [[0, npart], [1, cols]])


class Ctx:

    def __init__(self, nc, es):
        self.nc = nc
        self.sems = {}
        self.cnt = {}
        for name in ["dw", "dwa", "dwb", "dx", "dxa", "dxb", "do", "do2",
                     "doa", "dob", "tE", "vE", "sE", "gE"]:
            self.sems[name] = es.enter_context(nc.semaphore(name))
            self.cnt[name] = 0
        self.bank_rd = {}
        self.merge_scalar = True

    def inc(self, instr, name, n=1):
        instr.then_inc(self.sems[name], n)
        self.cnt[name] += n
        return self.cnt[name]

    def dma(self, instr, name):
        return self.inc(instr, name, 16)

    def wait(self, eng, name, val=None):
        v = self.cnt[name] if val is None else val
        if v:
            eng.wait_ge(self.sems[name], v)

    def bank_guard(self, eng, key):
        if key in self.bank_rd:
            sem, val = self.bank_rd[key]
            if val:
                eng.wait_ge(self.sems[sem], val)

    def bank_mark(self, key, sem):
        self.bank_rd[key] = (sem, self.cnt[sem])


def build(debug=()):
    nc = bass.Bass("TRN2", target_bir_lowering=False)
    debug = set(debug)

    x_in = nc.dram_tensor("x", [L, H], F32, kind="ExternalInput").ap()
    w_in = nc.dram_tensor("w_in", [H, PROJ], F32, kind="ExternalInput").ap()
    conv_w = nc.dram_tensor("conv_w", [CONVD, 4], F32, kind="ExternalInput").ap()
    a_log = nc.dram_tensor("a_log", [1, NH], F32, kind="ExternalInput").ap()
    dt_bias = nc.dram_tensor("dt_bias", [1, NH], F32, kind="ExternalInput").ap()
    d_vec = nc.dram_tensor("d_vec", [1, NH], F32, kind="ExternalInput").ap()
    ln_w = nc.dram_tensor("ln_w", [1, H], F32, kind="ExternalInput").ap()
    norm_w = nc.dram_tensor("norm_w", [1, DS], F32, kind="ExternalInput").ap()
    mup = nc.dram_tensor("mup", [1, PROJ], F32, kind="ExternalInput").ap()
    w_out = nc.dram_tensor("w_out", [DS, H], F32, kind="ExternalInput").ap()
    ffn_a = nc.dram_tensor("ffn_a", [512, H], F32, kind="ExternalInput").ap()
    ffn_b = nc.dram_tensor("ffn_b", [H, 512], F32, kind="ExternalInput").ap()
    ffn_wq = nc.dram_tensor("ffn_wq", [16, H], F32, kind="ExternalInput").ap()
    ffn_keys = nc.dram_tensor("ffn_keys", [NH, 16], F32, kind="ExternalInput").ap()

    out_t = nc.dram_tensor("out", [L, H], F32, kind="ExternalOutput").ap()

    def scratch(name, shape, dtype):
        if name in debug:
            return nc.dram_tensor(name, shape, dtype, kind="ExternalOutput").ap()
        return nc.dram_tensor(name, shape, dtype).ap()

    zspill = scratch("zspill", [L, DS], BF16)
    xsspill = scratch("xsspill", [DS, L], BF16)
    x1spill = scratch("x1spill", [L, H], BF16)

    es = contextlib.ExitStack()
    with es:
        C = Ctx(nc, es)
        block = es.enter_context(nc.Block())

        rbuf = {"tensor": [], "vector": [], "scalar": []}
        rec = {e: _RecEngine(rbuf[e]) for e in rbuf}
        _vstate = ([], [])
        wrapped = {"tensor": rec["tensor"],
                   "vector": _AutoDrain(rec["vector"], _vstate),
                   "scalar": _AutoDrain(rec["scalar"])}

        def flush_one(e):
            calls = rbuf[e]
            if calls:
                rbuf[e] = []
                rec[e]._buf = rbuf[e]
                getattr(block, e)(lambda eng, calls=calls: _replay(eng, calls))

        def sec(eng_name, fn):
            if eng_name in rbuf and C.merge_scalar:
                fn(wrapped[eng_name])
                if len(rbuf[eng_name]) > 48:
                    flush_one(eng_name)
            elif eng_name == "scalar":
                flush_one("scalar")
                getattr(block, eng_name)(lambda e: fn(_AutoDrain(e)))
            elif eng_name == "vector":
                flush_one("vector")
                getattr(block, eng_name)(
                    lambda e: fn(_AutoDrain(e, _vstate)))
            elif eng_name == "tensor":
                flush_one("tensor")
                getattr(block, eng_name)(fn)
            else:
                getattr(block, eng_name)(fn)

        def flush():
            for e in rbuf:
                flush_one(e)

        def barrier():
            flush()
            snap = dict(C.cnt)
            for e in ALL_ENG:
                def f(eng, snap=snap):
                    for name, v in snap.items():
                        if v:
                            eng.wait_ge(C.sems[name], v)
                sec(e, f)

        A = lambda n, sh, dt: es.enter_context(nc.sbuf_tensor(n, sh, dt))

        ones_f32 = A("ones_f32", [128, 128], F32)
        ones_f16 = A("ones_f16", [128, 128], F16)
        ident_bf = A("ident_bf", [128, 128], BF16)
        ident_f32 = A("ident_f32", [128, 128], F32)
        tri_f16 = A("tri_f16", [128, 128], F16)
        su_f16 = A("su_f16", [128, 128], F16)
        tri_f32 = A("tri_f32", [128, 128], F32)
        e2_f16 = A("e2_f16", [64, DS], F16)
        e512_bf = A("e512_bf", [64, 512], BF16)
        dbias_bc = A("dbias_bc", [128, NH], F32)
        nega_bc = A("nega_bc", [128, NH], F32)
        dcol_bc = A("dcol_bc", [128, NH], F32)
        convwF = A("convwF", [128, 34 * 4], F32)
        iota_f = A("iota_f", [128, 128], F32)
        iota_c = A("iota_c", [128, 1], F32)
        smalls = A("smalls", [128, 32], F32)
        sm64 = A("sm64", [64, 16], F32)
        dtT = A("dtT", [128, NB * NH], F32)
        h_bf = A("h_bf", [128, DS], BF16)
        tailsave = A("tailsave", [128, 34 * 3], BF16)
        bmcf = A("bmcf", [128, 2 * TT], BF16)
        wbuf = [A(f"wbuf{i}", [128, 8192], BF16) for i in range(2)]
        bnb = [A(f"bnb{i}", [128, 24], F32) for i in range(2)]

        pa = [es.enter_context(nc.psum_tensor(f"pa{i}", [128, 512], F32)) for i in range(4)]
        pdm = es.enter_context(nc.psum_tensor("pdm", [128, 1024], F32))
        pb = [es.enter_context(nc.psum_tensor(f"pb{i}", [128, 1024], BF16)) for i in range(2)]

        with contextlib.ExitStack() as sp:
            e2scr = sp.enter_context(nc.sbuf_tensor("e2scr", [64, DS], F32))

            def setup_sync(s):
                C.dma(s.dma_start(convwF[:].rearrange("p (c k) -> p c k", k=4),
                                  conv_w.rearrange("(c p) k -> p c k", p=128)), "dx")
                C.dma(s.dma_start(dbias_bc[:], bcast_ap(dt_bias, 128, NH)), "dx")
                C.dma(s.dma_start(nega_bc[:], bcast_ap(a_log, 128, NH)), "dx")
                C.dma(s.dma_start(dcol_bc[:], bcast_ap(d_vec, 128, NH)), "dx")
            sec("sync", setup_sync)

            def setup_gp(g):
                g.iota(iota_f[:], [[1, 128]], base=0, channel_multiplier=0,
                       allow_small_or_imprecise_dtypes=True)
                g.iota(iota_c[:], [[0, 1]], base=0, channel_multiplier=1,
                       allow_small_or_imprecise_dtypes=True)
                C.inc(g.iota(e2scr[:, :].rearrange("p (a b) -> p a b", b=64),
                             [[1, 64], [0, 64]], base=0, channel_multiplier=0,
                             allow_small_or_imprecise_dtypes=True), "gE")
            sec("gpsimd", setup_gp)

            def setup_v(v):
                C.wait(v, "gE")
                C.wait(v, "dx")
                v.memset(ones_f32[:], 1.0)
                v.memset(ones_f16[:], 1.0)
                v.memset(h_bf[:], 0.0)
                v.memset(tailsave[:], 0.0)
                v.tensor_scalar(ident_f32[:], iota_f[:], iota_c[:], None, OP.is_equal)
                v.tensor_copy(ident_bf[:], ident_f32[:])
                v.tensor_scalar(tri_f32[:], iota_f[:], iota_c[:], None, OP.is_ge)
                v.tensor_copy(tri_f16[:], tri_f32[:])
                v.tensor_scalar(su_f16[:], iota_f[:], iota_c[:], None, OP.is_lt)
                v.tensor_scalar(e2scr[:, :], e2scr[:, :], iota_c[0:64, :],
                                None, OP.is_equal)
                C.inc(v.tensor_copy(e2_f16[:], e2scr[:, :]), "vE")
            sec("vector", setup_v)
            v_e2 = C.cnt["vE"]

            def setup_gp2(g):
                C.wait(g, "vE", v_e2)
                C.inc(g.iota(e2scr[:, 0:512].rearrange("p (a b) -> p a b", b=8),
                             [[1, 64], [0, 8]], base=0, channel_multiplier=0,
                             allow_small_or_imprecise_dtypes=True), "gE")
            sec("gpsimd", setup_gp2)

            def setup_v2(v):
                C.wait(v, "gE")
                v.tensor_scalar(e2scr[:, 0:512], e2scr[:, 0:512], iota_c[0:64, :],
                                None, OP.is_equal)
                C.inc(v.tensor_copy(e512_bf[:], e2scr[:, 0:512]), "vE")
            sec("vector", setup_v2)
            v_e5 = C.cnt["vE"]

            def setup_s(s):
                C.wait(s, "vE", v_e5)
                C.inc(s.activation(nega_bc[:], nega_bc[:], AF.Exp), "sE")
            sec("scalar", setup_s)

            def setup_v3(v):
                C.wait(v, "sE")
                C.inc(v.tensor_scalar(nega_bc[:], nega_bc[:], -1.0, None, OP.mult),
                      "vE")
            sec("vector", setup_v3)

            barrier()

        for tile in range(NT):
            tok0 = tile * TT

            with contextlib.ExitStack() as ph:
                Aph = lambda n, sh, dt: ph.enter_context(
                    nc.sbuf_tensor(f"{n}_t{tile}", sh, dt))
                xhatF = Aph("xhatF", [128, 32 * TT], BF16)

                with contextlib.ExitStack() as p0s:
                    A0 = lambda n, sh, dt: p0s.enter_context(
                        nc.sbuf_tensor(f"{n}_t{tile}", sh, dt))
                    lnw_bc = A0("lnw_bc", [128, H], F16)
                    xblk = [A0(f"xblk{i}", [128, H], F32) for i in range(2)]
                    xhat_tok = A0("xhat_tok", [128, H], BF16)
                    with nc.named_scope(f"P0_t{tile}"):
                        _p0(nc, C, sec, tok0, x_in, ln_w, lnw_bc, xblk, xhat_tok,
                            xhatF, bnb, smalls, ident_bf, pa, pb)
                        barrier()

                with contextlib.ExitStack() as a1s:
                    A1 = lambda n, sh, dt: a1s.enter_context(
                        nc.sbuf_tensor(f"{n}_t{tile}", sh, dt))
                    mup_bc = A1("mup_bc", [128, PROJ], F16)
                    xbct = [A1(f"xbct{i}", [128, EXT], BF16) for i in range(8)]
                    ztmp = A1("ztmp", [128, 512], F32)
                    bst = A1("bst", [128, 512], BF16)
                    accb = A1("accb", [128, TT], F32)
                    zstg = [A1(f"zstg{i}", [128, 512], BF16) for i in range(2)]
                    xstg = [A1(f"xstg{i}", [128, TT], BF16) for i in range(2)]

                    def mup_load(g):
                        C.dma(g.dma_start(mup_bc[:], bcast_ap(mup, 128, PROJ)), "dw")
                    sec("gpsimd", mup_load)

                    with nc.named_scope(f"A1_t{tile}"):
                        _a1(nc, C, sec, tok0, w_in, xhatF, wbuf, mup_bc, ztmp, bst,
                            xbct, accb, convwF, tailsave, bmcf, dtT, xstg, xsspill,
                            ident_bf, pa, pb)
                        barrier()
                    with nc.named_scope(f"A2_t{tile}"):
                        _a2(nc, C, sec, tok0, w_in, xhatF, wbuf, mup_bc, ztmp,
                            zstg, zspill, pa)
                        barrier()

            with contextlib.ExitStack() as ph:
                Aph = lambda n, sh, dt: ph.enter_context(
                    nc.sbuf_tensor(f"{n}_t{tile}", sh, dt))
                yhatF = Aph("yhatF", [128, 32 * TT], BF16)
                nw_bc = Aph("nw_bc", [128, DS], F16)
                ysb = Aph("ysb", [128, DS], BF16)
                sc = {n: Aph(n, sh, dt) for n, sh, dt in [
                    ("xsT", [128, DS], BF16),
                    ("yhat_tok", [128, DS], BF16),
                    ("vexp", [128, 512], F32),
                    ("ytmp", [128, 512], BF16),
                    ("dxst", [128, 512], BF16),
                    ("yA", [128, 512], BF16),
                    ("xtld", [128, 512], BF16),
                    ("hdec", [128, 512], BF16),
                    ("dtc", [128, NH], F32),
                    ("dta32", [128, NH], F32),
                    ("dta16", [128, NH], F16),
                    ("srow", [1, 64], F32),
                    ("sbc", [128, 64], F32),
                    ("t64a", [128, NH], F32),
                    ("t64b", [128, NH], F32),
                    ("t64c", [128, NH], F32),
                ]}
                sc["cbtm"] = [Aph(f"cbtm{i}", [128, 128], F32) for i in range(2)]
                sc["bmt"] = [Aph(f"bmt{i}", [128, 128], BF16) for i in range(2)]
                sc["vf16"] = [Aph(f"vf16{i}", [64, 128], F16) for i in range(2)]
                sc["expd"] = [Aph(f"expd{i}", [128, 1024], F16) for i in range(2)]
                sc["rhsd"] = [Aph(f"rhsd{i}", [128, 512], F16) for i in range(2)]
                sc["mtil"] = [Aph(f"mtil{i}", [128, 1024], BF16) for i in range(2)]
                sc["xsTd"] = [Aph(f"xsTd{i}", [128, 512], BF16) for i in range(2)]
                x1stg = [Aph(f"x1stg{i}", [128, 512], BF16) for i in range(2)]

                def nw_load(g):
                    C.dma(g.dma_start(nw_bc[:], bcast_ap(norm_w, 128, DS)), "dw")
                sec("gpsimd", nw_load)

                with nc.named_scope(f"SCAN_t{tile}"):
                    C.merge_scalar = False
                    X = {"xrd_free": {0: 0, 1: 0}, "zrd_free": {0: 0, 1: 0},
                         "st_last": 0, "scum": 0, "tcu_prev": 0, "vyx_last": 0,
                         "front_v": 0, "vt2": 0, "ytr": 0, "base_g": 0,
                         "dmm": {}, "expfree": {}, "mfree": {}, "dxs": {},
                         "st": {}, "hev": {}}
                    fr = lambda ch: _scan_front(
                        nc, C, sec, tok0, ch, sc, wbuf, xsspill, zspill, dtT,
                        dbias_bc, nega_bc, bmcf, ones_f16, ones_f32, tri_f16,
                        su_f16, tri_f32, ident_bf, ident_f32, sm64, pa, pb, X)
                    fr(0)
                    for ch in range(NB):
                        _scan_groups(nc, C, sec, tok0, ch, sc, wbuf, dcol_bc,
                                     h_bf, bmcf, ysb, e2_f16, bnb, tri_f16,
                                     su_f16, ident_bf, pa, pdm, pb, X)
                        if ch + 1 < NB:
                            fr(ch + 1)
                        _scan_tail(nc, C, sec, tok0, ch, sc, nw_bc, ysb, yhatF,
                                   ident_bf, smalls, sm64, bnb, pa, pb, X)
                    barrier()
                    C.merge_scalar = True
                with nc.named_scope(f"E_t{tile}"):
                    _e_outproj(nc, C, sec, tok0, w_out, yhatF, wbuf, x1stg,
                               x1spill, pa)
                    barrier()

            with contextlib.ExitStack() as ph:
                Aph = lambda n, sh, dt: ph.enter_context(
                    nc.sbuf_tensor(f"{n}_t{tile}", sh, dt))
                awT = Aph("awT", [128, 32 * 512], BF16)
                bwT = Aph("bwT", [128, 4 * H], BF16)
                mT = Aph("mT", [128, 32 * 64], BF16)
                keysT = Aph("keysT", [16, 64], BF16)
                x2F = Aph("x2F", [128, 32 * 128], BF16)
                x2tok = [Aph(f"x2tok{i}", [128, H], BF16) for i in range(2)]
                bnb2 = [Aph(f"bnb2_{i}", [128, 24], F32) for i in range(2)]
                x3f = Aph("x3f", [128, H], F32)
                xblk2 = Aph("xblk2", [128, H], F32)
                ysb2 = Aph("ysb2", [128, H], F32)
                fscr = Aph("fscr", [128, 512], F32)
                fscr2 = Aph("fscr2", [128, 64], F32)
                gq = Aph("gq", [128, 64], F32)
                gwork = Aph("gwork", [128, 64], F32)
                zhat = Aph("zhat", [128, 512], BF16)
                zhT = Aph("zhT", [128, 512], BF16)
                rcols = [Aph(f"rcol{i}", [128, 1], F32) for i in range(4)]

                with nc.named_scope(f"FP_t{tile}"):
                    _f_prep(nc, C, sec, ffn_a, ffn_b, ffn_wq, ffn_keys, wbuf,
                            awT, bwT, mT, keysT, ident_bf, pa, pb)
                    barrier()
                do_at = {}
                Xf = {"aq": {}, "x2Fev": {}, "x2tr": {}, "s1": {}, "vln": {},
                      "tb": {}}
                with nc.named_scope(f"F_t{tile}"):
                    fx = lambda b: _f_x(nc, C, sec, tok0, b, x_in, x1spill,
                                        xblk2, ysb2, x2F, x2tok, awT, mT, bnb,
                                        smalls, ident_bf, pa, pb, Xf)
                    fy = lambda b: _f_y(nc, C, sec, tok0, b, out_t, x3f, x2tok,
                                        bwT, fscr, fscr2, gq, gwork, wbuf[1], zhat,
                                        zhT, bnb2, smalls, rcols, ident_bf, pa,
                                        pdm, pb, do_at, Xf)
                    fx(0)
                    for b in range(NB):
                        if b + 1 < NB:
                            fx(b + 1)
                        fy(b)
                    barrier()

        def fin(s):
            C.wait(s, "do")
            C.wait(s, "do2")
            C.wait(s, "doa")
            C.wait(s, "dob")
        sec("sync", fin)
        flush()

    return nc



def _ln_stats(C, v, bnb, src_ap, ncols, mu_col, m2_col, sm, c0, want_meansq):
    nst = ncols // 512
    for i in range(nst):
        v.bn_stats(bnb[i % 2][:, (i // 2) * 6:(i // 2) * 6 + 6],
                   src_ap[:, i * 512:(i + 1) * 512])
    v.bn_aggr(sm[:, c0:c0 + 2], bnb[0][:, 0:(nst // 2) * 6])
    v.bn_aggr(sm[:, c0 + 2:c0 + 4], bnb[1][:, 0:(nst // 2) * 6])
    mA, vA = sm[:, c0:c0 + 1], sm[:, c0 + 1:c0 + 2]
    mB, vB = sm[:, c0 + 2:c0 + 3], sm[:, c0 + 3:c0 + 4]
    t0, t1 = sm[:, c0 + 4:c0 + 5], sm[:, c0 + 5:c0 + 6]
    v.tensor_tensor(mu_col, mA, mB, OP.add)
    v.tensor_scalar(mu_col, mu_col, 0.5, None, OP.mult)
    v.tensor_tensor(t0, mA, mB, OP.subtract)
    v.tensor_tensor(t0, t0, t0, OP.mult)
    v.tensor_tensor(t1, vA, vB, OP.add)
    v.tensor_scalar(t1, t1, 0.5, None, OP.mult)
    v.scalar_tensor_tensor(m2_col, t0, 0.25, t1, OP.mult, OP.add)
    if want_meansq:
        v.tensor_tensor(t0, mu_col, mu_col, OP.mult)
        v.tensor_tensor(m2_col, m2_col, t0, OP.add)
    v.tensor_scalar(m2_col, m2_col, EPS, None, OP.add)


def _rstd_finish(s, src_col, dst_col, tmp_col):
    s.activation(tmp_col, src_col, AF.Ln)
    return s.activation(dst_col, tmp_col, AF.Exp, scale=-0.5)


def _p0(nc, C, sec, tok0, x_in, ln_w, lnw_bc, xblk, xhat_tok, xhatF, bnb,
        smalls, ident_bf, pa, pb):
    dx_at = {}
    v_stt = {}
    t_done = {}

    def g_lnw(g):
        C.dma(g.dma_start(lnw_bc[:], bcast_ap(ln_w, 128, H)), "dw")
    sec("gpsimd", g_lnw)
    dw_lnw = C.cnt["dw"]

    def d_pre(s):
        for b in range(2):
            dx_at[b] = C.dma(s.dma_start(
                xblk[b % 2][:], x_in[tok0 + b * 128:tok0 + (b + 1) * 128, :]),
                "dxa" if b % 2 == 0 else "dxb")
    sec("sync", d_pre)

    for b in range(NB):
        def vf(v, b=b, dxf=None):
            C.wait(v, "dxa" if b % 2 == 0 else "dxb")
            if b == 0:
                C.wait(v, "dw", dw_lnw)
            _ln_stats(C, v, bnb, xblk[b % 2][:], H, smalls[:, 8:9], smalls[:, 9:10],
                      smalls, 12, want_meansq=True)
            C.inc(v.tensor_copy(smalls[:, 18:19], smalls[:, 9:10]), "vE")
        sec("vector", vf)
        v_bn = C.cnt["vE"]

        def sf(s, b=b):
            C.wait(s, "vE", v_bn)
            C.inc(_rstd_finish(s, smalls[:, 18:19], smalls[:, 10:11],
                               smalls[:, 11:12]), "sE")
        sec("scalar", sf)
        s_rs = C.cnt["sE"]

        def vg(v, b=b):
            C.wait(v, "sE", s_rs)
            if b >= 1:
                C.wait(v, "tE", t_done[b - 1])
            C.inc(v.scalar_tensor_tensor(xhat_tok[:], xblk[b % 2][:],
                                         smalls[:, 10:11], lnw_bc[:],
                                         OP.mult, OP.mult), "vE")
        sec("vector", vg)
        v_stt[b] = C.cnt["vE"]

        if b + 2 < NB:
            def df(s, b=b):
                C.wait(s, "vE", v_stt[b])
                dx_at[b + 2] = C.dma(s.dma_start(
                    xblk[b % 2][:],
                    x_in[tok0 + (b + 2) * 128:tok0 + (b + 3) * 128, :]),
                    "dxa" if b % 2 == 0 else "dxb")
            sec("sync", df)

        for jb in range(4):
            def tf(t, b=b, jb=jb):
                if jb == 0:
                    C.wait(t, "vE", v_stt[b])
                key = f"pb{jb % 2}"
                C.bank_guard(t, key)
                for i in range(8):
                    c = jb * 8 + i
                    mm = t.transpose(pb[jb % 2][:, i * 128:(i + 1) * 128],
                                     xhat_tok[:, c * 128:(c + 1) * 128], ident_bf[:])
                C.inc(mm, "tE")
            sec("tensor", tf)
            t_tr = C.cnt["tE"]

            def sg(v, b=b, jb=jb, t_tr=t_tr):
                v.wait_ge(C.sems["tE"], t_tr)
                dstap = xhatF[:, jb * 8 * TT:(jb * 8 + 8) * TT] \
                    .rearrange("p (c t) -> p c t", t=TT)[:, :, b * 128:(b + 1) * 128]
                C.inc(v.tensor_copy(
                    dstap, pb[jb % 2][:, 0:1024].rearrange("p (c t) -> p c t", t=128)),
                    "vE")
                C.bank_mark(f"pb{jb % 2}", "vE")
            sec("vector", sg)
        t_done[b] = C.cnt["tE"]


def _a1(nc, C, sec, tok0, w_in, xhatF, wbuf, mup_bc, ztmp, bst, xbct, accb,
        convwF, tailsave, bmcf, dtT, xstg, xsspill, ident_bf, pa, pb):
    n_slab = 18
    dw_at = {}
    mm_done = {}
    ev_done = {}
    tg_at = {}
    conv_tap = {}
    spill_at = {}
    s_silu_at = {}

    def pre(g):
        def f(gp, g=g):
            col0 = 4096 + g * 256
            w = 64 if g == 17 else 256
            if g >= 2:
                gp.wait_ge(C.sems["tE"], mm_done[g - 2])
            dw_at[g] = C.dma(gp.dma_start(
                wbuf[g % 2][:, 0:32 * w].rearrange("p (c f) -> p c f", f=w),
                w_in.rearrange("(c p) f -> p c f", p=128)[:, :, col0:col0 + w]),
                "dwa" if g % 2 == 0 else "dwb")
        sec("gpsimd", f)

    pre(0)
    pre(1)

    for g in range(n_slab):
        col0 = 4096 + g * 256
        w = 64 if g == 17 else 256

        if g < 16:
            def gtail(gp, g=g):
                if 2 * g - 8 >= 0:
                    gp.wait_ge(C.sems["vE"], conv_tap[2 * g + 1 - 8])
                cc0 = 2 * g
                gp.tensor_copy(xbct[cc0 % 8][:, 0:3],
                               tailsave[:, cc0 * 3:cc0 * 3 + 3])
                C.inc(gp.tensor_copy(xbct[(cc0 + 1) % 8][:, 0:3],
                                     tailsave[:, (cc0 + 1) * 3:(cc0 + 1) * 3 + 3]),
                      "gE")
            sec("gpsimd", gtail)
        g_tail_in = C.cnt["gE"]

        for b in range(NB):
            idx = g * NB + b

            def tf(t, g=g, b=b, w=w, idx=idx):
                if b == 0:
                    C.wait(t, "dwa" if g % 2 == 0 else "dwb")
                C.bank_guard(t, f"pa{idx % 3}")
                for c in range(32):
                    mm = t.matmul(pa[idx % 3][:, 0:w],
                                  xhatF[:, c * TT + b * 128:c * TT + (b + 1) * 128],
                                  wbuf[g % 2][:, c * w:(c + 1) * w],
                                  start=(c == 0), stop=(c == 31))
                C.inc(mm, "tE")
                mm_done[g] = C.cnt["tE"]
            sec("tensor", tf)
            t_mm = C.cnt["tE"]

            def vf(v, g=g, b=b, w=w, col0=col0, idx=idx, t_mm=t_mm):
                C.wait(v, "tE", t_mm)
                if idx == 0:
                    C.wait(v, "dw")
                if g < 17 and idx - 1 in tg_at:
                    C.wait(v, "tE", tg_at[idx - 1])
                if g == 17:
                    C.inc(v.tensor_tensor(dtT[:, b * 64:(b + 1) * 64],
                                          pa[idx % 3][:, 0:64],
                                          mup_bc[:, col0:col0 + 64], OP.mult), "vE")
                else:
                    C.inc(v.tensor_tensor(bst[:, 0:w], pa[idx % 3][:, 0:w],
                                          mup_bc[:, col0:col0 + w], OP.mult), "vE")
                C.bank_mark(f"pa{idx % 3}", "vE")
                ev_done[idx] = C.cnt["vE"]
            sec("vector", vf)
            v_ev = C.cnt["vE"]

            if g < 17:
                def tg(t, g=g, b=b, idx=idx, v_ev=v_ev):
                    C.wait(t, "vE", v_ev)
                    key = f"pb{idx % 2}"
                    C.bank_guard(t, key)
                    for i in range(2):
                        mm = t.transpose(pb[idx % 2][:, i * 128:(i + 1) * 128],
                                         bst[:, i * 128:(i + 1) * 128], ident_bf[:])
                    C.inc(mm, "tE")
                    tg_at[idx] = C.cnt["tE"]
                sec("tensor", tg)
                t_tr = C.cnt["tE"]

                def sg(v, g=g, b=b, idx=idx, t_tr=t_tr):
                    v.wait_ge(C.sems["tE"], t_tr)
                    last = None
                    for i in range(2):
                        cc = 2 * g + i
                        if cc < 32:
                            dst = xbct[cc % 8][:, 3 + b * 128:3 + (b + 1) * 128]
                        else:
                            dst = bmcf[:, (cc - 32) * TT + b * 128:
                                       (cc - 32) * TT + (b + 1) * 128]
                        last = v.tensor_copy(dst, pb[idx % 2][:, i * 128:(i + 1) * 128])
                    C.inc(last, "vE")
                    C.bank_mark(f"pb{idx % 2}", "vE")
                sec("vector", sg)

        if g + 2 < n_slab:
            pre(g + 2)

        if g < 16:
            for i in range(2):
                cc = 2 * g + i
                _conv_chunk(nc, C, sec, tok0, cc, xbct[cc % 8], accb, convwF,
                            tailsave, xstg, xsspill, bmcf, conv_tap, spill_at)
        elif g == 16:
            for i in range(2):
                cc = 32 + i
                _conv_chunk(nc, C, sec, tok0, cc, None, accb, convwF,
                            tailsave, xstg, xsspill, bmcf, conv_tap, spill_at)


def _conv_chunk(nc, C, sec, tok0, cc, buf, accb, convwF, tailsave, xstg,
                xsspill, bmcf, conv_tap, spill_at):
    if cc >= 32:
        data = bmcf[:, (cc - 32) * TT:(cc - 32 + 1) * TT]
    else:
        data = buf[:, 3:3 + TT]
    sE_now = C.cnt["sE"]
    vE_now = C.cnt["vE"]
    gE_now = C.cnt["gE"]

    def vf(v, cc=cc, data=data, sE_now=sE_now, vE_now=vE_now, gE_now=gE_now):
        C.wait(v, "sE", sE_now)
        C.wait(v, "vE", vE_now)
        C.wait(v, "gE", gE_now)
        w = lambda k: convwF[:, cc * 4 + k:cc * 4 + k + 1]
        if cc < 32:
            ext = buf
            v.tensor_scalar(accb[:], ext[:, 0:TT], w(0), None, OP.mult)
            for k in range(1, 3):
                v.scalar_tensor_tensor(accb[:], ext[:, k:k + TT], w(k), accb[:],
                                       OP.mult, OP.add)
            last = v.scalar_tensor_tensor(accb[:], ext[:, 3:3 + TT], w(3), accb[:],
                                          OP.mult, OP.add)
        else:
            ts3 = tailsave[:, cc * 3:cc * 3 + 3]
            v.tensor_scalar(accb[:], data[:], w(3), None, OP.mult)
            for k in range(3):
                sh = 3 - k
                v.scalar_tensor_tensor(accb[:, sh:TT], data[:, 0:TT - sh], w(k),
                                       accb[:, sh:TT], OP.mult, OP.add)
                last = v.scalar_tensor_tensor(accb[:, 0:sh], ts3[:, 3 - sh:3], w(k),
                                              accb[:, 0:sh], OP.mult, OP.add)
        C.inc(last, "vE")
        conv_tap[cc] = C.cnt["vE"]
    sec("vector", vf)
    v_tap = C.cnt["vE"]

    def gp2(g, cc=cc, data=data, v_tap=v_tap):
        C.wait(g, "vE", v_tap)
        C.inc(g.tensor_copy(tailsave[:, cc * 3:cc * 3 + 3], data[:, TT - 3:TT]), "gE")
    sec("gpsimd", gp2)
    g_tail = C.cnt["gE"]

    def sf(s, cc=cc, data=data, v_tap=v_tap, g_tail=g_tail):
        C.wait(s, "vE", v_tap)
        C.wait(s, "gE", g_tail)
        if cc < 32:
            if cc >= 2:
                C.wait(s, "doa" if cc % 2 == 0 else "dob")
            dst = xstg[cc % 2][:]
        else:
            dst = data
        C.inc(s.activation(dst, accb[:], AF.Silu), "sE")
    sec("scalar", sf)
    s_silu = C.cnt["sE"]

    if cc < 32:
        def df(sy, cc=cc, s_silu=s_silu):
            C.wait(sy, "sE", s_silu)
            spill_at[cc] = C.dma(sy.dma_start(
                xsspill[cc * 128:(cc + 1) * 128, tok0:tok0 + TT], xstg[cc % 2][:]),
                "doa" if cc % 2 == 0 else "dob")
        sec("sync", df)


def _a2(nc, C, sec, tok0, w_in, xhatF, wbuf, mup_bc, ztmp, zstg, zspill, pa):
    dw_at = {}
    mm_done = {}
    do_at = {}
    sz_at = {}

    def pre(g):
        def f(gp, g=g):
            col0 = g * 256
            if g >= 2:
                gp.wait_ge(C.sems["tE"], mm_done[g - 2])
            dw_at[g] = C.dma(gp.dma_start(
                wbuf[g % 2][:, 0:32 * 256].rearrange("p (c f) -> p c f", f=256),
                w_in.rearrange("(c p) f -> p c f", p=128)[:, :, col0:col0 + 256]),
                "dwa" if g % 2 == 0 else "dwb")
        sec("gpsimd", f)

    pre(0)
    pre(1)
    for g in range(16):
        col0 = g * 256
        for b in range(NB):
            idx = g * NB + b

            def tf(t, g=g, b=b, idx=idx):
                if b == 0:
                    C.wait(t, "dwa" if g % 2 == 0 else "dwb")
                C.bank_guard(t, f"pa{idx % 3}")
                for c in range(32):
                    mm = t.matmul(pa[idx % 3][:, 0:256],
                                  xhatF[:, c * TT + b * 128:c * TT + (b + 1) * 128],
                                  wbuf[g % 2][:, c * 256:(c + 1) * 256],
                                  start=(c == 0), stop=(c == 31))
                C.inc(mm, "tE")
                mm_done[g] = C.cnt["tE"]
            sec("tensor", tf)
            t_mm = C.cnt["tE"]

            def vf(v, col0=col0, idx=idx, t_mm=t_mm):
                C.wait(v, "tE", t_mm)
                if idx == 0:
                    C.wait(v, "dw")
                if idx - 1 in sz_at:
                    C.wait(v, "sE", sz_at[idx - 1])
                C.inc(v.tensor_tensor(ztmp[:, 0:256], pa[idx % 3][:, 0:256],
                                      mup_bc[:, col0:col0 + 256], OP.mult), "vE")
                C.bank_mark(f"pa{idx % 3}", "vE")
            sec("vector", vf)
            v_ev = C.cnt["vE"]

            def sf(s, idx=idx, v_ev=v_ev):
                C.wait(s, "vE", v_ev)
                if idx >= 2:
                    C.wait(s, "doa" if idx % 2 == 0 else "dob")
                C.inc(s.activation(zstg[idx % 2][:, 0:256], ztmp[:, 0:256],
                                   AF.Silu), "sE")
                sz_at[idx] = C.cnt["sE"]
            sec("scalar", sf)
            s_z = C.cnt["sE"]

            def df(sy, b=b, col0=col0, idx=idx, s_z=s_z):
                C.wait(sy, "sE", s_z)
                do_at[idx] = C.dma(sy.dma_start(
                    zspill[tok0 + b * 128:tok0 + (b + 1) * 128, col0:col0 + 256],
                    zstg[idx % 2][:, 0:256]), "doa" if idx % 2 == 0 else "dob")
            sec("sync", df)
        if g + 2 < 16:
            pre(g + 2)


def _bc(ap, dims):
    return bass.AP(ap.tensor, ap.offset, dims)


def _scan_front(nc, C, sec, tok0, ch, sc, wbuf, xsspill, zspill, dtT,
                dbias_bc, nega_bc, bmcf, ones_f16, ones_f32, tri_f16,
                su_f16, tri_f32, ident_bf, ident_f32, sm64, pa, pb, X):
    t0c = tok0 + ch * 128
    par = ch % 2
    xrd = wbuf[0][:, par * 4096:par * 4096 + 4096]
    zrd = wbuf[1][:, par * 4096:par * 4096 + 4096]

    def d0(sy, par=par):
        if X["xrd_free"][par]:
            sy.wait_ge(C.sems["tE"], X["xrd_free"][par])
        if X["zrd_free"][par]:
            sy.wait_ge(C.sems["vE"], X["zrd_free"][par])
        dsem = "dxa" if par == 0 else "dxb"
        C.dma(sy.dma_start(
            xrd.rearrange("p (c t) -> p c t", t=128),
            xsspill.rearrange("(c p) t -> p c t", p=128)[:, :, t0c:t0c + 128]), dsem)
        C.dma(sy.dma_start(zrd, zspill[t0c:t0c + 128, :]), dsem)
    sec("sync", d0)
    dsemn = "dxa" if par == 0 else "dxb"
    dxv = C.cnt[dsemn]

    for jb in range(4):
        def tf(t, jb=jb, dxv=dxv, dsemn=dsemn):
            if jb == 0:
                C.wait(t, dsemn, dxv)
            key = f"pb{jb % 2}"
            C.bank_guard(t, key)
            for i in range(8):
                c = jb * 8 + i
                mm = t.transpose(pb[jb % 2][:, i * 128:(i + 1) * 128],
                                 xrd[:, c * 128:(c + 1) * 128], ident_bf[:])
            C.inc(mm, "tE")
        sec("tensor", tf)
        t_tr = C.cnt["tE"]

        def sg(v, jb=jb, t_tr=t_tr):
            v.wait_ge(C.sems["tE"], t_tr)
            C.inc(v.tensor_copy(sc["xsT"][:, jb * 1024:(jb + 1) * 1024],
                                pb[jb % 2][:, 0:1024]), "vE")
            C.bank_mark(f"pb{jb % 2}", "vE")
        sec("vector", sg)
    X["xrd_free"][par] = C.cnt["tE"]

    def v_dt(v):
        dts = dtT[:, ch * 64:(ch + 1) * 64]
        if X["scum"]:
            C.wait(v, "sE", X["scum"])
        v.tensor_tensor(sc["t64a"][:], dts, dbias_bc[:], OP.add)
        C.inc(v.scalar_tensor_tensor(sc["t64b"][:], sc["t64a"][:], -1.0,
                                     sc["t64a"][:], OP.mult, OP.min), "vE")
    sec("vector", v_dt)
    v_1 = C.cnt["vE"]

    def s_dt(s, v_1=v_1):
        C.wait(s, "vE", v_1)
        s.activation(sc["t64c"][:], sc["t64b"][:], AF.Exp)
        C.inc(s.activation(sc["t64b"][:], sc["t64c"][:], AF.Ln, bias=1.0), "sE")
    sec("scalar", s_dt)
    s_1 = C.cnt["sE"]

    def v_dt2(v, s_1=s_1):
        C.wait(v, "sE", s_1)
        if X["tcu_prev"]:
            v.wait_ge(C.sems["tE"], X["tcu_prev"])
        v.tensor_scalar(sc["t64c"][:], sc["t64a"][:], 0.0, None, OP.max)
        v.tensor_tensor(sc["dtc"][:], sc["t64b"][:], sc["t64c"][:], OP.add)
        v.tensor_tensor(sc["dta32"][:], sc["dtc"][:], nega_bc[:], OP.mult)
        C.inc(v.tensor_copy(sc["dta16"][:], sc["dta32"][:]), "vE")
    sec("vector", v_dt2)
    v_2 = C.cnt["vE"]

    def t_cum(t, v_2=v_2):
        C.wait(t, "vE", v_2)
        C.bank_guard(t, "pa3")
        t.matmul(pa[3][0:64, 0:128], sc["dta16"][:], tri_f16[:], start=True,
                 stop=True)
        t.matmul(pa[3][:, 128:192], su_f16[:], sc["dta16"][:], start=True,
                 stop=True)
        C.inc(t.matmul(pa[3][0:1, 192:256], ones_f16[:, 0:1], sc["dta16"][:],
                       start=True, stop=True), "tE")
    sec("tensor", t_cum)
    t_cu = C.cnt["tE"]
    X["tcu_prev"] = t_cu

    def s_cum(s, t_cu=t_cu, par=par):
        C.wait(s, "tE", t_cu)
        if X["vyx_last"]:
            C.wait(s, "vE", X["vyx_last"])
        s.activation(sc["vf16"][par][:], pa[3][0:64, 0:128], AF.Exp)
        C.inc(s.activation(sc["t64a"][:], pa[3][:, 128:192], AF.Exp), "sE")
        C.bank_mark("pa3", "sE")
    sec("scalar", s_cum)
    s_2 = C.cnt["sE"]
    X["scum"] = s_2

    def v_sb(v, t_cu=t_cu):
        v.wait_ge(C.sems["tE"], t_cu)
        C.inc(v.tensor_copy(sc["srow"][:], pa[3][0:1, 192:256]), "vE")
    sec("vector", v_sb)
    v_sr = C.cnt["vE"]

    def t_sb2(t, v_sr=v_sr):
        C.wait(t, "vE", v_sr)
        C.bank_guard(t, "pa3")
        C.inc(t.matmul(pa[3][:, 448:512], ones_f32[0:1, :], sc["srow"][:],
                       start=True, stop=True), "tE")
    sec("tensor", t_sb2)
    t_sb2a = C.cnt["tE"]

    def s_sb(s, t_sb2a=t_sb2a):
        s.wait_ge(C.sems["tE"], t_sb2a)
        C.inc(s.activation(sc["sbc"][:], pa[3][:, 448:512], AF.Exp), "sE")
        C.bank_mark("pa3", "sE")
    sec("scalar", s_sb)

    def t_cb(t, par=par):
        C.wait(t, "vE")
        C.bank_guard(t, "pa3")
        t.matmul(pa[3][:, 256:384], bmcf[:, ch * 128:(ch + 1) * 128],
                 bmcf[:, TT + ch * 128:TT + (ch + 1) * 128], start=True, stop=True)
        C.bank_guard(t, "pb0")
        C.inc(t.transpose(pb[0][:, 0:128], bmcf[:, ch * 128:(ch + 1) * 128],
                          ident_bf[:]), "tE")
    sec("tensor", t_cb)
    t_cb_at = C.cnt["tE"]

    def v_cb(v, t_cb_at=t_cb_at, par=par):
        v.wait_ge(C.sems["tE"], t_cb_at)
        v.tensor_tensor(sc["cbtm"][par][:], pa[3][:, 256:384], tri_f32[:], OP.mult)
        C.bank_mark("pa3", "vE")
        C.inc(v.tensor_copy(sc["bmt"][par][:], pb[0][:, 0:128]), "vE")
        C.bank_mark("pb0", "vE")
    sec("vector", v_cb)
    X["front_v"] = C.cnt["vE"]


def _scan_groups(nc, C, sec, tok0, ch, sc, wbuf, dcol_bc, h_bf, bmcf, ysb,
                 e2_f16, bnb, tri_f16, su_f16, ident_bf, pa, pdm, pb, X):
    par = ch % 2
    zrd = wbuf[1][:, par * 4096:par * 4096 + 4096]
    front_v = X["front_v"]

    for g in range(8):
        gg = X["base_g"] + g
        heads = list(range(8 * g, 8 * g + 8))

        for half in range(2):
            hh = 2 * gg + half
            h0 = 8 * g + half * 4

            def v_r(v, hh=hh, h0=h0, front_v=front_v):
                C.wait(v, "vE", front_v)
                if X["dmm"].get(hh - 2):
                    v.wait_ge(C.sems["tE"], X["dmm"][hh - 2])
                C.inc(v.tensor_tensor(
                    sc["rhsd"][hh % 2][:].rearrange("p (a b) -> p a b", b=128),
                    _bc(tri_f16[:], [[128, 128], [0, 4], [1, 128]]),
                    _bc(sc["dta32"][:, h0:h0 + 4], [[64, 128], [1, 4], [0, 128]]),
                    OP.mult), "vE")
            sec("vector", v_r)
            v_rh = C.cnt["vE"]

            def t_d(t, hh=hh, v_rh=v_rh):
                C.wait(t, "vE", v_rh)
                key = f"pd{hh % 2}"
                C.bank_guard(t, key)
                half0 = (hh % 2) * 512
                for i in range(4):
                    mm = t.matmul(pdm[:, half0 + i * 128:half0 + (i + 1) * 128],
                                  su_f16[:],
                                  sc["rhsd"][hh % 2][:, i * 128:(i + 1) * 128],
                                  start=(i == 0), stop=(i == 3))
                C.inc(mm, "tE")
            sec("tensor", t_d)
            X["dmm"][hh] = C.cnt["tE"]

        def s_e(s, gg=gg):
            s.wait_ge(C.sems["tE"], X["dmm"][2 * gg + 1])
            if X["expfree"].get(gg - 2):
                C.wait(s, "vE", X["expfree"][gg - 2])
            C.inc(s.activation(sc["expd"][gg % 2][:], pdm[:, 0:1024], AF.Exp),
                  "sE")
            C.bank_mark("pd0", "sE")
            C.bank_mark("pd1", "sE")
        sec("scalar", s_e)
        exp_g = C.cnt["sE"]

        def v_gA(v, g=g, gg=gg, par=par, exp_g=exp_g):
            C.wait(v, "sE", exp_g)
            if X["mfree"].get(gg - 2):
                v.wait_ge(C.sems["tE"], X["mfree"][gg - 2])
            v.tensor_tensor(
                sc["mtil"][gg % 2][:].rearrange("p (a b) -> p a b", b=128),
                sc["expd"][gg % 2][:].rearrange("p (a b) -> p a b", b=128),
                _bc(sc["cbtm"][par][:], [[128, 128], [0, 8], [1, 128]]),
                OP.mult)
            v.tensor_tensor(
                sc["xsTd"][gg % 2][:].rearrange("p (a b) -> p a b", b=64),
                sc["xsT"][:, g * 512:(g + 1) * 512].rearrange("p (a b) -> p a b", b=64),
                _bc(sc["dtc"][:, g * 8:g * 8 + 8], [[64, 128], [1, 8], [0, 64]]),
                OP.mult)
            C.inc(v.tensor_tensor(
                sc["dxst"][:].rearrange("p (a b) -> p a b", b=64),
                sc["xsT"][:, g * 512:(g + 1) * 512].rearrange("p (a b) -> p a b", b=64),
                _bc(dcol_bc[:, g * 8:g * 8 + 8], [[64, 128], [1, 8], [0, 64]]),
                OP.mult), "vE")
        sec("vector", v_gA)
        v_ga = C.cnt["vE"]
        X["expfree"][gg] = v_ga

        def t_iv(t, g=g, gg=gg, par=par, v_ga=v_ga):
            C.wait(t, "vE", v_ga)
            C.bank_guard(t, "pa1")
            for i in range(8):
                t.matmul(pa[1][:, i * 64:(i + 1) * 64],
                         sc["mtil"][gg % 2][:, i * 128:(i + 1) * 128],
                         sc["xsTd"][gg % 2][:, i * 64:(i + 1) * 64],
                         start=(i == 0), stop=(i == 7))
            C.bank_guard(t, "pa0")
            if X["hev"].get(g):
                t.wait_ge(C.sems["vE"], X["hev"][g])
            for i, r in enumerate(heads):
                mm = t.matmul(pa[0][:, i * 64:(i + 1) * 64],
                              bmcf[:, TT + ch * 128:TT + (ch + 1) * 128],
                              h_bf[:, r * 64:(r + 1) * 64],
                              start=(i == 0), stop=(i == 7))
            C.inc(mm, "tE")
            key = f"pd{g % 2}"
            C.bank_guard(t, key)
            C.inc(t.matmul(pdm[:, (g % 2) * 512:(g % 2) * 512 + 512],
                           sc["vf16"][par][:], e2_f16[:, g * 512:(g + 1) * 512],
                           start=True, stop=True), "tE")
        sec("tensor", t_iv)
        t_ve = C.cnt["tE"]
        t_int = t_ve - 1
        X["mfree"][gg] = t_int

        def v_gB(v, g=g, t_ve=t_ve, t_int=t_int):
            v.wait_ge(C.sems["tE"], t_ve)
            v.tensor_copy(sc["vexp"][:], pdm[:, (g % 2) * 512:(g % 2) * 512 + 512])
            C.bank_mark(f"pd{g % 2}", "vE")
            C.inc(v.scalar_tensor_tensor(sc["ytmp"][:], pa[0][:, 0:512], 1.0,
                                         sc["vexp"][:], OP.mult, OP.mult), "vE")
            C.bank_mark("pa0", "vE")
        sec("vector", v_gB)
        v_yt = C.cnt["vE"]

        def v_gC(v, g=g, gg=gg, par=par, t_int=t_int):
            v.scalar_tensor_tensor(sc["yA"][:], pa[1][:, 0:512], 1.0,
                                   sc["ytmp"][:], OP.mult, OP.add)
            C.bank_mark("pa1", "vE")
            v.scalar_tensor_tensor(sc["yA"][:], sc["dxst"][:], 1.0, sc["yA"][:],
                                   OP.mult, OP.add)
            v.tensor_tensor(ysb[:, g * 512:(g + 1) * 512], sc["yA"][:],
                            zrd[:, g * 512:(g + 1) * 512], OP.mult)
            v.bn_stats(bnb[g % 2][:, (g // 2) * 6:(g // 2) * 6 + 6],
                       ysb[:, g * 512:(g + 1) * 512])
            if X["st"].get(gg - 1):
                v.wait_ge(C.sems["tE"], X["st"][gg - 1])
            v.tensor_tensor(
                sc["xtld"][:].rearrange("p (a b) -> p a b", b=64),
                sc["xsTd"][gg % 2][:].rearrange("p (a b) -> p a b", b=64),
                _bc(sc["t64a"][:, g * 8:g * 8 + 8], [[64, 128], [1, 8], [0, 64]]),
                OP.mult)
            C.inc(v.tensor_tensor(
                sc["hdec"][:].rearrange("p (a b) -> p a b", b=64),
                h_bf[:, g * 512:(g + 1) * 512].rearrange("p (a b) -> p a b", b=64),
                _bc(sc["sbc"][:, g * 8:g * 8 + 8], [[64, 128], [1, 8], [0, 64]]),
                OP.mult), "vE")
        sec("vector", v_gC)
        v_xt = C.cnt["vE"]

        def t_st(t, g=g, gg=gg, par=par, v_xt=v_xt):
            C.wait(t, "vE", v_xt)
            C.bank_guard(t, "pa2")
            for i in range(8):
                t.matmul(pa[2][:, i * 64:(i + 1) * 64], sc["bmt"][par][:],
                         sc["xtld"][:, i * 64:(i + 1) * 64],
                         start=(i == 0), stop=False)
            C.inc(t.matmul(pa[2][:, 0:512], ident_bf[:], sc["hdec"][:],
                           start=False, stop=True), "tE")
            X["st"][gg] = C.cnt["tE"]
        sec("tensor", t_st)
        t_hu = C.cnt["tE"]
        if g == 7:
            X["st_last"] = t_hu

        def v_h(v, g=g, t_hu=t_hu):
            v.wait_ge(C.sems["tE"], t_hu)
            C.inc(v.tensor_copy(h_bf[:, g * 512:(g + 1) * 512], pa[2][:, 0:512]),
                  "vE")
            C.bank_mark("pa2", "vE")
            X["hev"][g] = C.cnt["vE"]
        sec("vector", v_h)

    X["vyx_last"] = C.cnt["vE"]
    X["zrd_free"][par] = C.cnt["vE"]
    X["base_g"] += 8


def _scan_tail(nc, C, sec, tok0, ch, sc, nw_bc, ysb, yhatF, ident_bf, smalls,
               sm64, bnb, pa, pb, X):
    def v_tail(v):
        v.bn_aggr(smalls[:, 12:14], bnb[0][:, 0:24])
        v.bn_aggr(smalls[:, 20:22], bnb[1][:, 0:24])
        mA, vA = smalls[:, 12:13], smalls[:, 13:14]
        mB, vB = smalls[:, 20:21], smalls[:, 21:22]
        t0, t1 = smalls[:, 16:17], smalls[:, 17:18]
        v.tensor_tensor(smalls[:, 8:9], mA, mB, OP.add)
        v.tensor_scalar(smalls[:, 8:9], smalls[:, 8:9], 0.5, None, OP.mult)
        v.tensor_tensor(t0, mA, mB, OP.subtract)
        v.tensor_tensor(t0, t0, t0, OP.mult)
        v.tensor_tensor(t1, vA, vB, OP.add)
        v.tensor_scalar(t1, t1, 0.5, None, OP.mult)
        v.scalar_tensor_tensor(smalls[:, 9:10], t0, 0.25, t1, OP.mult, OP.add)
        v.tensor_tensor(t0, smalls[:, 8:9], smalls[:, 8:9], OP.mult)
        v.tensor_tensor(smalls[:, 9:10], smalls[:, 9:10], t0, OP.add)
        v.tensor_scalar(smalls[:, 9:10], smalls[:, 9:10], EPS, None, OP.add)
        C.inc(v.tensor_copy(smalls[:, 18:19], smalls[:, 9:10]), "vE")
    sec("vector", v_tail)
    v_t1 = C.cnt["vE"]

    def s_tail(s, v_t1=v_t1):
        C.wait(s, "vE", v_t1)
        if X["vt2"]:
            C.wait(s, "vE", X["vt2"])
        C.inc(_rstd_finish(s, smalls[:, 18:19], smalls[:, 10:11],
                           smalls[:, 11:12]), "sE")
    sec("scalar", s_tail)
    s_t1 = C.cnt["sE"]

    def v_tail2(v, s_t1=s_t1):
        C.wait(v, "sE", s_t1)
        if not X["ytr"]:
            C.wait(v, "dw")
        if X["ytr"]:
            v.wait_ge(C.sems["tE"], X["ytr"])
        C.inc(v.scalar_tensor_tensor(sc["yhat_tok"][:], ysb[:], smalls[:, 10:11],
                                     nw_bc[:], OP.mult, OP.mult), "vE")
    sec("vector", v_tail2)
    v_t2 = C.cnt["vE"]
    X["vt2"] = v_t2

    for jb in range(4):
        def tf2(t, jb=jb, v_t2=v_t2):
            if jb == 0:
                C.wait(t, "vE", v_t2)
            key = f"pb{jb % 2}"
            C.bank_guard(t, key)
            for i in range(8):
                c = jb * 8 + i
                mm = t.transpose(pb[jb % 2][:, i * 128:(i + 1) * 128],
                                 sc["yhat_tok"][:, c * 128:(c + 1) * 128],
                                 ident_bf[:])
            C.inc(mm, "tE")
        sec("tensor", tf2)
        t_tr2 = C.cnt["tE"]

        def sg2(v, jb=jb, t_tr2=t_tr2):
            v.wait_ge(C.sems["tE"], t_tr2)
            dstap = yhatF[:, jb * 8 * TT:(jb * 8 + 8) * TT] \
                .rearrange("p (c t) -> p c t", t=TT)[:, :, ch * 128:(ch + 1) * 128]
            C.inc(v.tensor_copy(
                dstap, pb[jb % 2][:, 0:1024].rearrange("p (c t) -> p c t", t=128)),
                "vE")
            C.bank_mark(f"pb{jb % 2}", "vE")
        sec("vector", sg2)
    X["ytr"] = C.cnt["tE"]


def _e_outproj(nc, C, sec, tok0, w_out, yhatF, wbuf, x1stg, x1spill, pa):
    dw_at = {}
    mm_done = {}
    do_at = {}

    def pre(g):
        def f(gp, g=g):
            col0 = g * 256
            if g >= 2:
                gp.wait_ge(C.sems["tE"], mm_done[g - 2])
            dw_at[g] = C.dma(gp.dma_start(
                wbuf[g % 2][:, 0:32 * 256].rearrange("p (c f) -> p c f", f=256),
                w_out.rearrange("(c p) f -> p c f", p=128)[:, :, col0:col0 + 256]),
                "dwa" if g % 2 == 0 else "dwb")
        sec("gpsimd", f)

    pre(0)
    pre(1)
    for g in range(16):
        col0 = g * 256
        for b in range(NB):
            idx = g * NB + b

            def tf(t, g=g, b=b, idx=idx):
                if b == 0:
                    C.wait(t, "dwa" if g % 2 == 0 else "dwb")
                C.bank_guard(t, f"pa{idx % 3}")
                for c in range(32):
                    mm = t.matmul(pa[idx % 3][:, 0:256],
                                  yhatF[:, c * TT + b * 128:c * TT + (b + 1) * 128],
                                  wbuf[g % 2][:, c * 256:(c + 1) * 256],
                                  start=(c == 0), stop=(c == 31))
                C.inc(mm, "tE")
                mm_done[g] = C.cnt["tE"]
            sec("tensor", tf)
            t_mm = C.cnt["tE"]

            def sf(v, idx=idx, t_mm=t_mm):
                v.wait_ge(C.sems["tE"], t_mm)
                if idx >= 2:
                    C.wait(v, "doa" if idx % 2 == 0 else "dob")
                C.inc(v.tensor_copy(x1stg[idx % 2][:, 0:256], pa[idx % 3][:, 0:256]),
                      "vE")
                C.bank_mark(f"pa{idx % 3}", "vE")
            sec("vector", sf)
            s_x1 = C.cnt["vE"]

            def df(sy, b=b, col0=col0, idx=idx, s_x1=s_x1):
                C.wait(sy, "vE", s_x1)
                do_at[idx] = C.dma(sy.dma_start(
                    x1spill[tok0 + b * 128:tok0 + (b + 1) * 128, col0:col0 + 256],
                    x1stg[idx % 2][:, 0:256]), "doa" if idx % 2 == 0 else "dob")
            sec("sync", df)
        if g + 2 < 16:
            pre(g + 2)


def _f_prep(nc, C, sec, ffn_a, ffn_b, ffn_wq, ffn_keys, wbuf, awT, bwT,
            mT, keysT, ident_bf, pa, pb):
    def d0(gp):
        for h in range(2):
            C.dma(gp.dma_start(
                wbuf[h][:, 0:8192].rearrange("p (j d) -> p j d", d=4096),
                ffn_a.rearrange("(j p) d -> p j d", p=128)[:, 2 * h:2 * h + 2, :]),
                "dw")
    sec("gpsimd", d0)
    dw_a = C.cnt["dw"]

    for cb in range(16):
        def tf(t, cb=cb):
            if cb == 0:
                C.wait(t, "dw", dw_a)
            key = f"pb{cb % 2}"
            C.bank_guard(t, key)
            for cc in range(2):
                c = cb * 2 + cc
                for j in range(4):
                    mm = t.transpose(
                        pb[cb % 2][:, cc * 512 + j * 128:cc * 512 + (j + 1) * 128],
                        wbuf[j // 2][:, (j % 2) * 4096 + c * 128:
                                     (j % 2) * 4096 + (c + 1) * 128],
                        ident_bf[:])
            C.inc(mm, "tE")
        sec("tensor", tf)
        t_tr = C.cnt["tE"]

        def sg(s, cb=cb, t_tr=t_tr):
            s.wait_ge(C.sems["tE"], t_tr)
            C.inc(s.activation(awT[:, cb * 1024:(cb + 1) * 1024],
                               pb[cb % 2][:, 0:1024], AF.Copy), "sE")
            C.bank_mark(f"pb{cb % 2}", "sE")
        sec("scalar", sg)
    t_a_done = C.cnt["tE"]

    def d1(gp, t_a_done=t_a_done):
        gp.wait_ge(C.sems["tE"], t_a_done)
        for h in range(2):
            C.dma(gp.dma_start(
                wbuf[h][:, 0:8192].rearrange("p (j n) -> p j n", n=512),
                ffn_b.rearrange("(j p) n -> p j n", p=128)[:, 16 * h:16 * h + 16, :]),
                "dw")
    sec("gpsimd", d1)
    dw_b = C.cnt["dw"]

    for nch in range(4):
        for jb in range(4):
            def tf2(t, nch=nch, jb=jb):
                if nch == 0 and jb == 0:
                    C.wait(t, "dw", dw_b)
                key = f"pb{jb % 2}"
                C.bank_guard(t, key)
                for i in range(8):
                    j = jb * 8 + i
                    mm = t.transpose(pb[jb % 2][:, i * 128:(i + 1) * 128],
                                     wbuf[j // 16][:, (j % 16) * 512 + nch * 128:
                                                   (j % 16) * 512 + (nch + 1) * 128],
                                     ident_bf[:])
                C.inc(mm, "tE")
            sec("tensor", tf2)
            t_tr2 = C.cnt["tE"]

            def sg2(s, nch=nch, jb=jb, t_tr2=t_tr2):
                s.wait_ge(C.sems["tE"], t_tr2)
                C.inc(s.activation(
                    bwT[:, nch * 4096 + jb * 1024:nch * 4096 + (jb + 1) * 1024],
                    pb[jb % 2][:, 0:1024], AF.Copy), "sE")
                C.bank_mark(f"pb{jb % 2}", "sE")
            sec("scalar", sg2)
    t_b_done = C.cnt["tE"]

    def d2(gp, t_b_done=t_b_done):
        gp.wait_ge(C.sems["tE"], t_b_done)
        C.dma(gp.dma_start(wbuf[0][0:16, 0:4096], ffn_wq), "dw")
        C.dma(gp.dma_start(wbuf[0][0:64, 4096:4112], ffn_keys), "dw")
    sec("gpsimd", d2)
    dw_q = C.cnt["dw"]

    def tf4(t):
        C.wait(t, "dw", dw_q)
        C.bank_guard(t, "pb0")
        C.inc(t.transpose(pb[0][0:16, 0:64], wbuf[0][0:64, 4096:4112],
                          ident_bf[0:64, 0:64]), "tE")
    sec("tensor", tf4)
    t_k = C.cnt["tE"]

    def sg4(s, t_k=t_k):
        s.wait_ge(C.sems["tE"], t_k)
        C.inc(s.activation(keysT[:], pb[0][0:16, 0:64], AF.Copy), "sE")
        C.bank_mark("pb0", "sE")
    sec("scalar", sg4)
    s_k = C.cnt["sE"]

    for mb in range(4):
        def tm(t, mb=mb, s_k=s_k):
            if mb == 0:
                C.wait(t, "sE", s_k)
            C.bank_guard(t, f"pa{mb % 2}")
            for i in range(8):
                c = mb * 8 + i
                mm = t.matmul(pa[mb % 2][:, i * 64:(i + 1) * 64],
                              wbuf[0][0:16, c * 128:(c + 1) * 128],
                              keysT[:], start=True, stop=True)
            C.inc(mm, "tE")
        sec("tensor", tm)
        t_m = C.cnt["tE"]

        def vm(v, mb=mb, t_m=t_m):
            v.wait_ge(C.sems["tE"], t_m)
            C.inc(v.tensor_copy(mT[:, mb * 512:(mb + 1) * 512],
                                pa[mb % 2][:, 0:512]), "vE")
            C.bank_mark(f"pa{mb % 2}", "vE")
        sec("vector", vm)


def _f_x(nc, C, sec, tok0, b, x_in, x1spill, xblk2, ysb2, x2F, x2tok,
         awT, mT, bnb, smalls, ident_bf, pa, pb, Xf):
    t0b = tok0 + b * 128
    par = b % 2
    zi = 0 if par == 0 else 3
    si = 1 if par == 0 else 2

    def d0(sy, b=b):
        if Xf["aq"].get(b - 1):
            sy.wait_ge(C.sems["tE"], Xf["aq"][b - 1])
        if Xf["x2Fev"].get(b - 1):
            sy.wait_ge(C.sems["sE"], Xf["x2Fev"][b - 1])
        if Xf["vln"].get(b - 1):
            sy.wait_ge(C.sems["vE"], Xf["vln"][b - 1])
        dsem = "dxa" if b % 2 == 0 else "dxb"
        C.dma(sy.dma_start(xblk2[:], x_in[t0b:t0b + 128, :]), dsem)
        C.dma(sy.dma_start(x2F[:], x1spill[t0b:t0b + 128, :]), dsem)
    sec("sync", d0)

    def v0(v):
        C.wait(v, "dxa" if b % 2 == 0 else "dxb")
        if Xf["s1"].get(b - 1):
            v.wait_ge(C.sems["sE"], Xf["s1"][b - 1])
        v.tensor_tensor(ysb2[:], x2F[:, 0:H], xblk2[:], OP.add)
        _ln_stats(C, v, bnb, ysb2[:], H, smalls[:, 8:9], smalls[:, 9:10],
                  smalls, 12, want_meansq=False)
        C.inc(v.tensor_copy(smalls[:, 18:19], smalls[:, 9:10]), "vE")
    sec("vector", v0)
    v_ln = C.cnt["vE"]
    Xf["vln"][b] = v_ln

    def s0(s, v_ln=v_ln, par=par):
        C.wait(s, "vE", v_ln)
        _rstd_finish(s, smalls[:, 18:19], smalls[:, 10:11], smalls[:, 11:12])
        s.activation(smalls[:, 11:12], smalls[:, 8:9], AF.Copy,
                     scale=smalls[:, 10:11])
        s.activation(smalls[:, 19:20], smalls[:, 11:12], AF.Copy, scale=-1.0)
        if Xf["tb"].get(b - 2):
            s.wait_ge(C.sems["tE"], Xf["tb"][b - 2])
        C.inc(s.activation(x2tok[par][:], ysb2[:], AF.Identity,
                           scale=smalls[:, 10:11], bias=smalls[:, 19:20]), "sE")
    sec("scalar", s0)
    s_x2 = C.cnt["sE"]
    Xf["s1"][b] = s_x2

    for jb in range(4):
        def tf(t, jb=jb, s_x2=s_x2, par=par):
            if jb == 0:
                C.wait(t, "sE", s_x2)
            C.bank_guard(t, f"pb{jb % 2}")
            for i in range(8):
                c = jb * 8 + i
                mm = t.transpose(pb[jb % 2][:, i * 128:(i + 1) * 128],
                                 x2tok[par][:, c * 128:(c + 1) * 128], ident_bf[:])
            C.inc(mm, "tE")
        sec("tensor", tf)
        t_tr = C.cnt["tE"]

        def sg(s, jb=jb, t_tr=t_tr):
            s.wait_ge(C.sems["tE"], t_tr)
            C.inc(s.activation(x2F[:, jb * 1024:(jb + 1) * 1024],
                               pb[jb % 2][:, 0:1024], AF.Copy), "sE")
            C.bank_mark(f"pb{jb % 2}", "sE")
        sec("scalar", sg)
    Xf["x2tr"][b] = C.cnt["tE"]
    s_x2f = C.cnt["sE"]
    Xf["x2Fev"][b] = s_x2f

    def t_a(t, s_x2f=s_x2f, zi=zi, si=si):
        C.wait(t, "sE", s_x2f)
        C.bank_guard(t, f"pa{zi}")
        C.bank_guard(t, f"pa{si}")
        last = None
        for c in range(32):
            t.matmul(pa[zi][:, 0:512], x2F[:, c * 128:(c + 1) * 128],
                     awT[:, c * 512:(c + 1) * 512], start=(c == 0), stop=(c == 31))
            last = t.matmul(pa[si][:, 0:64], x2F[:, c * 128:(c + 1) * 128],
                            mT[:, c * 64:(c + 1) * 64], start=(c == 0),
                            stop=(c == 31))
        C.inc(last, "tE")
    sec("tensor", t_a)
    Xf["aq"][b] = C.cnt["tE"]


def _f_y(nc, C, sec, tok0, b, out_t, x3f, x2tok, bwT, fscr, fscr2, gq, gwork,
         fcmp, zhat, zhT, bnb2, smalls, rcols, ident_bf, pa, pdm, pb,
         do_at, Xf):
    t0b = tok0 + b * 128
    par = b % 2
    zi = 0 if par == 0 else 3
    si = 1 if par == 0 else 2
    t_aq = Xf["aq"][b]

    def v_sc(v, t_aq=t_aq, si=si):
        C.wait(v, "tE", t_aq)
        v.tensor_copy(gq[:], pa[si][:, 0:64])
        C.bank_mark(f"pa{si}", "vE")
        v.tensor_tensor(
            fcmp[:, 0:4096].rearrange("p (a b) -> p a b", b=64),
            _bc(gq[:], [[64, 128], [0, 64], [1, 64]]),
            _bc(gq[:], [[64, 128], [1, 64], [0, 64]]),
            OP.is_gt)
        v.reduce_sum(fscr2[:], fcmp[:, 0:4096].rearrange("p (a b) -> p a b", b=64),
                     axis=AX)
        v.reduce_max(rcols[0][:], gq[:], axis=AX)
        C.inc(v.tensor_scalar(rcols[1][:], rcols[0][:], -1.0, None, OP.mult),
              "vE")
    sec("vector", v_sc)
    v_rt = C.cnt["vE"]

    def s_r(s, v_rt=v_rt):
        C.wait(s, "vE", v_rt)
        C.inc(s.activation(fscr[:, 0:64], gq[:], AF.Exp, bias=rcols[1][:]), "sE")
    sec("scalar", s_r)
    s_re = C.cnt["sE"]

    def v_g(v, s_re=s_re):
        C.wait(v, "sE", s_re)
        v.tensor_scalar(gwork[:], fscr2[:], 15.0, None, OP.is_le)
        v.tensor_tensor(gwork[:], gwork[:], fscr[:, 0:64], OP.mult)
        v.reduce_sum(rcols[3][:], gwork[:], axis=AX)
        v.reciprocal(rcols[0][:], rcols[3][:])
        C.inc(v.tensor_scalar(gwork[:], gwork[:], rcols[0][:], 2.0,
                              OP.mult, OP.mult), "vE")
    sec("vector", v_g)

    def v_z(v, zi=zi):
        C.inc(v.scalar_tensor_tensor(
            zhat[:].rearrange("p (a b) -> p a b", b=8),
            pa[zi][:, 0:512].rearrange("p (a b) -> p a b", b=8), 1.0,
            _bc(gwork[:], [[64, 128], [1, 64], [0, 8]]),
            OP.mult, OP.mult), "vE")
        C.bank_mark(f"pa{zi}", "vE")
    sec("vector", v_z)
    v_zh = C.cnt["vE"]

    def t_zt(t, v_zh=v_zh):
        C.wait(t, "vE", v_zh)
        C.bank_guard(t, "pb0")
        last = None
        for i in range(4):
            last = t.transpose(pb[0][:, i * 128:(i + 1) * 128],
                               zhat[:, i * 128:(i + 1) * 128], ident_bf[:])
        C.inc(last, "tE")
    sec("tensor", t_zt)
    t_zT = C.cnt["tE"]

    def s_zt(s, t_zT=t_zT):
        s.wait_ge(C.sems["tE"], t_zT)
        C.inc(s.activation(zhT[:], pb[0][:, 0:512], AF.Copy), "sE")
        C.bank_mark("pb0", "sE")
    sec("scalar", s_zt)
    s_zTT = C.cnt["sE"]

    for og in range(8):
        def t_b(t, og=og, s_zTT=s_zTT, par=par):
            if og == 0:
                C.wait(t, "sE", s_zTT)
            key = f"pd{og % 2}"
            C.bank_guard(t, key)
            half = (og % 2) * 512
            for nch in range(4):
                t.matmul(pdm[:, half:half + 512],
                         zhT[:, nch * 128:(nch + 1) * 128],
                         bwT[:, nch * 4096 + og * 512:
                             nch * 4096 + (og + 1) * 512],
                         start=(nch == 0), stop=False)
            C.inc(t.matmul(pdm[:, half:half + 512], ident_bf[:],
                           x2tok[par][:, og * 512:(og + 1) * 512],
                           start=False, stop=True), "tE")
        sec("tensor", t_b)
        t_bo = C.cnt["tE"]

        def s_vo(s, og=og, t_bo=t_bo, par=par):
            s.wait_ge(C.sems["tE"], t_bo)
            if og == 0:
                C.wait(s, "do" if par == 1 else "do2")
            half = (og % 2) * 512
            C.inc(s.activation(x3f[:, og * 512:(og + 1) * 512],
                               pdm[:, half:half + 512], AF.Copy), "sE")
            C.bank_mark(f"pd{og % 2}", "sE")
        sec("scalar", s_vo)
    Xf["tb"][b] = C.cnt["tE"]
    s_voe = C.cnt["sE"]

    def v_l2(v, s_voe=s_voe):
        C.wait(v, "sE", s_voe)
        _ln_stats(C, v, bnb2, x3f[:], H, smalls[:, 20:21], smalls[:, 21:22],
                  smalls, 24, want_meansq=False)
        C.inc(v.tensor_copy(smalls[:, 30:31], smalls[:, 21:22]), "vE")
    sec("vector", v_l2)
    v_l2a = C.cnt["vE"]

    def s_l2(s, v_l2a=v_l2a):
        C.wait(s, "vE", v_l2a)
        _rstd_finish(s, smalls[:, 30:31], smalls[:, 22:23], smalls[:, 23:24])
        s.activation(smalls[:, 23:24], smalls[:, 20:21], AF.Copy,
                     scale=smalls[:, 22:23])
        s.activation(smalls[:, 26:27], smalls[:, 23:24], AF.Copy, scale=-1.0)
        C.inc(s.activation(x3f[:], x3f[:], AF.Identity,
                           scale=smalls[:, 22:23], bias=smalls[:, 26:27]), "sE")
    sec("scalar", s_l2)
    s_ot = C.cnt["sE"]

    def d_out(sy, b=b, s_ot=s_ot, par=par):
        sy.wait_ge(C.sems["sE"], s_ot)
        do_at[b] = C.dma(sy.dma_start(out_t[t0b:t0b + 128, :], x3f[:]),
                         "do" if par == 0 else "do2")
    sec("sync", d_out)


def kernel(**inputs):
    _ = inputs.get("L0_attn_residual")
    x = np.ascontiguousarray(inputs["L0_attn_hidden_states"], dtype=np.float32)
    nc = build()
    base = {
        "w_in": np.ascontiguousarray(inputs["L0_attn_in_proj_w"], np.float32),
        "conv_w": np.ascontiguousarray(inputs["L0_attn_conv_w"], np.float32),
        "a_log": np.ascontiguousarray(inputs["L0_attn_A_log"], np.float32).reshape(1, NH),
        "dt_bias": np.ascontiguousarray(inputs["L0_attn_dt_bias"], np.float32).reshape(1, NH),
        "d_vec": np.ascontiguousarray(inputs["L0_attn_D"], np.float32).reshape(1, NH),
        "ln_w": np.ascontiguousarray(inputs["L0_attn_ln_w"], np.float32).reshape(1, H),
        "norm_w": np.ascontiguousarray(inputs["L0_attn_norm_w"], np.float32).reshape(1, DS),
        "mup": np.ascontiguousarray(inputs["L0_attn_mup_vector"], np.float32).reshape(1, PROJ),
        "w_out": np.ascontiguousarray(inputs["L0_attn_out_proj_w"], np.float32),
        "ffn_a": np.ascontiguousarray(inputs["L0_ffn_A_w"], np.float32),
        "ffn_b": np.ascontiguousarray(inputs["L0_ffn_B_w"], np.float32),
        "ffn_wq": np.ascontiguousarray(inputs["L0_ffn_Wq_w"], np.float32),
        "ffn_keys": np.ascontiguousarray(inputs["L0_ffn_keys"], np.float32),
    }
    in_maps = [dict(base, x=np.ascontiguousarray(x[c])) for c in range(NCORES)]
    res = run_bass_kernel_spmd(nc, in_maps, core_ids=list(range(NCORES)))
    return np.stack([res.results[c]["out"] for c in range(NCORES)], axis=0)
